# version 06 (final)
# baseline (speedup 1.0000x reference)
import math
import numpy as np

import concourse.bass as bass
import concourse.mybir as mybir
from concourse.bass_utils import run_bass_kernel_spmd
from concourse.aot_env import allow_ici_all_to_all

F32 = mybir.dt.float32
BF16 = mybir.dt.bfloat16
I32 = mybir.dt.int32
AF = mybir.ActivationFunctionType
OP = mybir.AluOpType

D = 2048
S = 8192
NCORE = 8
TPC = S // NCORE
TH = 256
NTD = D // 128
NTI = 8192 // 128
IFF = 8192
NH = 16
DH = 128
DKV = 512
DR = 64
EPS = 1e-5
HPC = NH // NCORE

RSQ = 1.0 / math.sqrt(float(DH))
FW2048 = 1.0 / math.sqrt(2048.0)
FW8192 = 1.0 / math.sqrt(8192.0)
SOFT_C_SHIFT = 65.0
DBG_TAPS = False
PHASE_LIMIT = 7

ENGS = ("sync", "tensor", "vector", "scalar", "gpsimd")

class _DryInstr:
    def then_inc(self, *a, **kw):
        return self

    def __getattr__(self, name):
        return lambda *a, **kw: _DryInstr()


class _Dry:
    def __getattr__(self, name):
        return lambda *a, **kw: _DryInstr()


_DRY = _Dry()



def ts(i, w):
    return slice(i * w, (i + 1) * w)


class Build:
    def __init__(self, nc):
        self.nc = nc
        self.ops = {e: [] for e in ENGS}
        self.ch = {}
        self.live = {}
        self.fq = {}
        self.dry = False

    def on(self, eng):
        def deco(fn):
            self.dry = True
            fn(_DRY)
            self.dry = False
            self.ops[eng].append(fn)
            return None
        return deco

    def realize(self):
        nc = self.nc
        with nc.Block() as block:
            @block.sync
            def _(e):
                for fn in self.ops["sync"]:
                    fn(e)

            @block.tensor
            def _(e):
                for fn in self.ops["tensor"]:
                    fn(e)

            @block.vector
            def _(e):
                for fn in self.ops["vector"]:
                    fn(e)

            @block.scalar
            def _(e):
                for fn in self.ops["scalar"]:
                    fn(e)

            @block.gpsimd
            def _(e):
                for fn in self.ops["gpsimd"]:
                    fn(e)

    def chan(self, name):
        if name not in self.ch:
            self.ch[name] = [self.nc.alloc_semaphore(name="ch_" + name), 0]
        return self.ch[name]

    def inc(self, name, instr, k=1):
        c = self.chan(name)
        if self.dry:
            c[1] += k
            return c[1]
        instr.then_inc(c[0], k)
        return c[1]

    def mark(self, name):
        return self.chan(name)[1]

    def wait(self, eng, name, n):
        if self.dry:
            return
        eng.wait_ge(self.chan(name)[0], n)

    def fence(self, eng, last, tag):
        c = self.chan(tag)
        q = self.fq.setdefault(tag, [])
        if self.dry:
            c[1] += 1
            q.append(c[1])
            return
        n = q.pop(0)
        last.then_inc(c[0], 1)
        eng.wait_ge(c[0], n)

    def sbuf(self, name, shape, dtype):
        cm = self.nc.sbuf_tensor(name, shape, dtype)
        h = cm.__enter__()
        self.live[name] = cm
        return h

    def psum(self, name, shape, dtype=F32):
        cm = self.nc.psum_tensor(name, shape, dtype)
        h = cm.__enter__()
        self.live[name] = cm
        return h

    def free(self, *names):
        for n in names:
            self.live.pop(n).__exit__(None, None, None)



def alloc_persist(b):
    k = {}
    k["ones_col_f"] = b.sbuf("ones_col_f", [128, 1], F32)
    k["ones_row_f"] = b.sbuf("ones_row_f", [1, 128], F32)
    k["ones_col_b"] = b.sbuf("ones_col_b", [128, 1], BF16)
    k["ones_row_b"] = b.sbuf("ones_row_b", [1, 128], BF16)
    k["h128f"] = b.sbuf("h128f", [128, 128], F32)
    k["h128b"] = b.sbuf("h128b", [128, 128], BF16)
    k["id128b"] = b.sbuf("id128b", [128, 128], BF16)
    k["id128f"] = b.sbuf("id128f", [128, 128], F32)
    k["wscol"] = b.sbuf("wscol", [128, 12], F32)
    k["ones128"] = b.sbuf("ones128", [128, 128], F32)
    k["i_ii"] = b.sbuf("i_ii", [128, 1], I32)
    k["i_jj"] = b.sbuf("i_jj", [128, 128], I32)
    k["i_hx"] = b.sbuf("i_hx", [128, 128], I32)
    k["i_hy"] = b.sbuf("i_hy", [128, 128], I32)
    k["wsc"] = b.sbuf("wsc", [1, 16], F32)
    k["rs"] = b.sbuf("rs", [128, S], F32)
    k["gq_col"] = b.sbuf("gq_col", [128, 4], F32)
    k["gkv_col"] = b.sbuf("gkv_col", [128, 4], F32)
    k["bout_col"] = b.sbuf("bout_col", [128, 16], F32)
    return k


def emit_const_setup(b, k):
    @b.on("vector")
    def _(v):
        v.memset(k["ones_col_f"][:], 1.0)
        v.memset(k["ones_row_f"][:], 1.0)
        v.memset(k["ones_col_b"][:], 1.0)
        v.memset(k["ones128"][:], 1.0)
        last = v.memset(k["ones_row_b"][:], 1.0)
        b.inc("vI", last)

    @b.on("gpsimd")
    def _(g):
        b.inc("gI", g.iota(k["i_jj"][:], [[1, 128]], channel_multiplier=0))
        b.inc("gI", g.iota(k["i_ii"][:], [[0, 1]], channel_multiplier=1))

    gI = b.mark("gI")

    @b.on("vector")
    def _(v):
        b.wait(v, "gI", gI)
        v.tensor_scalar(k["i_hx"][:], k["i_jj"][:], k["i_ii"][:, 0:1], None, OP.bitwise_and)
        for sh in (4, 2, 1):
            v.tensor_scalar(k["i_hy"][:], k["i_hx"][:], sh, None, OP.logical_shift_right)
            v.tensor_tensor(k["i_hx"][:], k["i_hx"][:], k["i_hy"][:], OP.bitwise_xor)
        v.tensor_scalar(k["i_hx"][:], k["i_hx"][:], 1, None, OP.bitwise_and)
        v.tensor_scalar(k["i_hy"][:], k["i_jj"][:], k["i_ii"][:, 0:1], None, OP.bitwise_xor)
        v.tensor_copy(k["id128f"][:], k["i_hy"][:])
        last = v.tensor_scalar(k["id128f"][:], k["id128f"][:], 0.0, None, OP.is_equal)
        b.inc("vI", last)

    vI = b.mark("vI")

    @b.on("scalar")
    def _(s):
        b.wait(s, "vI", vI)
        s.activation(k["id128b"][:], k["id128f"][:], AF.Copy, bias=0.0, scale=1.0)
        s.activation(k["h128b"][:], k["i_hx"][:], AF.Copy, bias=1.0, scale=-2.0)
        last = s.activation(k["h128f"][:], k["i_hx"][:], AF.Copy, bias=1.0, scale=-2.0)
        b.inc("init", last)



def round_ops(v, src, dst, rt, n):
    v.tensor_copy(rt["i32"][:, :n], src)
    v.tensor_copy(rt["kf"][:, :n], rt["i32"][:, :n])
    v.tensor_tensor(rt["d"][:, :n], src, rt["kf"][:, :n], OP.subtract)
    v.tensor_scalar(rt["a"][:, :n], rt["d"][:, :n], 0.5, None, OP.is_gt)
    v.tensor_scalar(rt["bb"][:, :n], rt["d"][:, :n], -0.5, None, OP.is_lt)
    v.tensor_tensor(rt["a"][:, :n], rt["a"][:, :n], rt["bb"][:, :n], OP.subtract)
    v.tensor_tensor(rt["d"][:, :n], rt["kf"][:, :n], rt["a"][:, :n], OP.add)
    return v.tensor_copy(dst, rt["d"][:, :n])


def radix4_rounds(v, tile_ap, ntiles, t1, t2):
    last = None
    stride = 1
    while stride * 4 <= ntiles:
        for g in range(ntiles // (4 * stride)):
            for off in range(stride):
                base = g * 4 * stride + off
                A = tile_ap(base)
                B = tile_ap(base + stride)
                C = tile_ap(base + 2 * stride)
                Dd = tile_ap(base + 3 * stride)
                v.tensor_tensor(t1, A, B, OP.add)
                v.tensor_tensor(t2, C, Dd, OP.add)
                v.tensor_tensor(A, A, B, OP.subtract)
                v.tensor_tensor(C, C, Dd, OP.subtract)
                v.tensor_tensor(B, A, C, OP.add)
                v.tensor_tensor(Dd, A, C, OP.subtract)
                v.tensor_tensor(A, t1, t2, OP.add)
                last = v.tensor_tensor(C, t1, t2, OP.subtract)
        stride *= 4
    assert stride == ntiles, f"ntiles={ntiles} not a power of 4"
    return last




def emit_wsc_cols(b, k, pstmp):
    wn = b.mark("wsc")

    @b.on("tensor")
    def _(t):
        b.wait(t, "init", 1)
        b.wait(t, "wsc", wn)
        last = t.matmul(pstmp[:, 0:3], k["ones_row_f"][:], k["wsc"][0:1, 0:3],
                        start=True, stop=True)
        b.inc("tS", last)

    tn = b.mark("tS")

    @b.on("vector")
    def _(v):
        b.wait(v, "tS", tn)
        last = v.tensor_copy(k["wscol"][:, 0:3], pstmp[:, 0:3])
        b.inc("wsccol", last)


def emit_tok_scales(b, k, MXA, PS2, TT, fwc, targets):
    NCK = TH // 128
    vMp = b.mark("vM")
    tX0 = b.mark("tX")
    dbg = getattr(b, "_taps", None)
    if dbg is not None and "t_mxa" in dbg:
        @b.on("gpsimd")
        def _(g):
            b.wait(g, "vM", vMp)
            dma = g.dma_start(out=dbg["t_mxa"], in_=MXA[:, :TH])
            b.inc("dTap", dma, 16)
        _tn = b.mark("dTap")
        @b.on("vector")
        def _(v):
            b.wait(v, "dTap", _tn)

    @b.on("tensor")
    def _(t):
        b.wait(t, "vM", vMp)
        for c in range(NCK):
            last = t.matmul(PS2[c % 2][:, :128], MXA[:, ts(c, 128)], k["id128f"][:],
                            start=True, stop=True)
            b.inc("tX", last)

    @b.on("vector")
    def _(v):
        for c in range(NCK):
            b.wait(v, "tX", tX0 + c + 1)
            last = v.reduce_max(TT[:, c:c + 1], PS2[c % 2][:, :128], axis=mybir.AxisListType.X)
        b.fence(v, last, "fv")
        last = v.tensor_scalar(TT[:, 4:4 + NCK], TT[:, 0:NCK], fwc, EPS, OP.mult, OP.max)
        b.fence(v, last, "fv")
        last = v.reciprocal(TT[:, 8:8 + NCK], TT[:, 4:4 + NCK])
        b.fence(v, last, "fv")
        for j, (dst, kind, widx, fmul) in enumerate(targets):
            cbase = 12 + j * NCK
            if kind == "crow":
                last = v.tensor_scalar(TT[:, cbase:cbase + NCK], TT[:, 8:8 + NCK],
                                       127.0 * fwc, None, OP.mult)
            else:
                last = v.tensor_scalar(TT[:, cbase:cbase + NCK], TT[:, 4:4 + NCK],
                                       k["wscol"][:, widx:widx + 1], fmul / 127.0,
                                       OP.mult, OP.mult)
            if j + 1 < len(targets):
                b.fence(v, last, "fv")
        b.inc("vR", last)

    if dbg is not None and "t_tt" in dbg:
        _vr = b.mark("vR")
        @b.on("gpsimd")
        def _(g):
            b.wait(g, "vR", _vr)
            dma = g.dma_start(out=dbg["t_tt"], in_=TT[:, :TH])
            b.inc("dTap", dma, 16)
        _tn2 = b.mark("dTap")
        @b.on("vector")
        def _(v):
            b.wait(v, "dTap", _tn2)

    for j, (dst, kind, widx, fmul) in enumerate(targets):
        for c in range(NCK):
            vRc = b.mark("vR")
            tXc = b.mark("tX")

            @b.on("vector")
            def _(v, j=j, c=c, vRc=vRc):
                b.wait(v, "vR", vRc)
                last = v.tensor_scalar(TT[:, 32:160], k["ones128"][:],
                                       TT[:, 12 + j * NCK + c:12 + j * NCK + c + 1],
                                       None, OP.mult)
                b.inc("vR", last)

            vRb = b.mark("vR")

            @b.on("tensor")
            def _(t, c=c, vRb=vRb):
                b.wait(t, "vR", vRb)
                last = t.matmul(PS2[c % 2][:, :128], TT[:, 32:160], k["id128f"][:],
                                start=True, stop=True)
                b.inc("tX", last)

            tXb = b.mark("tX")

            @b.on("vector")
            def _(v, dst=dst, c=c, tXb=tXb):
                b.wait(v, "tX", tXb)
                last = v.tensor_copy(dst[:, ts(c, 128)], PS2[c % 2][:, :128])
                b.inc("vR", last)



def emit_ffn_call(b, k, x_in, x_out, tern_g, tern_u, tern_d, call, rdy, taps=None):
    for half in range(TPC // TH):
        emit_ffn_half(b, k, x_in, x_out, tern_g, tern_u, tern_d,
                      f"f{call}h{half}", half, rdy, taps if half == 0 else None)


def emit_ffn_half(b, k, x_in, x_out, tern_g, tern_u, tern_d, u, half, rdy, taps=None):
    dOe_prev = b.mark("dOe")
    dOo_prev = b.mark("dOo")
    RT = {"i32": b.sbuf(u + "RTi", [128, TH], I32),
          "kf": b.sbuf(u + "RTk", [128, TH], F32),
          "d": b.sbuf(u + "RTd", [128, TH], F32),
          "a": b.sbuf(u + "RTa", [128, TH], F32),
          "bb": b.sbuf(u + "RTb", [128, TH], F32)}
    MXA = b.sbuf(u + "MXA", [128, TH], F32)
    CB = b.sbuf(u + "CB", [128, TH], F32)
    FBg = b.sbuf(u + "FBg", [128, TH], F32)
    FBu = b.sbuf(u + "FBu", [128, TH], F32)
    TT = b.sbuf(u + "TT", [128, TH], F32)
    ABT = b.sbuf(u + "ABT", [128, 2 * TH], F32)
    ROWS = b.sbuf(u + "ROWS", [1, 8 * TH], F32)
    HQa = b.sbuf(u + "HQa", [128, (NTI // 2) * TH], BF16)
    HQb = b.sbuf(u + "HQb", [128, (NTI // 2) * TH], BF16)
    XQ = b.sbuf(u + "XQ", [128, NTD * TH], BF16)
    WA = b.sbuf(u + "WA", [128, NTD * TH], F32)
    XR = b.sbuf(u + "XR", [128, 2 * TH], F32)
    SQ = b.sbuf(u + "SQ", [128, 2 * TH], F32)

    def hq(i):
        return (HQa if i < NTI // 2 else HQb)[:, ts(i % (NTI // 2), TH)]

    def row(i):
        return ROWS[0:1, ts(i, TH)]

    PSsum = b.psum(u + "PSsum", [1, TH])
    PSssq = b.psum(u + "PSssq", [1, TH])
    PSb = [b.psum(u + "PSb0", [128, TH]), b.psum(u + "PSb1", [128, TH])]

    dX0 = b.mark("dX")
    sA0 = b.mark("sA")
    tS0_sq = b.mark("tS")
    vE_prev_q = b.mark("vE")

    @b.on("gpsimd")
    def _(g):
        b.wait(g, "init", 1)
        for _c, _n in rdy:
            b.wait(g, _c, _n)
        b.wait(g, "vE", vE_prev_q)
        dma = g.dma_start(out=WA[:].rearrange("p (i t) -> p i t", i=NTD),
                          in_=x_in.rearrange("(i p) t -> p i t", p=128)[:, :, ts(half, TH)])
        b.inc("dX", dma, 16)

    @b.on("scalar")
    def _(s):
        b.wait(s, "dX", dX0 + 16)
        for i in range(NTD):
            if i >= 2:
                b.wait(s, "tS", tS0_sq + i - 1)
            last = s.activation(SQ[:, ts(i % 2, TH)], WA[:, ts(i, TH)], AF.Square)
            b.inc("sA", last)

    @b.on("tensor")
    def _(t):
        b.wait(t, "init", 1)
        for i in range(NTD):
            b.wait(t, "sA", sA0 + i + 1)
            t.matmul(PSsum[:], k["ones_col_f"][:], WA[:, ts(i, TH)],
                     start=(i == 0), stop=(i == NTD - 1))
            last = t.matmul(PSssq[:], k["ones_col_f"][:], SQ[:, ts(i % 2, TH)],
                            start=(i == 0), stop=(i == NTD - 1))
            b.inc("tS", last)

    tS_stats = b.mark("tS")

    @b.on("vector")
    def _(v):
        b.wait(v, "tS", tS_stats)
        last = v.tensor_scalar(row(0), PSsum[:], 1.0 / D, None, OP.mult)
        b.fence(v, last, "fv")
        last = v.tensor_scalar(row(1), PSssq[:], 1.0 / D, None, OP.mult)
        b.fence(v, last, "fv")
        last = v.tensor_tensor(row(2), row(0), row(0), OP.mult)
        b.fence(v, last, "fv")
        last = v.tensor_tensor(row(1), row(1), row(2), OP.subtract)
        b.fence(v, last, "fv")
        last = v.tensor_scalar(row(1), row(1), EPS, None, OP.add)
        b.inc("vR", last)

    vR1 = b.mark("vR")

    @b.on("vector")
    def _(v):
        last = v.reciprocal(row(2), row(1))
        b.inc("vR", last)

    vR1b = b.mark("vR")

    @b.on("scalar")
    def _(s):
        b.wait(s, "vR", vR1b)
        last = s.activation(row(3), row(2), AF.Sqrt)
        b.inc("sR", last)

    sR1 = b.mark("sR")

    @b.on("tensor")
    def _(t):
        b.wait(t, "sR", sR1)
        t.matmul(PSb[0][:], k["ones_row_f"][:], row(0), start=True, stop=True)
        last = t.matmul(PSb[1][:], k["ones_row_f"][:], row(3), start=True, stop=True)
        b.inc("tS", last)

    tS_bc = b.mark("tS")

    @b.on("vector")
    def _(v):
        b.wait(v, "tS", tS_bc)
        for i in range(NTD):
            v.tensor_tensor(WA[:, ts(i, TH)], WA[:, ts(i, TH)], PSb[0][:], OP.subtract)
            last = v.tensor_tensor(WA[:, ts(i, TH)], WA[:, ts(i, TH)], PSb[1][:], OP.mult)
            b.inc("vE", last)

    vE_ln = b.mark("vE")

    @b.on("vector")
    def _(v):
        b.wait(v, "vE", vE_ln)
        last = radix4_rounds(v, lambda i: WA[:, ts(i, TH)], NTD, TT[:, :TH], MXA[:, :TH])
        b.inc("vF", last)

    vF1 = b.mark("vF")
    sE0 = b.mark("sE")
    tH0 = b.mark("tH")
    vM0_a = b.mark("vM")

    @b.on("tensor")
    def _(t):
        b.wait(t, "vF", vF1)
        for i in range(NTD):
            if i >= 2:
                b.wait(t, "sE", sE0 + i - 1)
            last = t.matmul(PSb[i % 2][:], k["h128f"][:], WA[:, ts(i, TH)],
                            start=True, stop=True)
            b.inc("tH", last)

    @b.on("scalar")
    def _(s):
        for i in range(NTD):
            b.wait(s, "tH", tH0 + i + 1)
            if i >= 2:
                b.wait(s, "vM", vM0_a + i - 1)
            s.copy(WA[:, ts(i, TH)], PSb[i % 2][:])
            last = s.activation(ABT[:, ts(i % 2, TH)], PSb[i % 2][:], AF.Abs)
            b.inc("sE", last)

    sE_h = b.mark("sE")

    @b.on("vector")
    def _(v):
        v.memset(MXA[:, :TH], 0.0)
        for i in range(NTD):
            b.wait(v, "sE", sE0 + i + 1)
            last = v.tensor_tensor(MXA[:, :TH], MXA[:, :TH], ABT[:, ts(i % 2, TH)], OP.max)
            b.inc("vM", last)

    b._taps = taps
    emit_tok_scales(b, k, MXA, PSb, TT, FW2048,
                    [(CB, "crow", None, None), (FBg, "wrow", 0, 1.0), (FBu, "wrow", 1, 1.0)])
    b._taps = None
    vR2 = b.mark("vR")

    @b.on("vector")
    def _(v):
        b.wait(v, "vR", vR2)
        last = None
        for i in range(NTD):
            v.tensor_tensor(TT[:, :TH], WA[:, ts(i, TH)], CB[:, :TH], OP.mult)
            last = round_ops(v, TT[:, :TH], XQ[:, ts(i, TH)], RT, TH)
        b.inc("vQ", last)

    vQ1 = b.mark("vQ")
    b.free(u + "SQ", u + "XR", u + "WA")

    def tap(name, src_ap, ch, n, eng="vector"):
        if taps is None or name not in taps:
            return

        @b.on("gpsimd")
        def _(g):
            b.wait(g, ch, n)
            dma = g.dma_start(out=taps[name], in_=src_ap)
            b.inc("dTap", dma, 16)

        tn = b.mark("dTap")

        @b.on(eng)
        def _(e):
            b.wait(e, "dTap", tn)

    tap("t_ln", WA[:, ts(0, TH)], "vE", vE_ln)
    tap("t_fw", WA[:, ts(0, TH)], "vF", vF1)
    tap("t_hev", WA[:, ts(0, TH)], "sE", sE0 + NTD)
    tap("t_cb", CB[:, :TH], "vR", vR2)
    tap("t_fbg", FBg[:, :TH], "vR", vR2)
    tap("t_xq", XQ[:, ts(0, TH)], "vQ", vQ1)

    HBa = b.sbuf(u + "HBa", [128, (NTI // 2) * TH], F32)
    HBb = b.sbuf(u + "HBb", [128, (NTI // 2) * TH], F32)
    GW = b.sbuf(u + "GW", [128, 2 * D], BF16)
    UW = b.sbuf(u + "UW", [128, 2 * D], BF16)
    GE = b.sbuf(u + "GE", [128, 2 * TH], F32)
    UE = b.sbuf(u + "UE", [128, 2 * TH], F32)
    SIL = b.sbuf(u + "SIL", [128, 2 * TH], F32)
    PSg = [b.psum(u + "PSg0", [128, TH]), b.psum(u + "PSg1", [128, TH])]
    PSu = [b.psum(u + "PSu0", [128, TH]), b.psum(u + "PSu1", [128, TH])]

    def hb(i):
        return (HBa if i < NTI // 2 else HBb)[:, ts(i % (NTI // 2), TH)]

    dWe0 = b.mark("dWe")
    dWo0 = b.mark("dWo")
    tM0 = b.mark("tM")
    vE_gu0 = b.mark("vE")
    sA_sil0 = b.mark("sA")
    vH0 = b.mark("vH")

    @b.on("sync")
    def _(sy):
        for _c, _n in rdy:
            b.wait(sy, _c, _n)
        b.wait(sy, "dOe", dOe_prev)
        b.wait(sy, "dOo", dOo_prev)
        b.wait(sy, "vQ", vQ1)
        for ot in range(NTI):
            if ot >= 2:
                b.wait(sy, "tM", tM0 + ot - 1)
            sl, lt = divmod(ot, 8)
            ch = "dWe" if ot % 2 == 0 else "dWo"
            dma = sy.dma_start(out=GW[:, ts(ot % 2, D)],
                               in_=tern_g[sl, lt].rearrange("p k o -> p (k o)"))
            b.inc(ch, dma, 16)
            dma = sy.dma_start(out=UW[:, ts(ot % 2, D)],
                               in_=tern_u[sl, lt].rearrange("p k o -> p (k o)"))
            b.inc(ch, dma, 16)

    @b.on("tensor")
    def _(t):
        b.wait(t, "vQ", vQ1)
        for ot in range(NTI):
            ch = "dWe" if ot % 2 == 0 else "dWo"
            b.wait(t, ch, (dWe0 if ot % 2 == 0 else dWo0) + 32 * (ot // 2 + 1))
            if ot >= 2:
                b.wait(t, "vH", vH0 + ot - 1)
            pg, pu = PSg[ot % 2], PSu[ot % 2]
            for kk in range(NTD):
                t.matmul(pg[:], GW[:, (ot % 2) * D + kk * 128:(ot % 2) * D + kk * 128 + 128],
                         XQ[:, ts(kk, TH)], start=(kk == 0), stop=(kk == NTD - 1))
                last = t.matmul(pu[:], UW[:, (ot % 2) * D + kk * 128:(ot % 2) * D + kk * 128 + 128],
                                XQ[:, ts(kk, TH)], start=(kk == 0), stop=(kk == NTD - 1))
            b.inc("tM", last)

    @b.on("scalar")
    def _(s):
        for ot in range(NTI):
            b.wait(s, "vE", vE_gu0 + 2 * ot + 1)
            last = s.activation(SIL[:, ts(ot % 2, TH)], GE[:, ts(ot % 2, TH)], AF.Silu)
            b.inc("sA", last)

    @b.on("vector")
    def _(v):
        for ot in range(NTI):
            b.wait(v, "tM", tM0 + ot + 1)
            last = v.tensor_tensor(GE[:, ts(ot % 2, TH)], PSg[ot % 2][:], FBg[:, :TH], OP.mult)
            b.inc("vE", last)
            last = v.tensor_tensor(UE[:, ts(ot % 2, TH)], PSu[ot % 2][:], FBu[:, :TH], OP.mult)
            b.inc("vE", last)
            if ot >= 1:
                b.wait(v, "sA", sA_sil0 + ot)
                last = v.tensor_tensor(hb(ot - 1), SIL[:, ts((ot - 1) % 2, TH)],
                                       UE[:, ts((ot - 1) % 2, TH)], OP.mult)
                b.inc("vH", last)
        b.wait(v, "sA", sA_sil0 + NTI)
        last = v.tensor_tensor(hb(NTI - 1), SIL[:, ts((NTI - 1) % 2, TH)],
                               UE[:, ts((NTI - 1) % 2, TH)], OP.mult)
        b.inc("vH", last)

    vH_gu = b.mark("vH")
    tap("t_hb", hb(0), "vH", vH_gu)
    b.free(u + "SIL", u + "UE", u + "GE", u + "UW", u + "GW")

    @b.on("vector")
    def _(v):
        b.wait(v, "vH", vH_gu)
        last = radix4_rounds(v, hb, NTI, TT[:, :TH], MXA[:, :TH])
        b.inc("vF", last)

    vF2 = b.mark("vF")
    sE1 = b.mark("sE")
    tH1 = b.mark("tH")
    vM0_b = b.mark("vM")

    @b.on("tensor")
    def _(t):
        b.wait(t, "vF", vF2)
        for i in range(NTI):
            if i >= 2:
                b.wait(t, "sE", sE1 + i - 1)
            last = t.matmul(PSg[i % 2][:], k["h128f"][:], hb(i), start=True, stop=True)
            b.inc("tH", last)

    @b.on("scalar")
    def _(s):
        for i in range(NTI):
            b.wait(s, "tH", tH1 + i + 1)
            if i >= 2:
                b.wait(s, "vM", vM0_b + i - 1)
            s.copy(hb(i), PSg[i % 2][:])
            last = s.activation(ABT[:, ts(i % 2, TH)], PSg[i % 2][:], AF.Abs)
            b.inc("sE", last)

    sE_h2 = b.mark("sE")

    @b.on("vector")
    def _(v):
        v.memset(MXA[:, :TH], 0.0)
        for i in range(NTI):
            b.wait(v, "sE", sE1 + i + 1)
            last = v.tensor_tensor(MXA[:, :TH], MXA[:, :TH], ABT[:, ts(i % 2, TH)], OP.max)
            b.inc("vM", last)

    emit_tok_scales(b, k, MXA, PSg, TT, FW8192,
                    [(CB, "crow", None, None), (FBg, "wrow", 2, 0.5)])
    vE_cb2 = b.mark("vR")

    @b.on("vector")
    def _(v):
        b.wait(v, "vR", vE_cb2)
        last = None
        for i in range(NTI):
            v.tensor_tensor(TT[:, :TH], hb(i), CB[:, :TH], OP.mult)
            last = round_ops(v, TT[:, :TH], hq(i), RT, TH)
        b.inc("vQ", last)

    vQ2 = b.mark("vQ")
    tap("t_hq", hq(0), "vQ", vQ2)
    tap("t_hb2", hb(0), "sE", sE1 + NTI)
    tap("t_cb2", CB[:, :TH], "vR", b.mark("vR"))
    b.free(u + "HBb", u + "HBa")

    DW = b.sbuf(u + "DW", [128, 2 * 16 * 128], BF16)
    XO = b.sbuf(u + "XO", [128, 2 * TH], F32)
    NDW = NTI // 16
    dDe0 = b.mark("dDe")
    dDo0 = b.mark("dDo")
    dXe0 = b.mark("dXe")
    dXo0 = b.mark("dXo")
    dOe0 = b.mark("dOe")
    dOo0 = b.mark("dOo")
    tM1 = b.mark("tM")
    vE_d0 = b.mark("vE")

    @b.on("sync")
    def _(sy):
        b.wait(sy, "vQ", vQ2)
        nload = 0
        for dt in range(NTD):
            sl, lt = divmod(dt, 2)
            for rr in range(NDW):
                if nload >= 2:
                    b.wait(sy, "tM", tM1 + nload - 1)
                ch = "dDe" if nload % 2 == 0 else "dDo"
                dma = sy.dma_start(out=DW[:, ts(nload % 2, 16 * 128)],
                                   in_=tern_d[sl, lt].rearrange("p k o -> p (k o)")[:, ts(rr, 16 * 128)])
                b.inc(ch, dma, 16)
                nload += 1

    @b.on("gpsimd")
    def _(g):
        b.wait(g, "vQ", vQ2)
        for dt in range(NTD):
            lch = "dXe" if dt % 2 == 0 else "dXo"
            och = "dOe" if dt % 2 == 0 else "dOo"
            if dt >= 2:
                b.wait(g, och, (dOe0 if dt % 2 == 0 else dOo0) + 16 * (dt // 2))
            dma = g.dma_start(out=XO[:, ts(dt % 2, TH)], in_=x_in[ts(dt, 128), ts(half, TH)])
            b.inc(lch, dma, 16)
            b.wait(g, "vE", vE_d0 + dt + 1)
            dma = g.dma_start(out=x_out[ts(dt, 128), ts(half, TH)], in_=XO[:, ts(dt % 2, TH)])
            b.inc(och, dma, 16)

    @b.on("tensor")
    def _(t):
        b.wait(t, "vQ", vQ2)
        nload = 0
        for dt in range(NTD):
            if dt >= 2:
                b.wait(t, "vE", vE_d0 + (dt - 1))
            ps = PSu[dt % 2]
            for rr in range(NDW):
                ch = "dDe" if nload % 2 == 0 else "dDo"
                b.wait(t, ch, (dDe0 if nload % 2 == 0 else dDo0) + 16 * (nload // 2 + 1))
                for kk in range(16):
                    gk = rr * 16 + kk
                    last = t.matmul(ps[:], DW[:, (nload % 2) * 2048 + kk * 128:(nload % 2) * 2048 + kk * 128 + 128],
                                    hq(gk), start=(gk == 0), stop=(gk == NTI - 1))
                b.inc("tM", last)
                nload += 1

    @b.on("vector")
    def _(v):
        for dt in range(NTD):
            b.wait(v, "tM", tM1 + NDW * (dt + 1))
            lch = "dXe" if dt % 2 == 0 else "dXo"
            b.wait(v, lch, (dXe0 if dt % 2 == 0 else dXo0) + 16 * (dt // 2 + 1))
            v.tensor_tensor(TT[:, :TH], PSu[dt % 2][:], FBg[:, :TH], OP.mult)
            last = v.tensor_tensor(XO[:, ts(dt % 2, TH)], XO[:, ts(dt % 2, TH)], TT[:, :TH], OP.add)
            b.inc("vE", last)

    dOe_end = b.mark("dOe")
    dOo_end = b.mark("dOo")

    @b.on("gpsimd")
    def _(g):
        b.wait(g, "dOe", dOe_end)
        b.wait(g, "dOo", dOo_end)

    b.free(u + "XO", u + "DW", u + "XQ", u + "HQb", u + "HQa",
           u + "ROWS", u + "ABT", u + "TT", u + "FBu", u + "FBg", u + "CB", u + "MXA",
           u + "RTb", u + "RTa", u + "RTd", u + "RTk", u + "RTi",
           u + "PSu1", u + "PSu0", u + "PSg1", u + "PSg0",
           u + "PSb1", u + "PSb0", u + "PSssq", u + "PSsum")



def build_ffn_test():
    nc = bass.Bass("TRN2", target_bir_lowering=False, debug=False, detect_race_conditions=False)
    x_in = nc.dram_tensor("x_in", [D, TPC], F32, kind="ExternalInput").ap()
    tg = nc.dram_tensor("tg", [NCORE, 8, 128, NTD, 128], BF16, kind="ExternalInput").ap()
    tu = nc.dram_tensor("tu", [NCORE, 8, 128, NTD, 128], BF16, kind="ExternalInput").ap()
    td = nc.dram_tensor("td", [NCORE, 2, 128, NTI, 128], BF16, kind="ExternalInput").ap()
    wsc_in = nc.dram_tensor("wsc_in", [1, 16], F32, kind="ExternalInput").ap()
    x_out = nc.dram_tensor("x_out", [D, TPC], F32, kind="ExternalOutput").ap()
    tapnames = ["t_ln", "t_fw", "t_hev", "t_cb", "t_fbg", "t_hb", "t_hb2", "t_cb2", "t_mxa", "t_tt"]
    taps = {n: nc.dram_tensor(n, [128, TH], F32, kind="ExternalOutput").ap()[:, :]
            for n in tapnames}
    taps["t_xq"] = nc.dram_tensor("t_xq", [128, TH], BF16, kind="ExternalOutput").ap()[:, :]
    taps["t_hq"] = nc.dram_tensor("t_hq", [128, TH], BF16, kind="ExternalOutput").ap()[:, :]

    b = Build(nc)
    k = alloc_persist(b)
    emit_const_setup(b, k)

    @b.on("gpsimd")
    def _(g):
        dma = g.dma_start(out=k["wsc"][:], in_=wsc_in[:, :])
        b.inc("wsc", dma, 16)

    pstmp = b.psum("pstmp_wsc", [128, 8])
    emit_wsc_cols(b, k, pstmp)
    b.free("pstmp_wsc")
    wn = b.mark("wsccol")
    emit_ffn_call(b, k, x_in, x_out, tg, tu, td, 0, [("wsccol", wn)], taps=taps)
    b.realize()
    return nc



def emit_rope(b, k, pos_row, dstkey="rs", slen=S, tag=""):
    RSC = b.sbuf(tag + "rsc", [64, 512], F32)
    THR = b.sbuf(tag + "thr", [1, 128], F32)
    JR = b.sbuf(tag + "jr", [1, 128], I32)
    PR = b.sbuf(tag + "pr", [1, slen], F32)
    RK = {"i32": b.sbuf(tag + "rk_i", [128, 512], I32), "kf": b.sbuf(tag + "rk_k", [128, 512], F32),
          "d": b.sbuf(tag + "rk_d", [128, 512], F32), "a": b.sbuf(tag + "rk_a", [128, 512], F32),
          "bb": b.sbuf(tag + "rk_b", [128, 512], F32)}
    W1 = b.sbuf(tag + "rk_w", [128, 512], F32)
    PSR = b.psum(tag + "psr", [128, 512])
    TWO_PI = 2.0 * math.pi
    LN1E4 = math.log(10000.0)

    dRp0 = b.mark("dRp")

    @b.on("gpsimd")
    def _(g):
        dma = g.dma_start(out=PR[:], in_=pos_row[:, :])
        b.inc("dRp", dma, 16)
        last = g.iota(JR[:], [[1, 128]], channel_multiplier=0)
        b.inc("gI2", last)

    gI2 = b.mark("gI2")

    @b.on("vector")
    def _(v):
        b.wait(v, "gI2", gI2)
        last = v.tensor_copy(THR[:], JR[:])
        b.fence(v, last, "fv")
        last = v.tensor_copy(RSC[0:1, 128:256], THR[:])
        b.fence(v, last, "fv")
        for ii, thr_c in enumerate((32.0, 64.0, 96.0)):
            last = v.tensor_scalar(RSC[0:1, 0:128], RSC[0:1, 128:256], thr_c, -32.0,
                                   OP.is_ge, OP.mult)
            b.fence(v, last, "fv")
            last = v.tensor_tensor(THR[:], THR[:], RSC[0:1, 0:128], OP.add)
            if ii < 2:
                b.fence(v, last, "fv")
        b.inc("vRo", last)

    vRo1 = b.mark("vRo")

    @b.on("scalar")
    def _(s):
        b.wait(s, "vRo", vRo1)
        last = s.activation(THR[:], THR[:], AF.Exp, bias=0.0, scale=-LN1E4 / 64.0)
        b.inc("sRo", last)

    sRo1 = b.mark("sRo")

    vRo_base = b.mark("vRo")

    @b.on("tensor")
    def _(t):
        b.wait(t, "init", 1)
        b.wait(t, "sRo", sRo1)
        b.wait(t, "dRp", dRp0 + 16)
        for c in range(slen // 512):
            if c >= 1:
                b.wait(t, "vRo", vRo_base + c)
            last = t.matmul(PSR[:], THR[:], PR[0:1, ts(c, 512)], start=True, stop=True)
            b.inc("tRo", last)

    tRo_b = b.mark("tRo") - slen // 512

    @b.on("vector")
    def _(v):
        C1 = 12868.0 / 2048.0
        C2 = TWO_PI - C1
        for c in range(slen // 512):
            b.wait(v, "tRo", tRo_b + c + 1)
            last = v.tensor_scalar(W1[:], PSR[:], math.pi / 4.0, 1.0 / TWO_PI, OP.add, OP.mult)
            b.fence(v, last, "fv")
            last = v.tensor_copy(RK["i32"][:], W1[:])
            b.fence(v, last, "fv")
            last = v.tensor_copy(RK["kf"][:], RK["i32"][:])
            b.fence(v, last, "fv")
            last = v.tensor_scalar(RK["a"][:], RK["kf"][:], -C1, None, OP.mult)
            b.fence(v, last, "fv")
            last = v.tensor_tensor(W1[:], PSR[:], RK["a"][:], OP.add)
            b.fence(v, last, "fv")
            last = v.tensor_scalar(RK["bb"][:], RK["kf"][:], -C2, math.pi / 4.0,
                                   OP.mult, OP.add)
            b.fence(v, last, "fv")
            last = v.tensor_tensor(W1[:], W1[:], RK["bb"][:], OP.add)
            b.fence(v, last, "fv")
            last = v.tensor_scalar(RK["a"][:], W1[:], math.pi, -TWO_PI, OP.is_gt, OP.mult)
            b.fence(v, last, "fv")
            last = v.tensor_tensor(W1[:], W1[:], RK["a"][:], OP.add)
            b.fence(v, last, "fv")
            last = v.tensor_scalar(RK["a"][:], W1[:], -math.pi, TWO_PI, OP.is_lt, OP.mult)
            b.fence(v, last, "fv")
            last = v.tensor_tensor(k[dstkey][:, ts(c, 512)], W1[:], RK["a"][:], OP.add)
            b.inc("vRo", last)

    vRo2 = b.mark("vRo")

    @b.on("scalar")
    def _(s):
        for c in range(slen // 512):
            b.wait(s, "vRo", vRo_base + c + 1)
            last = s.activation(k[dstkey][:, ts(c, 512)], k[dstkey][:, ts(c, 512)], AF.Sin, bias=0.0, scale=1.0)
            b.inc("sRo", last)

    sRo2 = b.mark("sRo")

    @b.on("vector")
    def _(v):
        b.wait(v, "sRo", sRo2)
        last = v.tensor_scalar(k[dstkey][:], k[dstkey][:], math.sqrt(2.0), None, OP.mult)
        b.inc("rope", last)

    b.free(tag + "psr", tag + "rk_w", tag + "rk_b", tag + "rk_a", tag + "rk_d",
           tag + "rk_k", tag + "rk_i", tag + "pr", tag + "jr", tag + "thr", tag + "rsc")



def emit_weight_prep(b, k, io):
    WB = b.sbuf("wpWB", [128, 2 * 512], F32)
    AB = b.sbuf("wpAB", [128, 2 * 512], F32)
    TB = b.sbuf("wpTB", [128, 512], BF16)
    TB2 = b.sbuf("wpTB2", [128, 512], F32)
    TO = b.sbuf("wpTO", [128, 512], BF16)
    ARR = b.sbuf("wpAR", [1, 16], F32)
    PSW = b.psum("wpPSW", [1, 512])
    PSX = b.psum("wpPSX", [128, 512])

    slices = [(io["wg_rows"], IFF // 8, D, 0), (io["wu_rows"], IFF // 8, D, 1),
              (io["wd_rows"], D // 8, IFF, 2)]
    ropeN_wp = b.mark("rope")

    for wi, (src, R, Cc, widx) in enumerate(slices):
        ntile = (R // 128) * (Cc // 512)
        dch = "dWPe" if True else ""
        d0e = b.mark("dWPe")
        d0o = b.mark("dWPo")
        sA0 = b.mark("sA")
        vM0 = b.mark("vM")

        @b.on("gpsimd")
        def _(g, src=src, R=R, Cc=Cc, ntile=ntile, d0e=d0e, d0o=d0o, sA0=sA0):
            b.wait(g, "init", 1)
            b.wait(g, "rope", ropeN_wp)
            for i in range(ntile):
                ch = "dWPe" if i % 2 == 0 else "dWPo"
                b.wait(g, "sA", sA0 + (i - 1 if i >= 2 else 0))
                r, c = divmod(i, Cc // 512)
                dma = g.dma_start(out=WB[:, ts(i % 2, 512)],
                                  in_=src[ts(r, 128), ts(c, 512)])
                b.inc(ch, dma, 16)

        @b.on("scalar")
        def _(s, ntile=ntile, d0e=d0e, d0o=d0o, tS0=b.mark("tS")):
            b.wait(s, "tS", tS0)
            for i in range(ntile):
                ch = "dWPe" if i % 2 == 0 else "dWPo"
                b.wait(s, ch, (d0e if i % 2 == 0 else d0o) + 16 * (i // 2 + 1))
                last = s.activation(AB[:, ts(i % 2, 512)], WB[:, ts(i % 2, 512)], AF.Abs)
                b.inc("sA", last)

        @b.on("tensor")
        def _(t, ntile=ntile, sA0=sA0, vM0=vM0):
            b.wait(t, "init", 1)
            b.wait(t, "vM", vM0)
            for i in range(ntile):
                b.wait(t, "sA", sA0 + i + 1)
                last = t.matmul(PSW[:], k["ones_col_f"][:], AB[:, ts(i % 2, 512)],
                                start=(i == 0), stop=(i == ntile - 1))
            b.inc("tS", last)

        tSw = b.mark("tS")

        @b.on("vector")
        def _(v, wi=wi, tSw=tSw):
            if wi == 0:
                v.memset(ARR[:], 0.0)
            b.wait(v, "tS", tSw)
            last = v.reduce_sum(ARR[0:1, wi:wi + 1], PSW[:], axis=mybir.AxisListType.X)
            b.inc("vM", last)

    vM_ar = b.mark("vM")
    if PHASE_LIMIT == 10:
        return [("vM", vM_ar)]

    @b.on("gpsimd")
    def _(g):
        b.wait(g, "vM", vM_ar)
        dma = g.dma_start(out=io["ar_in"][:, :], in_=ARR[0:1, 0:8])
        b.inc("dAR", dma, 16)

    dAR1 = b.mark("dAR")

    @b.on("gpsimd")
    def _(g):
        b.wait(g, "dAR", dAR1)
        cc = g.collective_compute("AllReduce", OP.add,
                                  replica_groups=[list(range(NCORE))],
                                  ins=[io["ar_in"].opt()],
                                  outs=[io["ar_out"].opt()])
        b.inc("ccAR", cc)

    ccAR1 = b.mark("ccAR")

    @b.on("gpsimd")
    def _(g):
        b.wait(g, "ccAR", ccAR1)
        dma = g.dma_start(out=ARR[0:1, 8:16], in_=io["ar_out"][:, :])
        b.inc("dAR", dma, 16)

    dAR2 = b.mark("dAR")
    if PHASE_LIMIT == 11:
        return [("dAR", dAR2)]

    @b.on("vector")
    def _(v):
        b.wait(v, "dAR", dAR2)
        last = v.tensor_scalar(k["wsc"][0:1, 0:3], ARR[0:1, 8:11], 1.0 / (IFF * D), EPS,
                               OP.mult, OP.max)
        b.inc("wsc", last)

    wsc1 = b.mark("wsc")

    @b.on("tensor")
    def _(t):
        b.wait(t, "wsc", wsc1)
        last = t.matmul(PSX[:, 0:3], k["ones_row_f"][:], k["wsc"][0:1, 0:3],
                        start=True, stop=True)
        b.inc("tS", last)

    tS_wc = b.mark("tS")

    @b.on("vector")
    def _(v):
        b.wait(v, "tS", tS_wc)
        last = v.tensor_copy(k["wscol"][:, 0:3], PSX[:, 0:3])
        b.fence(v, last, "fv")
        last = v.tensor_scalar(k["wscol"][:, 3:6], k["wscol"][:, 0:3], 0.5, None, OP.mult)
        b.fence(v, last, "fv")
        last = v.tensor_scalar(k["wscol"][:, 6:9], k["wscol"][:, 3:6], -1.0, None, OP.mult)
        b.inc("wsccol", last)

    wcol = b.mark("wsccol")
    if PHASE_LIMIT == 12:
        return [("wsccol", wcol)]

    for wi, (src, R, Cc, widx) in enumerate(slices):
        dst = [io["ag_g_in"], io["ag_u_in"], io["ag_d_in"]][wi]
        ntA = R // 128
        ntB = Cc // 512
        d1e = b.mark("dWQe")
        d1o = b.mark("dWQo")
        vT0 = b.mark("vT")
        tX1 = b.mark("tX")
        sE1 = b.mark("sE")
        dTo0 = b.mark("dTo")

        @b.on("gpsimd")
        def _(g, src=src, ntA=ntA, ntB=ntB, vT0=vT0, sE1=sE1, dst=dst):
            def store(i):
                b.wait(g, "sE", sE1 + 4 * (i + 1))
                a, c = divmod(i, ntB)
                dma = g.dma_start(
                    out=dst[a, :, 4 * c:4 * c + 4, :],
                    in_=TO[:, :512].rearrange("p (cc o) -> p cc o", cc=4))
                b.inc("dTo", dma, 16)
            for i in range(ntA * ntB):
                ch = "dWQe" if i % 2 == 0 else "dWQo"
                b.wait(g, "vT", vT0 + (i - 1 if i >= 2 else 0))
                a, c = divmod(i, ntB)
                dma = g.dma_start(out=WB[:, ts(i % 2, 512)],
                                  in_=src[ts(a, 128), ts(c, 512)])
                b.inc(ch, dma, 16)
                if i >= 1:
                    store(i - 1)
            store(ntA * ntB - 1)

        @b.on("vector")
        def _(v, ntA=ntA, ntB=ntB, d1e=d1e, d1o=d1o, widx=widx, sE1=sE1, tX1=tX1):
            b.wait(v, "wsccol", wcol)
            b.wait(v, "tX", tX1)
            for i in range(ntA * ntB):
                ch = "dWQe" if i % 2 == 0 else "dWQo"
                b.wait(v, ch, (d1e if i % 2 == 0 else d1o) + 16 * (i // 2 + 1))
                if i >= 1:
                    b.wait(v, "sE", sE1 + 4 * i)
                v.tensor_scalar(TB2[:, :512], WB[:, ts(i % 2, 512)],
                                k["wscol"][:, 3 + widx:4 + widx], None, OP.is_gt)
                v.tensor_scalar(AB[:, ts(i % 2, 512)], WB[:, ts(i % 2, 512)],
                                k["wscol"][:, 6 + widx:7 + widx], None, OP.is_lt)
                last = v.tensor_tensor(TB[:, :512], TB2[:, :512], AB[:, ts(i % 2, 512)],
                                       OP.subtract)
                b.inc("vT", last)

        @b.on("tensor")
        def _(t, ntA=ntA, ntB=ntB, vT0=vT0, sE1=sE1):
            for i in range(ntA * ntB):
                b.wait(t, "vT", vT0 + i + 1)
                b.wait(t, "sE", sE1 + 4 * i)
                for c4 in range(4):
                    last = t.matmul(PSX[:, ts(c4, 128)], TB[:, ts(c4, 128)],
                                    k["id128b"][:], start=True, stop=True)
                b.inc("tX", last)

        @b.on("scalar")
        def _(s, ntA=ntA, ntB=ntB, tX1=tX1, dTo0=dTo0):
            for i in range(ntA * ntB):
                b.wait(s, "tX", tX1 + i + 1)
                b.wait(s, "dTo", dTo0 + 16 * i)
                for c4 in range(4):
                    last = s.copy(TO[:, ts(c4, 128)], PSX[:, ts(c4, 128)])
                b.inc("sE", last, 4)



    dTo_all = b.mark("dTo")
    if PHASE_LIMIT == 13:
        return [("dTo", dTo_all)]

    @b.on("gpsimd")
    def _(g):
        b.wait(g, "dTo", dTo_all)
        for ji, (key_in, key_out) in enumerate((("ag_g_in", "ag_g"), ("ag_u_in", "ag_u"),
                                                ("ag_d_in", "ag_d"))):
            cc = g.collective_compute("AllGather", OP.bypass,
                                      replica_groups=[list(range(NCORE))],
                                      ins=[io[key_in].opt()],
                                      outs=[io[key_out].opt()])
            b.inc("ternAG" if ji == 2 else "ccAG", cc)

    b.free("wpPSX", "wpPSW", "wpAR", "wpTO", "wpTB2", "wpTB", "wpAB", "wpWB")



def emit_proj(b, k, io, gates):
    x1 = io["x1"]
    NQ = TPC // TH
    LNW = b.sbuf("pjLNW", [128, NTD * TH], F32)
    SQ2 = b.sbuf("pjSQ", [128, 2 * TH], F32)
    WDQ = b.sbuf("pjWDQ", [128, 16 * 512], F32)
    WDKV = b.sbuf("pjWDKV", [128, 16 * 512], F32)
    OUTT = b.sbuf("pjOUT", [128, 16 * TH], F32)
    OHL = b.sbuf("pjOHL", [128, 2 * TH], BF16)
    ROWS2 = b.sbuf("pjROWS", [1, 8 * TH], F32)
    PS1 = b.psum("pjPS1", [1, TH])
    PS2a = b.psum("pjPS2a", [1, TH])
    PSB = [b.psum("pjPSB0", [128, TH]), b.psum("pjPSB1", [128, TH])]
    PSC = [b.psum("pjPSC0", [128, TH]), b.psum("pjPSC1", [128, TH])]

    def row2(i):
        return ROWS2[0:1, ts(i, TH)]

    @b.on("sync")
    def _(sy):
        for _c, _n in gates:
            b.wait(sy, _c, _n)
        dma = sy.dma_start(out=WDQ[:].rearrange("p (h c) -> p h c", h=16),
                           in_=io["Wdq"].rearrange("(h p) c -> p h c", p=128))
        b.inc("dPJ", dma, 16)
        dma = sy.dma_start(out=WDKV[:].rearrange("p (h c) -> p h c", h=16),
                           in_=io["Wdkv"].rearrange("(h p) c -> p h c", p=128))
        b.inc("dPJ", dma, 16)

    dPJw = b.mark("dPJ")

    for q in range(NQ):
        dXq0 = b.mark("dXq")
        sAq0 = b.mark("sA")
        tSq0 = b.mark("tS")
        dAGq0 = b.mark("dAGq")

        @b.on("gpsimd")
        def _(g, q=q, dAGq0=dAGq0):
            for _c, _n in gates:
                b.wait(g, _c, _n)
            if q > 0:
                b.wait(g, "dAGq", dAGq0)
            dma = g.dma_start(out=LNW[:].rearrange("p (i t) -> p i t", i=NTD),
                              in_=x1.rearrange("(i p) t -> p i t", p=128)[:, :, ts(q, TH)])
            b.inc("dXq", dma, 16)

        @b.on("scalar")
        def _(s, dXq0=dXq0, tSq0=tSq0):
            b.wait(s, "dXq", dXq0 + 16)
            for i in range(NTD):
                if i >= 2:
                    b.wait(s, "tS", tSq0 + i - 1)
                last = s.activation(SQ2[:, ts(i % 2, TH)], LNW[:, ts(i, TH)], AF.Square)
                b.inc("sA", last)

        @b.on("tensor")
        def _(t, sAq0=sAq0):
            b.wait(t, "init", 1)
            for i in range(NTD):
                b.wait(t, "sA", sAq0 + i + 1)
                t.matmul(PS1[:], k["ones_col_f"][:], LNW[:, ts(i, TH)],
                         start=(i == 0), stop=(i == NTD - 1))
                last = t.matmul(PS2a[:], k["ones_col_f"][:], SQ2[:, ts(i % 2, TH)],
                                start=(i == 0), stop=(i == NTD - 1))
                b.inc("tS", last)

        tS_st = b.mark("tS")

        @b.on("vector")
        def _(v, tS_st=tS_st):
            b.wait(v, "tS", tS_st)
            last = v.tensor_scalar(row2(0), PS1[:], 1.0 / D, None, OP.mult)
            b.fence(v, last, "fv")
            last = v.tensor_scalar(row2(1), PS2a[:], 1.0 / D, None, OP.mult)
            b.fence(v, last, "fv")
            last = v.tensor_tensor(row2(2), row2(0), row2(0), OP.mult)
            b.fence(v, last, "fv")
            last = v.tensor_tensor(row2(1), row2(1), row2(2), OP.subtract)
            b.fence(v, last, "fv")
            last = v.tensor_scalar(row2(1), row2(1), EPS, None, OP.add)
            b.fence(v, last, "fv")
            last = v.reciprocal(row2(2), row2(1))
            b.inc("vR", last)

        vR_r = b.mark("vR")

        @b.on("scalar")
        def _(s, vR_r=vR_r):
            b.wait(s, "vR", vR_r)
            last = s.activation(row2(3), row2(2), AF.Sqrt)
            b.inc("sR", last)

        sR_r = b.mark("sR")

        @b.on("tensor")
        def _(t, sR_r=sR_r):
            b.wait(t, "sR", sR_r)
            t.matmul(PSB[0][:], k["ones_row_f"][:], row2(0), start=True, stop=True)
            last = t.matmul(PSB[1][:], k["ones_row_f"][:], row2(3), start=True, stop=True)
            b.inc("tS", last)

        tS_bc = b.mark("tS")

        @b.on("vector")
        def _(v, tS_bc=tS_bc):
            b.wait(v, "tS", tS_bc)
            for i in range(NTD):
                v.tensor_tensor(LNW[:, ts(i, TH)], LNW[:, ts(i, TH)], PSB[0][:], OP.subtract)
                last = v.tensor_tensor(LNW[:, ts(i, TH)], LNW[:, ts(i, TH)], PSB[1][:], OP.mult)
            b.inc("vE", last)

        vE_ln2 = b.mark("vE")

        tM_p0 = b.mark("tM")

        @b.on("tensor")
        def _(t, vE_ln2=vE_ln2):
            b.wait(t, "vE", vE_ln2)
            b.wait(t, "dPJ", dPJw)
            for j in range(8):
                W = WDQ if j < 4 else WDKV
                cch = j % 4
                ps = PSC[j % 2]
                if j >= 2:
                    b.wait(t, "vE", vE_ln2 + j - 1)
                for hh in range(16):
                    last = t.matmul(ps[:], W[:, hh * 512 + cch * 128: hh * 512 + cch * 128 + 128],
                                    LNW[:, ts(hh, TH)], start=(hh == 0), stop=(hh == 15))
                b.inc("tM", last)

        @b.on("vector")
        def _(v, tM_p0=tM_p0):
            for j in range(8):
                b.wait(v, "tM", tM_p0 + j + 1)
                last = v.tensor_copy(OUTT[:, ts(j, TH)], PSC[j % 2][:])
                b.inc("vE", last)

        vE_craw = b.mark("vE")
        sA_r0 = b.mark("sA")
        tS_v0 = b.mark("tS")

        @b.on("scalar")
        def _(s, vE_craw=vE_craw, tS_v0=tS_v0):
            for j in range(8):
                b.wait(s, "vE", vE_craw - 8 + j + 1)
                if j >= 2:
                    b.wait(s, "tS", tS_v0 + j - 1)
                last = s.activation(SQ2[:, ts(j % 2, TH)], OUTT[:, ts(j, TH)], AF.Square)
                b.inc("sA", last)

        @b.on("tensor")
        def _(t, sA_r0=sA_r0):
            for j in range(8):
                b.wait(t, "sA", sA_r0 + j + 1)
                last = t.matmul(PS1[:] if j < 4 else PS2a[:], k["ones_col_f"][:],
                                SQ2[:, ts(j % 2, TH)], start=(j % 4 == 0), stop=(j % 4 == 3))
                b.inc("tS", last)

        tS_rms = b.mark("tS")

        @b.on("vector")
        def _(v, tS_rms=tS_rms):
            b.wait(v, "tS", tS_rms)
            last = v.tensor_scalar(row2(4), PS1[:], 1.0 / DKV, EPS, OP.mult, OP.add)
            b.fence(v, last, "fv")
            last = v.tensor_scalar(row2(5), PS2a[:], 1.0 / DKV, EPS, OP.mult, OP.add)
            b.fence(v, last, "fv")
            last = v.reciprocal(row2(4), row2(4))
            b.fence(v, last, "fv")
            last = v.reciprocal(row2(5), row2(5))
            b.inc("vR", last)

        vR_rms = b.mark("vR")

        @b.on("scalar")
        def _(s, vR_rms=vR_rms):
            b.wait(s, "vR", vR_rms)
            s.activation(row2(4), row2(4), AF.Sqrt)
            last = s.activation(row2(5), row2(5), AF.Sqrt)
            b.inc("sR", last)

        sR_rms = b.mark("sR")

        @b.on("tensor")
        def _(t, sR_rms=sR_rms):
            b.wait(t, "sR", sR_rms)
            t.matmul(PSB[0][:], k["ones_row_f"][:], row2(4), start=True, stop=True)
            last = t.matmul(PSB[1][:], k["ones_row_f"][:], row2(5), start=True, stop=True)
            b.inc("tS", last)

        tS_rbc = b.mark("tS")

        @b.on("vector")
        def _(v, tS_rbc=tS_rbc):
            b.wait(v, "tS", tS_rbc)
            for j in range(8):
                gcol = k["gq_col"] if j < 4 else k["gkv_col"]
                v.tensor_tensor(OUTT[:, ts(j, TH)], OUTT[:, ts(j, TH)],
                                PSB[0 if j < 4 else 1][:], OP.mult)
                last = v.tensor_scalar(OUTT[:, ts(j, TH)], OUTT[:, ts(j, TH)],
                                       gcol[:, (j % 4):(j % 4) + 1], None, OP.mult)
            b.inc("vE", last)

        vE_rms = b.mark("vE")

        vE_sp0 = b.mark("vE")
        dAG_s0 = b.mark("dAGq")
        vE_base_for_split = b.mark("vE")

        @b.on("vector")
        def _(v, vE_rms=vE_rms, dAG_s0=dAG_s0):
            b.wait(v, "vE", vE_rms)
            for j in range(24):
                if j >= 1:
                    b.wait(v, "dAGq", dAG_s0 + 32 * j)
                srcap = OUTT[:, ts(j, TH)] if j < 8 else LNW[:, ts(j - 8, TH)]
                last = v.tensor_copy(OHL[:, ts(0, TH)], srcap)
                b.fence(v, last, "fv")
                last = v.tensor_tensor(OHL[:, ts(1, TH)], srcap, OHL[:, ts(0, TH)], OP.subtract)
                b.inc("vE", last)

        @b.on("gpsimd")
        def _(g, q=q, vE_sp0=vE_sp0):
            for j in range(24):
                b.wait(g, "vE", vE_sp0 + j + 1)
                if j < 4:
                    ih, il = j, 4 + j
                elif j < 8:
                    ih, il = 8 + (j - 4), 12 + (j - 4)
                else:
                    ih, il = 16 + (j - 8), 32 + (j - 8)
                dma = g.dma_start(out=io["ag2_in"][ih, :, ts(q, TH)], in_=OHL[:, ts(0, TH)])
                b.inc("dAGq", dma, 16)
                dma = g.dma_start(out=io["ag2_in"][il, :, ts(q, TH)], in_=OHL[:, ts(1, TH)])
                b.inc("dAGq", dma, 16)

    dAG_all = b.mark("dAGq")

    @b.on("gpsimd")
    def _(g):
        b.wait(g, "dAGq", dAG_all)
        cc = g.collective_compute("AllGather", OP.bypass,
                                  replica_groups=[list(range(NCORE))],
                                  ins=[io["ag2_in"].opt()],
                                  outs=[io["ag2_out"].opt()])
        b.inc("projAG", cc)

    b.free("pjPSC1", "pjPSC0", "pjPSB1", "pjPSB0", "pjPS2a", "pjPS1",
           "pjROWS", "pjOHL", "pjOUT", "pjWDKV", "pjWDQ", "pjSQ", "pjLNW")



def emit_attn_masks(b, k, MK, pAG):
    QRB = b.sbuf("amQRB", [128, 512], F32)
    KCB = b.sbuf("amKCB", [128, 512], F32)
    QIR = b.sbuf("amQIR", [1, 512], F32)
    QII = b.sbuf("amQII", [1, 512], I32)
    KCF = b.sbuf("amKCF", [128, 1], F32)
    PSM = b.psum("amPSM", [128, 512])

    @b.on("gpsimd")
    def _(g):
        b.wait(g, "init", 1)
        b.wait(g, "projAG", pAG)
        last = g.iota(QII[:], [[1, 512]], channel_multiplier=0)
        b.inc("gI3", last)

    gI3 = b.mark("gI3")

    @b.on("vector")
    def _(v):
        b.wait(v, "gI3", gI3)
        last = v.tensor_copy(QIR[:], QII[:])
        b.fence(v, last, "fv")
        last = v.tensor_copy(KCF[:], k["i_ii"][:])
        b.inc("vA", last)

    vA1 = b.mark("vA")

    @b.on("tensor")
    def _(t):
        b.wait(t, "vA", vA1)
        last = t.matmul(PSM[:], k["ones_row_f"][:], QIR[:], start=True, stop=True)
        b.inc("tS", last)

    tSm = b.mark("tS")

    @b.on("vector")
    def _(v):
        b.wait(v, "tS", tSm)
        last = v.tensor_copy(QRB[:], PSM[:])
        b.fence(v, last, "fv")
        last = v.tensor_scalar(KCB[:], QRB[:], 0.0, KCF[:, 0:1], OP.mult, OP.add)
        b.fence(v, last, "fv")
        for r in range(4):
            last = v.tensor_scalar(MK[:, ts(r, 512)], KCB[:], 128.0 * r, None, OP.add)
            b.fence(v, last, "fv")
            last = v.tensor_tensor(MK[:, ts(r, 512)], QRB[:], MK[:, ts(r, 512)], OP.is_ge)
            if r < 3:
                b.fence(v, last, "fv")
        b.inc("amask", last)

    b.free("amPSM", "amKCF", "amQII", "amQIR", "amKCB", "amQRB")


def emit_attn(b, k, io):
    MK = b.sbuf("atMK", [128, 4 * 512], BF16)
    pAG = b.mark("projAG")
    emit_attn_masks(b, k, MK, pAG)
    amask_n = b.mark("amask")
    PW = 3 * 64 + 128

    WHL = b.sbuf("atWHL", [128, 2 * 4 * HPC * PW], BF16)
    WKRHL = b.sbuf("atWKRHL", [128, 2 * 16 * HPC * 64], BF16)
    NEGC = b.sbuf("atNEGC", [1, 8], F32)
    WQS = b.sbuf("atWQS", [128, HPC * 4 * 64], F32)
    WRS = b.sbuf("atWRS", [128, HPC * 4 * 64], F32)
    WKS = b.sbuf("atWKS", [128, HPC * 4 * 64], F32)
    WVS = b.sbuf("atWVS", [128, HPC * 4 * 128], F32)
    WKRS = b.sbuf("atWKRS", [128, 16 * HPC * 64], F32)

    @b.on("sync")
    def _(sy):
        b.wait(sy, "projAG", pAG)
        for nm, dst in (("Wuq", WQS), ("Wqr", WRS), ("Wuk", WKS), ("Wuv", WVS)):
            dma = sy.dma_start(out=dst[:].rearrange("p (c x) -> p c x", c=4),
                               in_=io[nm].rearrange("(c p) x -> p c x", p=128))
            b.inc("dAW", dma, 16)
        dma = sy.dma_start(out=WKRS[:].rearrange("p (h x) -> p h x", h=16),
                           in_=io["Wkr_h"].rearrange("(h p) x -> p h x", p=128))
        b.inc("dAW", dma, 16)

    dAW1 = b.mark("dAW")

    @b.on("vector")
    def _(v):
        b.wait(v, "dAW", dAW1)
        last = None
        for (wsrc, n) in ((WQS, HPC * 4 * 64), (WRS, HPC * 4 * 64), (WKS, HPC * 4 * 64),
                          (WVS, HPC * 4 * 128)):
            pass
        HALF = 4 * HPC * PW
        off = 0
        for wsrc, wid in ((WQS, HPC * 4 * 64), (WRS, HPC * 4 * 64), (WKS, HPC * 4 * 64),
                          (WVS, HPC * 4 * 128)):
            last = v.tensor_copy(WHL[:, off:off + wid], wsrc[:, :wid])
            b.fence(v, last, "fv")
            last = v.tensor_tensor(WHL[:, HALF + off: HALF + off + wid], wsrc[:, :wid],
                                   WHL[:, off:off + wid], OP.subtract)
            b.fence(v, last, "fv")
            off += wid
        KHALF = 16 * HPC * 64
        last = v.tensor_copy(WKRHL[:, :KHALF], WKRS[:, :KHALF])
        b.fence(v, last, "fv")
        last = v.tensor_tensor(WKRHL[:, KHALF:2 * KHALF], WKRS[:, :KHALF],
                               WKRHL[:, :KHALF], OP.subtract)
        b.inc("vA", last)

    vA_w = b.mark("vA")
    b.free("atWKRS", "atWVS", "atWKS", "atWRS", "atWQS")

    def wsl(hi, kind, cch, hh):
        HALF = 4 * HPC * PW
        base = (0 if hi else HALF)
        woff = {"q": 0, "r": HPC * 4 * 64, "k": 2 * HPC * 4 * 64, "v": 3 * HPC * 4 * 64}[kind]
        wid = 128 if kind == "v" else 64
        o = base + woff + (cch * HPC + hh) * wid
        return WHL[:, o:o + wid]

    def wkr(hi, hch, hh):
        KHALF = 16 * HPC * 64
        o = (0 if hi else KHALF) + (hch * HPC + hh) * 64
        return WKRHL[:, o:o + 64]

    emit_attn_kprep(b, k, io, wsl, wkr, NEGC, vA_w)
    for hh in range(HPC):
        gates = [("dKSe", b.mark("dKSe")), ("dKSo", b.mark("dKSo")),
                 ("dA2A", b.mark("dA2A")), ("vE", b.mark("vE"))]
        emit_attn_head(b, k, io, hh, MK, wsl, NEGC, amask_n, gates)

    dA2A_all = b.mark("dA2A")

    @b.on("gpsimd")
    def _(g):
        b.wait(g, "dA2A", dA2A_all)
        with allow_ici_all_to_all():
            cc = g.collective_compute("AllToAll", OP.bypass,
                                      replica_groups=[list(range(NCORE))],
                                      ins=[io["a2a_in"].opt()],
                                      outs=[io["a2a_out"].opt()])
        b.inc("attA2A", cc)

    b.free("atNEGC", "atWKRHL", "atWHL", "atMK")


def emit_attn_kprep(b, k, io, wsl, wkr, NEGC, vA_w):
    ag = io["ag2_out"]
    u = "kp"
    CCB = b.sbuf(u + "CCB", [128, 2 * 8 * 512], BF16)
    LNB = b.sbuf(u + "LNB", [128, 2 * 32 * 512], BF16)
    KT = b.sbuf(u + "KT", [128, 2 * 512], F32)
    KB = b.sbuf(u + "KB", [128, 4 * 512], BF16)
    KSQ = b.sbuf(u + "KSQ", [128, 512], F32)
    CMX = b.sbuf(u + "CMX", [1, 16], F32)
    PSr = b.psum(u + "PSr", [1, 512])
    PSr2 = b.psum(u + "PSr2", [1, 512])
    PSkr = b.psum(u + "PSkr", [128, 512])
    PSkb = b.psum(u + "PSkb", [128, 512])

    NKT = S // 512
    ropeN = b.mark("rope")
    dCK0 = {0: b.mark("dCKe"), 1: b.mark("dCKo")}
    tK0 = b.mark("tM")
    vK0 = b.mark("vE")
    sK0 = b.mark("sA")
    tR0 = b.mark("tS")
    vMX0 = b.mark("vM")
    dKS0 = {0: b.mark("dKSe"), 1: b.mark("dKSo")}

    @b.on("gpsimd")
    def _(g):
        b.wait(g, "vA", vA_w)

        def kstore(kt):
            ch = "dKSe" if kt % 2 == 0 else "dKSo"
            for hh in range(HPC):
                b.wait(g, "vE", vK0 + 2 * kt + hh + 1)
                dma = g.dma_start(out=io["kscr"][0, hh, :, ts(kt, 512)], in_=KB[:, ts(2 * hh, 512)])
                b.inc(ch, dma, 16)
                dma = g.dma_start(out=io["kscr"][1, hh, :, ts(kt, 512)], in_=KB[:, ts(2 * hh + 1, 512)])
                b.inc(ch, dma, 16)

        for kt in range(NKT):
            ch = "dCKe" if kt % 2 == 0 else "dCKo"
            if kt >= 2:
                b.wait(g, "tM", tK0 + (kt - 1))
            sl, tloc = kt // 2, (kt % 2) * 512
            base = (kt % 2) * 4096
            for cch in range(4):
                dma = g.dma_start(out=CCB[:, base + cch * 512: base + cch * 512 + 512],
                                  in_=ag[sl, 8 + cch, :, tloc:tloc + 512])
                b.inc(ch, dma, 16)
                dma = g.dma_start(out=CCB[:, base + 2048 + cch * 512: base + 2048 + cch * 512 + 512],
                                  in_=ag[sl, 12 + cch, :, tloc:tloc + 512])
                b.inc(ch, dma, 16)
            lbase = (kt % 2) * 16384
            for hch in range(16):
                dma = g.dma_start(out=LNB[:, lbase + hch * 512: lbase + hch * 512 + 512],
                                  in_=ag[sl, 16 + hch, :, tloc:tloc + 512])
                b.inc(ch, dma, 16)
                dma = g.dma_start(out=LNB[:, lbase + 8192 + hch * 512: lbase + 8192 + hch * 512 + 512],
                                  in_=ag[sl, 32 + hch, :, tloc:tloc + 512])
                b.inc(ch, dma, 16)
            if kt >= 1:
                kstore(kt - 1)
        kstore(NKT - 1)

    @b.on("tensor")
    def _(t):
        b.wait(t, "vA", vA_w)

        def psr(j):
            b.wait(t, "sA", sK0 + j + 1)
            if j >= 2:
                b.wait(t, "vM", vMX0 + j - 1)
            ps = PSr if j % 2 == 0 else PSr2
            last = t.matmul(ps[:], k["ones_col_f"][:], KSQ[:], start=True, stop=True)
            b.inc("tS", last)

        for kt in range(NKT):
            ch = "dCKe" if kt % 2 == 0 else "dCKo"
            b.wait(t, ch, dCK0[kt % 2] + 16 * 40 * (kt // 2 + 1))
            if kt >= 1:
                b.wait(t, "vE", vK0 + 2 * kt)
            base = (kt % 2) * 4096
            lbase = (kt % 2) * 16384
            for hh in range(HPC):
                ps = PSkr[64 * hh:64 * hh + 64, :]
                for hch in range(16):
                    lh = LNB[:, lbase + hch * 512: lbase + hch * 512 + 512]
                    ll = LNB[:, lbase + 8192 + hch * 512: lbase + 8192 + hch * 512 + 512]
                    wh, wl = wkr(True, hch, hh), wkr(False, hch, hh)
                    t.matmul(ps, wh, lh, start=(hch == 0), stop=False)
                    t.matmul(ps, wh, ll, start=False, stop=False)
                    t.matmul(ps, wl, lh, start=False, stop=(hch == 15))
            for hh in range(HPC):
                ps = PSkb[64 * (1 - hh):64 * (1 - hh) + 64, :]
                for cch in range(4):
                    ch_ = CCB[:, base + cch * 512: base + cch * 512 + 512]
                    cl_ = CCB[:, base + 2048 + cch * 512: base + 2048 + cch * 512 + 512]
                    wh, wl = wsl(True, "k", cch, hh), wsl(False, "k", cch, hh)
                    t.matmul(ps, wh, ch_, start=(cch == 0), stop=False)
                    t.matmul(ps, wh, cl_, start=False, stop=False)
                    last = t.matmul(ps, wl, ch_, start=False, stop=(cch == 3))
            b.inc("tM", last)
            if kt >= 1:
                psr(2 * (kt - 1))
                psr(2 * (kt - 1) + 1)
        psr(2 * (NKT - 1))
        psr(2 * (NKT - 1) + 1)

    @b.on("vector")
    def _(v):
        b.wait(v, "rope", ropeN)

        def rmax(j):
            hh = j % 2
            b.wait(v, "tS", tR0 + j + 1)
            if j >= 2:
                b.wait(v, "vM", vMX0 + j - 1)
            last = v.reduce_max(CMX[0:1, 8 + hh:9 + hh], (PSr if hh == 0 else PSr2)[:],
                                axis=mybir.AxisListType.X)
            b.fence(v, last, "fv")
            if j < 2:
                last = v.tensor_copy(CMX[0:1, hh:hh + 1], CMX[0:1, 8 + hh:9 + hh])
            else:
                last = v.tensor_tensor(CMX[0:1, hh:hh + 1], CMX[0:1, hh:hh + 1],
                                       CMX[0:1, 8 + hh:9 + hh], OP.max)
            b.inc("vM", last)

        for kt in range(NKT):
            b.wait(v, "tM", tK0 + kt + 1)
            if kt >= 1:
                b.wait(v, "sA", sK0 + 2 * kt)
                chp = "dKSe" if (kt - 1) % 2 == 0 else "dKSo"
                b.wait(v, chp, dKS0[(kt - 1) % 2] + 16 * 4 * ((kt - 1) // 2 + 1))
            for hh in range(HPC):
                v.tensor_tensor(KT[64 * hh:64 * hh + 64, ts(hh, 512)],
                                PSkr[64 * hh:64 * hh + 64, :],
                                k["rs"][64 * hh:64 * hh + 64, ts(kt, 512)], OP.mult)
                r0 = 64 * (1 - hh)
                v.tensor_copy(KT[r0:r0 + 64, ts(hh, 512)], PSkb[r0:r0 + 64, :])
                v.tensor_copy(KB[:, ts(2 * hh, 512)], KT[:, ts(hh, 512)])
                last = v.tensor_tensor(KB[:, ts(2 * hh + 1, 512)], KT[:, ts(hh, 512)],
                                       KB[:, ts(2 * hh, 512)], OP.subtract)
                b.inc("vE", last)
            if kt >= 1:
                rmax(2 * (kt - 1))
                rmax(2 * (kt - 1) + 1)
        rmax(2 * (NKT - 1))
        rmax(2 * (NKT - 1) + 1)

    @b.on("scalar")
    def _(s):
        for kt in range(NKT):
            for hh in range(HPC):
                j = 2 * kt + hh
                b.wait(s, "vE", vK0 + j + 1)
                if j >= 1:
                    b.wait(s, "tS", tR0 + j)
                last = s.activation(KSQ[:], KT[:, ts(hh, 512)], AF.Square)
                b.inc("sA", last)

    vM_end = b.mark("vM")

    @b.on("vector")
    def _(v):
        b.wait(v, "vM", vM_end)
        last = v.tensor_scalar(NEGC[0:1, 0:2], CMX[0:1, 0:2], -0.5, SOFT_C_SHIFT,
                               OP.mult, OP.add)
        b.inc("vCc", last)

    b.free("kpPSkb", "kpPSkr", "kpPSr2", "kpPSr", "kpCMX", "kpKSQ", "kpKB",
           "kpKT", "kpLNB", "kpCCB")


def emit_attn_head(b, k, io, hh, MK, wsl, NEGC, amask_n, gates):
    u = f"ah{hh}"
    ag = io["ag2_out"]
    NKT = S // 512
    QH = b.sbuf(u + "QH", [128, S], BF16)
    QL = b.sbuf(u + "QL", [128, S], BF16)
    KH = b.sbuf(u + "KH", [128, S], BF16)
    KL = b.sbuf(u + "KL", [128, S], BF16)
    VH = b.sbuf(u + "VH", [128, NTI * 128], BF16)
    VL = b.sbuf(u + "VL", [128, NTI * 128], BF16)
    NEGM = b.sbuf(u + "NM", [1, S], BF16)
    CCB2 = b.sbuf(u + "CCB2", [128, 8 * 512], BF16)
    QT = b.sbuf(u + "QT", [128, 512], F32)
    QS = b.sbuf(u + "QS", [128, 512], F32)
    PF = b.sbuf(u + "PF", [128, 2 * 512], F32)
    PHB = b.sbuf(u + "PHB", [128, 2 * 512], BF16)
    PLB = b.sbuf(u + "PLB", [128, 2 * 512], BF16)
    RLC = b.sbuf(u + "RLC", [128, 4], F32)
    RLR = b.sbuf(u + "RLR", [1, 512], F32)
    RLBS = b.sbuf(u + "RLBS", [128, 512], F32)
    OEV = b.sbuf(u + "OEV", [128, 512], F32)
    PSs = [b.psum(u + "PSs0", [128, 512]), b.psum(u + "PSs1", [128, 512])]
    PSavB = [b.psum(u + "PSav", [128, 512]), b.psum(u + "PSav1", [128, 512])]
    PSlB = [b.psum(u + "PSl", [128, 4]), b.psum(u + "PSl1", [128, 4])]
    PSr = b.psum(u + "PSr", [1, 512])
    PSx = b.psum(u + "PSx", [128, 512])

    dKH0 = b.mark("dKHe")

    @b.on("sync")
    def _(sy):
        b.wait(sy, "vCc", b.mark("vCc"))
        for _c, _n in gates:
            b.wait(sy, _c, _n)
        dma = sy.dma_start(out=KH[:], in_=io["kscr"][0, hh])
        b.inc("dKHe", dma, 16)
        dma = sy.dma_start(out=KL[:], in_=io["kscr"][1, hh])
        b.inc("dKHe", dma, 16)

    dKH1 = b.mark("dKHe")

    dCV0 = {0: b.mark("dCVe"), 1: b.mark("dCVo")}
    tV0 = b.mark("tM")
    vV0 = b.mark("vE")

    @b.on("gpsimd")
    def _(g):
        for _c, _n in gates:
            b.wait(g, _c, _n)
        for kt in range(NKT):
            ch = "dCVe" if kt % 2 == 0 else "dCVo"
            if kt >= 1:
                b.wait(g, "tM", tV0 + 4 * kt)
            sl, tloc = kt // 2, (kt % 2) * 512
            base = 0
            for cch in range(4):
                dma = g.dma_start(out=CCB2[:, base + cch * 512: base + cch * 512 + 512],
                                  in_=ag[sl, 8 + cch, :, tloc:tloc + 512])
                b.inc(ch, dma, 16)
                dma = g.dma_start(out=CCB2[:, base + 2048 + cch * 512: base + 2048 + cch * 512 + 512],
                                  in_=ag[sl, 12 + cch, :, tloc:tloc + 512])
                b.inc(ch, dma, 16)

    @b.on("tensor")
    def _(t):
        b.wait(t, "dKHe", dKH1)
        for kt in range(NKT):
            ch = "dCVe" if kt % 2 == 0 else "dCVo"
            b.wait(t, ch, dCV0[kt % 2] + 16 * 8 * (kt // 2 + 1))
            base = 0
            for k4 in range(4):
                j = 4 * kt + k4
                if j >= 2:
                    b.wait(t, "vE", vV0 + j - 1)
                ps = PSs[k4 % 2]
                for cch in range(4):
                    chs = CCB2[:, base + cch * 512 + k4 * 128: base + cch * 512 + k4 * 128 + 128]
                    cls = CCB2[:, base + 2048 + cch * 512 + k4 * 128: base + 2048 + cch * 512 + k4 * 128 + 128]
                    t.matmul(ps[:, :128], chs, wsl(True, "v", cch, hh), start=(cch == 0), stop=False)
                    t.matmul(ps[:, :128], cls, wsl(True, "v", cch, hh), start=False, stop=False)
                    last = t.matmul(ps[:, :128], chs, wsl(False, "v", cch, hh),
                                    start=False, stop=(cch == 3))
                b.inc("tM", last)

    @b.on("vector")
    def _(v):
        for kt in range(NKT):
            for k4 in range(4):
                j = 4 * kt + k4
                b.wait(v, "tM", tV0 + j + 1)
                v.tensor_copy(VH[:, ts(j, 128)], PSs[k4 % 2][:, :128])
                last = v.tensor_tensor(VL[:, ts(j, 128)], PSs[k4 % 2][:, :128],
                                       VH[:, ts(j, 128)], OP.subtract)
                b.inc("vE", last)

    vV_end = b.mark("vE")

    dCQ0 = {0: b.mark("dCQe"), 1: b.mark("dCQo")}
    tQ0 = b.mark("tS")
    vQp0 = b.mark("vR")
    sQ0 = b.mark("sA")

    @b.on("gpsimd")
    def _(g):
        for qt in range(NKT):
            ch = "dCQe" if qt % 2 == 0 else "dCQo"
            if qt >= 1:
                b.wait(g, "tS", tQ0 + (1 if qt == 1 else 2 * (qt - 1)))
            sl, tloc = qt // 2, (qt % 2) * 512
            base = 0
            for cch in range(4):
                dma = g.dma_start(out=CCB2[:, base + cch * 512: base + cch * 512 + 512],
                                  in_=ag[sl, 0 + cch, :, tloc:tloc + 512])
                b.inc(ch, dma, 16)
                dma = g.dma_start(out=CCB2[:, base + 2048 + cch * 512: base + 2048 + cch * 512 + 512],
                                  in_=ag[sl, 4 + cch, :, tloc:tloc + 512])
                b.inc(ch, dma, 16)

    @b.on("tensor")
    def _(t):
        b.wait(t, "vE", vV_end)

        def psrq(j):
            b.wait(t, "sA", sQ0 + j + 1)
            if j >= 1:
                b.wait(t, "vR", vQp0 + 3 * j + 2)
            last = t.matmul(PSr[:], k["ones_col_f"][:], QS[:, :], start=True, stop=True)
            b.inc("tS", last)

        for qt in range(NKT):
            ch = "dCQe" if qt % 2 == 0 else "dCQo"
            b.wait(t, ch, dCQ0[qt % 2] + 16 * 8 * (qt // 2 + 1))
            if qt >= 1:
                b.wait(t, "vR", vQp0 + (1 if qt == 1 else 3 * (qt - 1)))
            base = 0
            rbase = 64 * hh
            bbase = 64 * (1 - hh)
            for cch in range(4):
                chs = CCB2[:, base + cch * 512: base + cch * 512 + 512]
                cls = CCB2[:, base + 2048 + cch * 512: base + 2048 + cch * 512 + 512]
                t.matmul(PSx[bbase:bbase + 64, :], wsl(True, "q", cch, hh), chs, start=(cch == 0), stop=False)
                t.matmul(PSx[bbase:bbase + 64, :], wsl(True, "q", cch, hh), cls, start=False, stop=False)
                t.matmul(PSx[bbase:bbase + 64, :], wsl(False, "q", cch, hh), chs, start=False, stop=(cch == 3))
            for cch in range(4):
                chs = CCB2[:, base + cch * 512: base + cch * 512 + 512]
                cls = CCB2[:, base + 2048 + cch * 512: base + 2048 + cch * 512 + 512]
                t.matmul(PSx[rbase:rbase + 64, :], wsl(True, "r", cch, hh), chs, start=(cch == 0), stop=False)
                t.matmul(PSx[rbase:rbase + 64, :], wsl(True, "r", cch, hh), cls, start=False, stop=False)
                last = t.matmul(PSx[rbase:rbase + 64, :], wsl(False, "r", cch, hh), chs,
                                start=False, stop=(cch == 3))
            b.inc("tS", last)
            if qt >= 1:
                psrq(qt - 1)
        psrq(NKT - 1)

    @b.on("vector")
    def _(v):
        b.wait(v, "vCc", b.mark("vCc"))

        def negm_(j):
            b.wait(v, "tS", tQ0 + (2 * j + 3 if j < NKT - 1 else 2 * NKT))
            last = v.tensor_scalar(NEGM[0:1, ts(j, 512)], PSr[:], -0.5,
                                   NEGC[0:1, hh:hh + 1], OP.mult, OP.add)
            b.inc("vR", last)

        for qt in range(NKT):
            b.wait(v, "tS", tQ0 + (1 if qt == 0 else 2 * qt))
            if qt >= 1:
                b.wait(v, "sA", sQ0 + qt)
            rbase = 64 * hh
            bbase = 64 * (1 - hh)
            v.tensor_tensor(QT[rbase:rbase + 64, :], PSx[rbase:rbase + 64, :],
                            k["rs"][rbase:rbase + 64, ts(qt, 512)], OP.mult)
            last = v.tensor_copy(QT[bbase:bbase + 64, :], PSx[bbase:bbase + 64, :])
            b.inc("vR", last)
            last = v.tensor_scalar(QT[:, :], QT[:, :], RSQ, None, OP.mult)
            b.fence(v, last, "fv")
            v.tensor_copy(QH[:, ts(qt, 512)], QT[:, :])
            last = v.tensor_tensor(QL[:, ts(qt, 512)], QT[:, :], QH[:, ts(qt, 512)], OP.subtract)
            b.inc("vR", last)
            if qt >= 1:
                negm_(qt - 1)
        negm_(NKT - 1)

    @b.on("scalar")
    def _(s):
        for qt in range(NKT):
            b.wait(s, "vR", vQp0 + (2 if qt == 0 else 3 * qt + 1))
            if qt >= 1:
                b.wait(s, "tS", tQ0 + 2 * qt + 1)
            last = s.activation(QS[:, :], QT[:, :], AF.Square)
            b.inc("sA", last)

    vQp_end = b.mark("vR")

    tT0 = b.mark("tM")
    tA0 = b.mark("tA")
    sT0 = b.mark("sE")
    vT0 = b.mark("vH")
    tEp0 = b.mark("tS")
    vEp0 = b.mark("vE")
    dA0 = b.mark("dA2A")
    nj_cum = [0]
    for qt in range(NKT):
        nj_cum.append(nj_cum[-1] + 4 * (qt + 1))
    vQp_end_ = vQp_end
    ep_gate = []

    for qt in range(NKT):
        ep_gate.append(vEp0 + 4 * qt)

    @b.on("tensor")
    def _(t):
        b.wait(t, "vR", vQp_end_)
        b.wait(t, "amask", amask_n)

        def av(jp, qtp, fin):
            pslot = jp % 2
            kk = jp - nj_cum[qtp]
            PSav, PSl = PSavB[qtp % 2], PSlB[qtp % 2]
            b.wait(t, "vH", vT0 + jp + 1)
            t.matmul(PSav[:], VH[:, ts(kk, 128)], PHB[:, ts(pslot, 512)],
                     start=(jp == nj_cum[qtp]), stop=False)
            t.matmul(PSav[:], VH[:, ts(kk, 128)], PLB[:, ts(pslot, 512)], start=False, stop=False)
            t.matmul(PSav[:], VL[:, ts(kk, 128)], PHB[:, ts(pslot, 512)], start=False, stop=fin)
            for qc in range(4):
                t.matmul(PSl[:, qc:qc + 1], PHB[:, pslot * 512 + qc * 128: pslot * 512 + qc * 128 + 128],
                         k["ones_col_b"][:], start=(jp == nj_cum[qtp] and qc == 0), stop=False)
                last = t.matmul(PSl[:, qc:qc + 1], PLB[:, pslot * 512 + qc * 128: pslot * 512 + qc * 128 + 128],
                                k["ones_col_b"][:], start=False, stop=(fin and qc == 3))
            b.inc("tA", last)

        def epi_mm(qtp):
            b.wait(t, "vE", vEp0 + 4 * qtp + 1)
            for qc in range(4):
                last = t.matmul(PSx[0:1, ts(qc, 128)], RLC[:, qc:qc + 1], k["id128f"][:],
                                start=True, stop=True)
            b.inc("tS", last)
            b.wait(t, "vE", vEp0 + 4 * qtp + 2)
            last = t.matmul(PSx[:, :], k["ones_row_f"][:], RLR[0:1, :], start=True, stop=True)
            b.inc("tS", last)

        for qt in range(NKT):
            NJ = 4 * (qt + 1)
            if qt >= 2:
                b.wait(t, "vE", vEp0 + 4 * (qt - 1))
            for j in range(NJ):
                jg = nj_cum[qt] + j
                if jg >= 2:
                    b.wait(t, "sE", sT0 + jg - 1)
                ps = PSs[jg % 2]
                t.matmul(ps[:], KH[:, ts(j, 128)], QH[:, ts(qt, 512)], start=True, stop=False)
                t.matmul(ps[:], KH[:, ts(j, 128)], QL[:, ts(qt, 512)], start=False, stop=False)
                t.matmul(ps[:], KL[:, ts(j, 128)], QH[:, ts(qt, 512)], start=False, stop=False)
                last = t.matmul(ps[:], k["ones_row_b"][:], NEGM[0:1, ts(qt, 512)],
                                start=False, stop=True)
                b.inc("tM", last)
                if qt >= 1 and j == 1:
                    epi_mm(qt - 1)
                if j >= 1:
                    av(nj_cum[qt] + j - 1, qt, False)
            av(nj_cum[qt] + NJ - 1, qt, True)
        epi_mm(NKT - 1)

    @b.on("scalar")
    def _(s):
        for qt in range(NKT):
            for j in range(4 * (qt + 1)):
                jg = nj_cum[qt] + j
                b.wait(s, "tM", tT0 + jg + 1)
                if jg >= 2:
                    b.wait(s, "vH", vT0 + jg - 1)
                last = s.activation(PF[:, ts(jg % 2, 512)], PSs[jg % 2][:], AF.Exp)
                b.inc("sE", last)

    @b.on("vector")
    def _(v):
        for qt in range(NKT):
            for j in range(4 * (qt + 1)):
                jg = nj_cum[qt] + j
                b.wait(v, "sE", sT0 + jg + 1)
                if jg >= 2:
                    b.wait(v, "tA", tA0 + jg - 1)
                slot = jg % 2
                if j >= 4 * qt:
                    r = j - 4 * qt
                    v.tensor_tensor(PF[:, ts(slot, 512)], PF[:, ts(slot, 512)],
                                    MK[:, ts(r, 512)], OP.mult)
                v.tensor_copy(PHB[:, ts(slot, 512)], PF[:, ts(slot, 512)])
                last = v.tensor_tensor(PLB[:, ts(slot, 512)], PF[:, ts(slot, 512)],
                                       PHB[:, ts(slot, 512)], OP.subtract)
                b.inc("vH", last)
            b.wait(v, "tA", tA0 + nj_cum[qt + 1])
            last = v.reciprocal(RLC[:, 0:4], PSlB[qt % 2][:, 0:4])
            b.inc("vE", last)
            b.wait(v, "tS", tEp0 + 2 * qt + 1)
            last = v.tensor_copy(RLR[0:1, :], PSx[0:1, 0:512])
            b.inc("vE", last)
            b.wait(v, "tS", tEp0 + 2 * qt + 2)
            last = v.tensor_copy(RLBS[:, :], PSx[:, :512])
            b.inc("vE", last)
            if qt >= 1:
                b.wait(v, "dA2A", dA0 + 16 * qt)
            last = v.tensor_tensor(OEV[:, :], PSavB[qt % 2][:], RLBS[:, :], OP.mult)
            b.inc("vE", last)

    @b.on("gpsimd")
    def _(g):
        for qt in range(NKT):
            b.wait(g, "vE", vEp0 + 4 * qt + 4)
            dest, tloc = qt // 2, (qt % 2) * 512
            dma = g.dma_start(out=io["a2a_in"][dest, hh, :, tloc:tloc + 512], in_=OEV[:, :])
            b.inc("dA2A", dma, 16)

    b.free(u + "PSx", u + "PSr", u + "PSl1", u + "PSl", u + "PSav1", u + "PSav",
           u + "PSs1", u + "PSs0",
           u + "OEV", u + "RLBS", u + "RLR", u + "RLC", u + "PLB", u + "PHB",
           u + "PF", u + "QS", u + "QT", u + "CCB2", u + "NM", u + "VL", u + "VH",
           u + "KL", u + "KH", u + "QL", u + "QH")



def emit_wout(b, k, io, gates):
    NQ = TPC // TH
    AOT = b.sbuf("woAOT", [128, 16 * TH], F32)
    WOB = b.sbuf("woWOB", [128, 2 * 16 * 128], F32)
    XOB = b.sbuf("woXOB", [128, 2 * TH], F32)
    OO = b.sbuf("woOO", [128, 2 * TH], F32)
    PSW = [b.psum("woPSW0", [128, TH]), b.psum("woPSW1", [128, TH])]

    for q in range(NQ):
        dAO0 = b.mark("dAO")
        dWOe0 = b.mark("dWOe")
        dWOo0 = b.mark("dWOo")
        dX10 = {0: b.mark("dX1e"), 1: b.mark("dX1o")}
        tW0 = b.mark("tM")
        vW0 = b.mark("vE")
        dX2o0 = {0: b.mark("dX2e"), 1: b.mark("dX2o")}

        @b.on("gpsimd")
        def _(g, q=q, vW0=vW0):
            for _c, _n in gates:
                b.wait(g, _c, _n)
            if q > 0:
                b.wait(g, "vE", vW0)
            for i in range(16):
                dma = g.dma_start(out=AOT[:, ts(i, TH)],
                                  in_=io["a2a_out"][i // 2, i % 2, :, ts(q, TH)])
                b.inc("dAO", dma, 16)

        @b.on("sync")
        def _(sy, q=q, tW0=tW0):
            for _c, _n in gates:
                b.wait(sy, _c, _n)
            for oc in range(16):
                ch = "dWOe" if oc % 2 == 0 else "dWOo"
                if oc >= 2 or q > 0:
                    b.wait(sy, "tM", tW0 + oc - 1)
                dma = sy.dma_start(
                    out=WOB[:, ts(oc % 2, 2048)].rearrange("p (h o) -> p h o", h=16),
                    in_=io["Wout"].rearrange("(h p) o -> p h o", p=128)[:, :, ts(oc, 128)])
                b.inc(ch, dma, 16)

        @b.on("gpsimd")
        def _(g, q=q, vW0=vW0, dX2o0=dX2o0):
            for dt in range(16):
                ch = "dX1e" if dt % 2 == 0 else "dX1o"
                och = "dX2e" if dt % 2 == 0 else "dX2o"
                if dt >= 2:
                    b.wait(g, och, dX2o0[dt % 2] + 16 * (dt // 2))
                dma = g.dma_start(out=XOB[:, ts(dt % 2, TH)],
                                  in_=io["x1"][ts(dt, 128), ts(q, TH)])
                b.inc(ch, dma, 16)
                b.wait(g, "vE", vW0 + dt + 1)
                dma = g.dma_start(out=io["x2"][ts(dt, 128), ts(q, TH)],
                                  in_=OO[:, ts(dt % 2, TH)])
                b.inc(och, dma, 16)

        @b.on("tensor")
        def _(t, dAO0=dAO0, dWOe0=dWOe0, dWOo0=dWOo0, vW0=vW0):
            b.wait(t, "dAO", dAO0 + 16 * 16)
            for oc in range(16):
                ch = "dWOe" if oc % 2 == 0 else "dWOo"
                b.wait(t, ch, (dWOe0 if oc % 2 == 0 else dWOo0) + 16 * (oc // 2 + 1))
                if oc >= 2:
                    b.wait(t, "vE", vW0 + oc - 1)
                ps = PSW[oc % 2]
                for dch in range(16):
                    last = t.matmul(ps[:], WOB[:, (oc % 2) * 2048 + dch * 128:(oc % 2) * 2048 + dch * 128 + 128],
                                    AOT[:, ts(dch, TH)], start=(dch == 0), stop=(dch == 15))
                b.inc("tM", last)

        @b.on("vector")
        def _(v, tW0=tW0, dX10=dX10):
            for oc in range(16):
                b.wait(v, "tM", tW0 + oc + 1)
                ch = "dX1e" if oc % 2 == 0 else "dX1o"
                b.wait(v, ch, dX10[oc % 2] + 16 * (oc // 2 + 1))
                v.tensor_scalar(OO[:, ts(oc % 2, TH)], PSW[oc % 2][:],
                                k["bout_col"][:, oc:oc + 1], None, OP.add)
                last = v.tensor_tensor(OO[:, ts(oc % 2, TH)], OO[:, ts(oc % 2, TH)],
                                       XOB[:, ts(oc % 2, TH)], OP.add)
                b.inc("vE", last)

    dx2e, dx2o = b.mark("dX2e"), b.mark("dX2o")
    b.free("woPSW1", "woPSW0", "woOO", "woXOB", "woWOB", "woAOT")
    return dx2e, dx2o



def build_full():
    nc = bass.Bass("TRN2", target_bir_lowering=False, debug=False,
                   detect_race_conditions=False)
    ein = lambda n, s, d=F32: nc.dram_tensor(n, s, d, kind="ExternalInput").ap()
    x_own = ein("x_own", [D, TPC])
    wg_rows = ein("wg_rows", [IFF // 8, D])
    wu_rows = ein("wu_rows", [IFF // 8, D])
    wd_rows = ein("wd_rows", [D // 8, IFF])
    Wdq = ein("Wdq", [D, DKV])
    Wdkv = ein("Wdkv", [D, DKV])
    Wuq = ein("Wuq", [DKV, HPC * 64])
    Wqr = ein("Wqr", [DKV, HPC * 64])
    Wuk = ein("Wuk", [DKV, HPC * 64])
    Wuv = ein("Wuv", [DKV, HPC * 128])
    Wkr_h = ein("Wkr_h", [D, HPC * 64])
    Wout = ein("Wout", [D, D])
    bout = ein("bout", [128, 16])
    gq = ein("gq", [128, 4])
    gkv = ein("gkv", [128, 4])
    posf = ein("posf", [1, S])
    y_own = nc.dram_tensor("y_own", [D, TPC], F32, kind="ExternalOutput").ap()

    dram = (lambda n, s, d=F32: nc.dram_tensor(n, s, d, kind="ExternalOutput").ap()) \
        if DBG_TAPS else (lambda n, s, d=F32: nc.dram_tensor(n, s, d).ap())
    io = {
        "wg_rows": wg_rows, "wu_rows": wu_rows, "wd_rows": wd_rows,
        "ar_in": dram("ar_in", [1, 8]), "ar_out": dram("ar_out", [1, 8]),
        "ag_g_in": dram("ag_g_in", [8, 128, NTD, 128], BF16),
        "ag_u_in": dram("ag_u_in", [8, 128, NTD, 128], BF16),
        "ag_d_in": dram("ag_d_in", [2, 128, NTI, 128], BF16),
        "ag_g": dram("ag_g", [NCORE, 8, 128, NTD, 128], BF16),
        "ag_u": dram("ag_u", [NCORE, 8, 128, NTD, 128], BF16),
        "ag_d": dram("ag_d", [NCORE, 2, 128, NTI, 128], BF16),
        "x1": dram("x1s", [D, TPC]), "x2": dram("x2s", [D, TPC]),
        "Wdq": Wdq, "Wdkv": Wdkv, "Wkr_h": Wkr_h,
        "ag2_in": dram("ag2_in", [48, 128, TPC], BF16),
        "ag2_out": dram("ag2_out", [NCORE, 48, 128, TPC], BF16),
        "Wuq": Wuq, "Wqr": Wqr, "Wuk": Wuk, "Wuv": Wuv,
        "kscr": dram("kscr", [2, HPC, 128, S], BF16),
        "a2a_in": dram("a2a_in", [NCORE, HPC, 128, TPC]),
        "a2a_out": dram("a2a_out", [NCORE, HPC, 128, TPC]),
        "Wout": Wout,
    }

    b = Build(nc)
    k = alloc_persist(b)
    emit_const_setup(b, k)

    @b.on("gpsimd")
    def _(g):
        b.wait(g, "init", 1)
        dma = g.dma_start(out=k["gq_col"][:], in_=gq[:, :])
        b.inc("dMisc", dma, 16)
        dma = g.dma_start(out=k["gkv_col"][:], in_=gkv[:, :])
        b.inc("dMisc", dma, 16)
        dma = g.dma_start(out=k["bout_col"][:], in_=bout[:, :])
        b.inc("dMisc", dma, 16)

    def fini(chs):
        @b.on("gpsimd")
        def _(g):
            for c, n in chs:
                b.wait(g, c, n)
        b.realize()
        return nc

    if PHASE_LIMIT == -1:
        return fini([("init", 1), ("dMisc", 48)])
    emit_rope(b, k, posf, "rs", S, "rf")
    if PHASE_LIMIT == 0:
        return fini([("rope", 1), ("dMisc", 48)])
    _wp = emit_weight_prep(b, k, io)
    if _wp is not None:
        return fini(_wp)
    terng = [("wsccol", b.mark("wsccol")), ("ternAG", b.mark("ternAG")),
             ("ccAG", b.mark("ccAG"))]
    if PHASE_LIMIT == 1:
        return fini(terng + [("rope", 1)])
    emit_ffn_call(b, k, x_own, io["x1"], io["ag_g"], io["ag_u"], io["ag_d"], 0, terng)
    ffn1_done = [("dOe", b.mark("dOe")), ("dOo", b.mark("dOo"))]
    if PHASE_LIMIT == 2:
        return fini(ffn1_done + [("rope", 1)])
    emit_proj(b, k, io, ffn1_done)
    if PHASE_LIMIT == 3:
        return fini([("projAG", b.mark("projAG")), ("rope", 1)])
    emit_attn(b, k, io)
    att_done = [("attA2A", b.mark("attA2A")), ("dMisc", 48)]
    if PHASE_LIMIT == 5:
        return fini(att_done)
    dx2e, dx2o = emit_wout(b, k, io, att_done)
    x2_done = terng + [("dX2e", dx2e), ("dX2o", dx2o)]
    if PHASE_LIMIT == 6:
        return fini(x2_done)
    emit_ffn_call(b, k, io["x2"], y_own, io["ag_g"], io["ag_u"], io["ag_d"], 1, x2_done)

    return fini([("dOe", b.mark("dOe")), ("dOo", b.mark("dOo"))])


_NC_CACHE = [None]


def kernel(**inputs):
    xin = np.ascontiguousarray(inputs["L0_attn_x"][0])
    wg = np.ascontiguousarray(inputs["L0_ffn_w_gate"])
    wu = np.ascontiguousarray(inputs["L0_ffn_w_up"])
    wd = np.ascontiguousarray(inputs["L0_ffn_w_down"])
    Wdq = np.ascontiguousarray(inputs["L0_attn_W_dq"])
    Wuq = np.ascontiguousarray(inputs["L0_attn_W_uq"])
    Wdkv = np.ascontiguousarray(inputs["L0_attn_W_dkv"])
    Wuk = np.ascontiguousarray(inputs["L0_attn_W_uk"])
    Wuv = np.ascontiguousarray(inputs["L0_attn_W_uv"])
    Wqr = np.ascontiguousarray(inputs["L0_attn_W_qr"])
    Wkr = np.ascontiguousarray(inputs["L0_attn_W_kr"])
    gq = np.ascontiguousarray(inputs["L0_attn_g_q"])
    gkv = np.ascontiguousarray(inputs["L0_attn_g_kv"])
    Wout = np.ascontiguousarray(inputs["L0_attn_W_out"])
    bout = np.ascontiguousarray(inputs["L0_attn_b_out"])

    posf = np.arange(S, dtype=np.float32).reshape(1, S)
    in_maps = []
    for c in range(NCORE):
        h0 = HPC * c
        hsl = slice(64 * h0, 64 * (h0 + HPC))
        dsl = slice(128 * h0, 128 * (h0 + HPC))
        in_maps.append({
            "x_own": np.ascontiguousarray(xin[c * TPC:(c + 1) * TPC].T),
            "wg_rows": np.ascontiguousarray(wg[c * (IFF // 8):(c + 1) * (IFF // 8)]),
            "wu_rows": np.ascontiguousarray(wu[c * (IFF // 8):(c + 1) * (IFF // 8)]),
            "wd_rows": np.ascontiguousarray(wd[c * (D // 8):(c + 1) * (D // 8)]),
            "Wdq": Wdq, "Wdkv": Wdkv,
            "Wuq": np.ascontiguousarray(Wuq[:, hsl]),
            "Wqr": np.ascontiguousarray(Wqr[:, hsl]),
            "Wuk": np.ascontiguousarray(Wuk[:, hsl]),
            "Wuv": np.ascontiguousarray(Wuv[:, dsl]),
            "Wkr_h": np.ascontiguousarray(Wkr[:, hsl]),
            "Wout": Wout,
            "bout": np.ascontiguousarray(bout.reshape(16, 128).T),
            "gq": np.ascontiguousarray(gq.reshape(4, 128).T),
            "gkv": np.ascontiguousarray(gkv.reshape(4, 128).T),
            "posf": posf,
        })
    if _NC_CACHE[0] is None:
        _NC_CACHE[0] = build_full()
    res = run_bass_kernel_spmd(_NC_CACHE[0], in_maps, core_ids=list(range(NCORE)))
    out = np.empty((1, S, D), dtype=np.float32)
    for c in range(NCORE):
        out[0, c * TPC:(c + 1) * TPC] = res.results[c]["y_own"].T
    return out


# version 07
# speedup vs baseline: 1.0040x; 1.0040x over previous
import math
import numpy as np

import concourse.bass as bass
import concourse.mybir as mybir
from concourse.bass_utils import run_bass_kernel_spmd
from concourse.aot_env import allow_ici_all_to_all

F32 = mybir.dt.float32
BF16 = mybir.dt.bfloat16
I32 = mybir.dt.int32
AF = mybir.ActivationFunctionType
OP = mybir.AluOpType

D = 2048
S = 8192
NCORE = 8
TPC = S // NCORE
TH = 256
NTD = D // 128
NTI = 8192 // 128
IFF = 8192
NH = 16
DH = 128
DKV = 512
DR = 64
EPS = 1e-5
HPC = NH // NCORE

RSQ = 1.0 / math.sqrt(float(DH))
FW2048 = 1.0 / math.sqrt(2048.0)
FW8192 = 1.0 / math.sqrt(8192.0)
SOFT_C_SHIFT = 65.0
DBG_TAPS = False
PHASE_LIMIT = 7

ENGS = ("sync", "tensor", "vector", "scalar", "gpsimd")

class _DryInstr:
    def then_inc(self, *a, **kw):
        return self

    def __getattr__(self, name):
        return lambda *a, **kw: _DryInstr()


class _Dry:
    def __getattr__(self, name):
        return lambda *a, **kw: _DryInstr()


_DRY = _Dry()



def ts(i, w):
    return slice(i * w, (i + 1) * w)


class Build:
    def __init__(self, nc):
        self.nc = nc
        self.ops = {e: [] for e in ENGS}
        self.ch = {}
        self.live = {}
        self.fq = {}
        self.dry = False

    def on(self, eng):
        def deco(fn):
            self.dry = True
            fn(_DRY)
            self.dry = False
            self.ops[eng].append(fn)
            return None
        return deco

    def realize(self):
        nc = self.nc
        with nc.Block() as block:
            @block.sync
            def _(e):
                for fn in self.ops["sync"]:
                    fn(e)

            @block.tensor
            def _(e):
                for fn in self.ops["tensor"]:
                    fn(e)

            @block.vector
            def _(e):
                for fn in self.ops["vector"]:
                    fn(e)

            @block.scalar
            def _(e):
                for fn in self.ops["scalar"]:
                    fn(e)

            @block.gpsimd
            def _(e):
                for fn in self.ops["gpsimd"]:
                    fn(e)

    def chan(self, name):
        if name not in self.ch:
            self.ch[name] = [self.nc.alloc_semaphore(name="ch_" + name), 0]
        return self.ch[name]

    def inc(self, name, instr, k=1):
        c = self.chan(name)
        if self.dry:
            c[1] += k
            return c[1]
        instr.then_inc(c[0], k)
        return c[1]

    def mark(self, name):
        return self.chan(name)[1]

    def wait(self, eng, name, n):
        if self.dry:
            return
        eng.wait_ge(self.chan(name)[0], n)

    def fence(self, eng, last, tag):
        c = self.chan(tag)
        q = self.fq.setdefault(tag, [])
        if self.dry:
            c[1] += 1
            q.append(c[1])
            return
        n = q.pop(0)
        last.then_inc(c[0], 1)
        eng.wait_ge(c[0], n)

    def sbuf(self, name, shape, dtype):
        cm = self.nc.sbuf_tensor(name, shape, dtype)
        h = cm.__enter__()
        self.live[name] = cm
        return h

    def psum(self, name, shape, dtype=F32):
        cm = self.nc.psum_tensor(name, shape, dtype)
        h = cm.__enter__()
        self.live[name] = cm
        return h

    def free(self, *names):
        for n in names:
            self.live.pop(n).__exit__(None, None, None)



def alloc_persist(b):
    k = {}
    k["ones_col_f"] = b.sbuf("ones_col_f", [128, 1], F32)
    k["ones_row_f"] = b.sbuf("ones_row_f", [1, 128], F32)
    k["ones_col_b"] = b.sbuf("ones_col_b", [128, 1], BF16)
    k["ones_row_b"] = b.sbuf("ones_row_b", [1, 128], BF16)
    k["h128f"] = b.sbuf("h128f", [128, 128], F32)
    k["h128b"] = b.sbuf("h128b", [128, 128], BF16)
    k["id128b"] = b.sbuf("id128b", [128, 128], BF16)
    k["id128f"] = b.sbuf("id128f", [128, 128], F32)
    k["wscol"] = b.sbuf("wscol", [128, 12], F32)
    k["ones128"] = b.sbuf("ones128", [128, 128], F32)
    k["i_ii"] = b.sbuf("i_ii", [128, 1], I32)
    k["i_jj"] = b.sbuf("i_jj", [128, 128], I32)
    k["i_hx"] = b.sbuf("i_hx", [128, 128], I32)
    k["i_hy"] = b.sbuf("i_hy", [128, 128], I32)
    k["wsc"] = b.sbuf("wsc", [1, 16], F32)
    k["rs"] = b.sbuf("rs", [128, S], F32)
    k["gq_col"] = b.sbuf("gq_col", [128, 4], F32)
    k["gkv_col"] = b.sbuf("gkv_col", [128, 4], F32)
    k["bout_col"] = b.sbuf("bout_col", [128, 16], F32)
    return k


def emit_const_setup(b, k):
    @b.on("vector")
    def _(v):
        v.memset(k["ones_col_f"][:], 1.0)
        v.memset(k["ones_row_f"][:], 1.0)
        v.memset(k["ones_col_b"][:], 1.0)
        v.memset(k["ones128"][:], 1.0)
        last = v.memset(k["ones_row_b"][:], 1.0)
        b.inc("vI", last)

    @b.on("gpsimd")
    def _(g):
        b.inc("gI", g.iota(k["i_jj"][:], [[1, 128]], channel_multiplier=0))
        b.inc("gI", g.iota(k["i_ii"][:], [[0, 1]], channel_multiplier=1))

    gI = b.mark("gI")

    @b.on("vector")
    def _(v):
        b.wait(v, "gI", gI)
        v.tensor_scalar(k["i_hx"][:], k["i_jj"][:], k["i_ii"][:, 0:1], None, OP.bitwise_and)
        for sh in (4, 2, 1):
            v.tensor_scalar(k["i_hy"][:], k["i_hx"][:], sh, None, OP.logical_shift_right)
            v.tensor_tensor(k["i_hx"][:], k["i_hx"][:], k["i_hy"][:], OP.bitwise_xor)
        v.tensor_scalar(k["i_hx"][:], k["i_hx"][:], 1, None, OP.bitwise_and)
        v.tensor_scalar(k["i_hy"][:], k["i_jj"][:], k["i_ii"][:, 0:1], None, OP.bitwise_xor)
        v.tensor_copy(k["id128f"][:], k["i_hy"][:])
        last = v.tensor_scalar(k["id128f"][:], k["id128f"][:], 0.0, None, OP.is_equal)
        b.inc("vI", last)

    vI = b.mark("vI")

    @b.on("scalar")
    def _(s):
        b.wait(s, "vI", vI)
        s.activation(k["id128b"][:], k["id128f"][:], AF.Copy, bias=0.0, scale=1.0)
        s.activation(k["h128b"][:], k["i_hx"][:], AF.Copy, bias=1.0, scale=-2.0)
        last = s.activation(k["h128f"][:], k["i_hx"][:], AF.Copy, bias=1.0, scale=-2.0)
        b.inc("init", last)



def round_ops(v, src, dst, rt, n):
    v.tensor_copy(rt["i32"][:, :n], src)
    v.tensor_copy(rt["kf"][:, :n], rt["i32"][:, :n])
    v.tensor_tensor(rt["d"][:, :n], src, rt["kf"][:, :n], OP.subtract)
    v.tensor_scalar(rt["a"][:, :n], rt["d"][:, :n], 0.5, None, OP.is_gt)
    v.tensor_scalar(rt["bb"][:, :n], rt["d"][:, :n], -0.5, None, OP.is_lt)
    v.tensor_tensor(rt["a"][:, :n], rt["a"][:, :n], rt["bb"][:, :n], OP.subtract)
    v.tensor_tensor(rt["d"][:, :n], rt["kf"][:, :n], rt["a"][:, :n], OP.add)
    return v.tensor_copy(dst, rt["d"][:, :n])


def radix4_rounds(v, tile_ap, ntiles, t1, t2):
    last = None
    stride = 1
    while stride * 4 <= ntiles:
        for g in range(ntiles // (4 * stride)):
            for off in range(stride):
                base = g * 4 * stride + off
                A = tile_ap(base)
                B = tile_ap(base + stride)
                C = tile_ap(base + 2 * stride)
                Dd = tile_ap(base + 3 * stride)
                v.tensor_tensor(t1, A, B, OP.add)
                v.tensor_tensor(t2, C, Dd, OP.add)
                v.tensor_tensor(A, A, B, OP.subtract)
                v.tensor_tensor(C, C, Dd, OP.subtract)
                v.tensor_tensor(B, A, C, OP.add)
                v.tensor_tensor(Dd, A, C, OP.subtract)
                v.tensor_tensor(A, t1, t2, OP.add)
                last = v.tensor_tensor(C, t1, t2, OP.subtract)
        stride *= 4
    assert stride == ntiles, f"ntiles={ntiles} not a power of 4"
    return last




def emit_wsc_cols(b, k, pstmp):
    wn = b.mark("wsc")

    @b.on("tensor")
    def _(t):
        b.wait(t, "init", 1)
        b.wait(t, "wsc", wn)
        last = t.matmul(pstmp[:, 0:3], k["ones_row_f"][:], k["wsc"][0:1, 0:3],
                        start=True, stop=True)
        b.inc("tS", last)

    tn = b.mark("tS")

    @b.on("vector")
    def _(v):
        b.wait(v, "tS", tn)
        last = v.tensor_copy(k["wscol"][:, 0:3], pstmp[:, 0:3])
        b.inc("wsccol", last)


def emit_tok_scales(b, k, MXA, PS2, TT, fwc, targets):
    NCK = TH // 128
    vMp = b.mark("vM")
    tX0 = b.mark("tX")
    dbg = getattr(b, "_taps", None)
    if dbg is not None and "t_mxa" in dbg:
        @b.on("gpsimd")
        def _(g):
            b.wait(g, "vM", vMp)
            dma = g.dma_start(out=dbg["t_mxa"], in_=MXA[:, :TH])
            b.inc("dTap", dma, 16)
        _tn = b.mark("dTap")
        @b.on("vector")
        def _(v):
            b.wait(v, "dTap", _tn)

    @b.on("tensor")
    def _(t):
        b.wait(t, "vM", vMp)
        for c in range(NCK):
            last = t.matmul(PS2[c % 2][:, :128], MXA[:, ts(c, 128)], k["id128f"][:],
                            start=True, stop=True)
            b.inc("tX", last)

    @b.on("vector")
    def _(v):
        for c in range(NCK):
            b.wait(v, "tX", tX0 + c + 1)
            last = v.reduce_max(TT[:, c:c + 1], PS2[c % 2][:, :128], axis=mybir.AxisListType.X)
        b.fence(v, last, "fv")
        last = v.tensor_scalar(TT[:, 4:4 + NCK], TT[:, 0:NCK], fwc, EPS, OP.mult, OP.max)
        b.fence(v, last, "fv")
        last = v.reciprocal(TT[:, 8:8 + NCK], TT[:, 4:4 + NCK])
        b.fence(v, last, "fv")
        for j, (dst, kind, widx, fmul) in enumerate(targets):
            cbase = 12 + j * NCK
            if kind == "crow":
                last = v.tensor_scalar(TT[:, cbase:cbase + NCK], TT[:, 8:8 + NCK],
                                       127.0 * fwc, None, OP.mult)
            else:
                last = v.tensor_scalar(TT[:, cbase:cbase + NCK], TT[:, 4:4 + NCK],
                                       k["wscol"][:, widx:widx + 1], fmul / 127.0,
                                       OP.mult, OP.mult)
            if j + 1 < len(targets):
                b.fence(v, last, "fv")
        b.inc("vR", last)

    if dbg is not None and "t_tt" in dbg:
        _vr = b.mark("vR")
        @b.on("gpsimd")
        def _(g):
            b.wait(g, "vR", _vr)
            dma = g.dma_start(out=dbg["t_tt"], in_=TT[:, :TH])
            b.inc("dTap", dma, 16)
        _tn2 = b.mark("dTap")
        @b.on("vector")
        def _(v):
            b.wait(v, "dTap", _tn2)

    for j, (dst, kind, widx, fmul) in enumerate(targets):
        for c in range(NCK):
            vRc = b.mark("vR")
            tXc = b.mark("tX")

            @b.on("vector")
            def _(v, j=j, c=c, vRc=vRc):
                b.wait(v, "vR", vRc)
                last = v.tensor_scalar(TT[:, 32:160], k["ones128"][:],
                                       TT[:, 12 + j * NCK + c:12 + j * NCK + c + 1],
                                       None, OP.mult)
                b.inc("vR", last)

            vRb = b.mark("vR")

            @b.on("tensor")
            def _(t, c=c, vRb=vRb):
                b.wait(t, "vR", vRb)
                last = t.matmul(PS2[c % 2][:, :128], TT[:, 32:160], k["id128f"][:],
                                start=True, stop=True)
                b.inc("tX", last)

            tXb = b.mark("tX")

            @b.on("vector")
            def _(v, dst=dst, c=c, tXb=tXb):
                b.wait(v, "tX", tXb)
                last = v.tensor_copy(dst[:, ts(c, 128)], PS2[c % 2][:, :128])
                b.inc("vR", last)



def emit_ffn_call(b, k, x_in, x_out, tern_g, tern_u, tern_d, call, rdy, taps=None):
    for half in range(TPC // TH):
        emit_ffn_half(b, k, x_in, x_out, tern_g, tern_u, tern_d,
                      f"f{call}h{half}", half, rdy, taps if half == 0 else None)


def emit_ffn_half(b, k, x_in, x_out, tern_g, tern_u, tern_d, u, half, rdy, taps=None):
    dOe_prev = b.mark("dOe")
    dOo_prev = b.mark("dOo")
    RT = {"i32": b.sbuf(u + "RTi", [128, TH], I32),
          "kf": b.sbuf(u + "RTk", [128, TH], F32),
          "d": b.sbuf(u + "RTd", [128, TH], F32),
          "a": b.sbuf(u + "RTa", [128, TH], F32),
          "bb": b.sbuf(u + "RTb", [128, TH], F32)}
    MXA = b.sbuf(u + "MXA", [128, TH], F32)
    CB = b.sbuf(u + "CB", [128, TH], F32)
    FBg = b.sbuf(u + "FBg", [128, TH], F32)
    FBu = b.sbuf(u + "FBu", [128, TH], F32)
    TT = b.sbuf(u + "TT", [128, TH], F32)
    ABT = b.sbuf(u + "ABT", [128, 2 * TH], F32)
    ROWS = b.sbuf(u + "ROWS", [1, 8 * TH], F32)
    HQa = b.sbuf(u + "HQa", [128, (NTI // 2) * TH], BF16)
    HQb = b.sbuf(u + "HQb", [128, (NTI // 2) * TH], BF16)
    XQ = b.sbuf(u + "XQ", [128, NTD * TH], BF16)
    WA = b.sbuf(u + "WA", [128, NTD * TH], F32)
    XR = b.sbuf(u + "XR", [128, 2 * TH], F32)
    SQ = b.sbuf(u + "SQ", [128, 2 * TH], F32)

    def hq(i):
        return (HQa if i < NTI // 2 else HQb)[:, ts(i % (NTI // 2), TH)]

    def row(i):
        return ROWS[0:1, ts(i, TH)]

    PSsum = b.psum(u + "PSsum", [1, TH])
    PSssq = b.psum(u + "PSssq", [1, TH])
    PSb = [b.psum(u + "PSb0", [128, TH]), b.psum(u + "PSb1", [128, TH])]

    dX0 = b.mark("dX")
    sA0 = b.mark("sA")
    tS0_sq = b.mark("tS")
    vE_prev_q = b.mark("vE")

    @b.on("gpsimd")
    def _(g):
        b.wait(g, "init", 1)
        for _c, _n in rdy:
            b.wait(g, _c, _n)
        b.wait(g, "vE", vE_prev_q)
        dma = g.dma_start(out=WA[:].rearrange("p (i t) -> p i t", i=NTD),
                          in_=x_in.rearrange("(i p) t -> p i t", p=128)[:, :, ts(half, TH)])
        b.inc("dX", dma, 16)

    @b.on("scalar")
    def _(s):
        b.wait(s, "dX", dX0 + 16)
        for i in range(NTD):
            if i >= 2:
                b.wait(s, "tS", tS0_sq + i - 1)
            last = s.activation(SQ[:, ts(i % 2, TH)], WA[:, ts(i, TH)], AF.Square)
            b.inc("sA", last)

    @b.on("tensor")
    def _(t):
        b.wait(t, "init", 1)
        for i in range(NTD):
            b.wait(t, "sA", sA0 + i + 1)
            t.matmul(PSsum[:], k["ones_col_f"][:], WA[:, ts(i, TH)],
                     start=(i == 0), stop=(i == NTD - 1))
            last = t.matmul(PSssq[:], k["ones_col_f"][:], SQ[:, ts(i % 2, TH)],
                            start=(i == 0), stop=(i == NTD - 1))
            b.inc("tS", last)

    tS_stats = b.mark("tS")

    @b.on("vector")
    def _(v):
        b.wait(v, "tS", tS_stats)
        last = v.tensor_scalar(row(0), PSsum[:], 1.0 / D, None, OP.mult)
        b.fence(v, last, "fv")
        last = v.tensor_scalar(row(1), PSssq[:], 1.0 / D, None, OP.mult)
        b.fence(v, last, "fv")
        last = v.tensor_tensor(row(2), row(0), row(0), OP.mult)
        b.fence(v, last, "fv")
        last = v.tensor_tensor(row(1), row(1), row(2), OP.subtract)
        b.fence(v, last, "fv")
        last = v.tensor_scalar(row(1), row(1), EPS, None, OP.add)
        b.inc("vR", last)

    vR1 = b.mark("vR")

    @b.on("vector")
    def _(v):
        last = v.reciprocal(row(2), row(1))
        b.inc("vR", last)

    vR1b = b.mark("vR")

    @b.on("scalar")
    def _(s):
        b.wait(s, "vR", vR1b)
        last = s.activation(row(3), row(2), AF.Sqrt)
        b.inc("sR", last)

    sR1 = b.mark("sR")

    @b.on("tensor")
    def _(t):
        b.wait(t, "sR", sR1)
        t.matmul(PSb[0][:], k["ones_row_f"][:], row(0), start=True, stop=True)
        last = t.matmul(PSb[1][:], k["ones_row_f"][:], row(3), start=True, stop=True)
        b.inc("tS", last)

    tS_bc = b.mark("tS")

    @b.on("vector")
    def _(v):
        b.wait(v, "tS", tS_bc)
        for i in range(NTD):
            v.tensor_tensor(WA[:, ts(i, TH)], WA[:, ts(i, TH)], PSb[0][:], OP.subtract)
            last = v.tensor_tensor(WA[:, ts(i, TH)], WA[:, ts(i, TH)], PSb[1][:], OP.mult)
            b.inc("vE", last)

    vE_ln = b.mark("vE")

    @b.on("vector")
    def _(v):
        b.wait(v, "vE", vE_ln)
        last = radix4_rounds(v, lambda i: WA[:, ts(i, TH)], NTD, TT[:, :TH], MXA[:, :TH])
        b.inc("vF", last)

    vF1 = b.mark("vF")
    sE0 = b.mark("sE")
    tH0 = b.mark("tH")
    vM0_a = b.mark("vM")

    @b.on("tensor")
    def _(t):
        b.wait(t, "vF", vF1)
        for i in range(NTD):
            if i >= 2:
                b.wait(t, "sE", sE0 + i - 1)
            last = t.matmul(PSb[i % 2][:], k["h128f"][:], WA[:, ts(i, TH)],
                            start=True, stop=True)
            b.inc("tH", last)

    @b.on("scalar")
    def _(s):
        for i in range(NTD):
            b.wait(s, "tH", tH0 + i + 1)
            if i >= 2:
                b.wait(s, "vM", vM0_a + i - 1)
            s.copy(WA[:, ts(i, TH)], PSb[i % 2][:])
            last = s.activation(ABT[:, ts(i % 2, TH)], PSb[i % 2][:], AF.Abs)
            b.inc("sE", last)

    sE_h = b.mark("sE")

    @b.on("vector")
    def _(v):
        v.memset(MXA[:, :TH], 0.0)
        for i in range(NTD):
            b.wait(v, "sE", sE0 + i + 1)
            last = v.tensor_tensor(MXA[:, :TH], MXA[:, :TH], ABT[:, ts(i % 2, TH)], OP.max)
            b.inc("vM", last)

    b._taps = taps
    emit_tok_scales(b, k, MXA, PSb, TT, FW2048,
                    [(CB, "crow", None, None), (FBg, "wrow", 0, 1.0), (FBu, "wrow", 1, 1.0)])
    b._taps = None
    vR2 = b.mark("vR")

    @b.on("vector")
    def _(v):
        b.wait(v, "vR", vR2)
        last = None
        for i in range(NTD):
            v.tensor_tensor(TT[:, :TH], WA[:, ts(i, TH)], CB[:, :TH], OP.mult)
            last = round_ops(v, TT[:, :TH], XQ[:, ts(i, TH)], RT, TH)
        b.inc("vQ", last)

    vQ1 = b.mark("vQ")
    b.free(u + "SQ", u + "XR", u + "WA")

    def tap(name, src_ap, ch, n, eng="vector"):
        if taps is None or name not in taps:
            return

        @b.on("gpsimd")
        def _(g):
            b.wait(g, ch, n)
            dma = g.dma_start(out=taps[name], in_=src_ap)
            b.inc("dTap", dma, 16)

        tn = b.mark("dTap")

        @b.on(eng)
        def _(e):
            b.wait(e, "dTap", tn)

    tap("t_ln", WA[:, ts(0, TH)], "vE", vE_ln)
    tap("t_fw", WA[:, ts(0, TH)], "vF", vF1)
    tap("t_hev", WA[:, ts(0, TH)], "sE", sE0 + NTD)
    tap("t_cb", CB[:, :TH], "vR", vR2)
    tap("t_fbg", FBg[:, :TH], "vR", vR2)
    tap("t_xq", XQ[:, ts(0, TH)], "vQ", vQ1)

    HBa = b.sbuf(u + "HBa", [128, (NTI // 2) * TH], F32)
    HBb = b.sbuf(u + "HBb", [128, (NTI // 2) * TH], F32)
    GW = b.sbuf(u + "GW", [128, 2 * D], BF16)
    UW = b.sbuf(u + "UW", [128, 2 * D], BF16)
    GE = b.sbuf(u + "GE", [128, 2 * TH], F32)
    UE = b.sbuf(u + "UE", [128, 2 * TH], F32)
    SIL = b.sbuf(u + "SIL", [128, 2 * TH], F32)
    PSg = [b.psum(u + "PSg0", [128, TH]), b.psum(u + "PSg1", [128, TH])]
    PSu = [b.psum(u + "PSu0", [128, TH]), b.psum(u + "PSu1", [128, TH])]

    def hb(i):
        return (HBa if i < NTI // 2 else HBb)[:, ts(i % (NTI // 2), TH)]

    dWe0 = b.mark("dWe")
    dWo0 = b.mark("dWo")
    tM0 = b.mark("tM")
    vE_gu0 = b.mark("vE")
    sA_sil0 = b.mark("sA")
    vH0 = b.mark("vH")

    @b.on("sync")
    def _(sy):
        for _c, _n in rdy:
            b.wait(sy, _c, _n)
        b.wait(sy, "dOe", dOe_prev)
        b.wait(sy, "dOo", dOo_prev)
        b.wait(sy, "vQ", vQ1)
        for ot in range(NTI):
            if ot >= 2:
                b.wait(sy, "tM", tM0 + ot - 1)
            sl, lt = divmod(ot, 8)
            ch = "dWe" if ot % 2 == 0 else "dWo"
            dma = sy.dma_start(out=GW[:, ts(ot % 2, D)],
                               in_=tern_g[sl, lt].rearrange("p k o -> p (k o)"))
            b.inc(ch, dma, 16)
            dma = sy.dma_start(out=UW[:, ts(ot % 2, D)],
                               in_=tern_u[sl, lt].rearrange("p k o -> p (k o)"))
            b.inc(ch, dma, 16)

    @b.on("tensor")
    def _(t):
        b.wait(t, "vQ", vQ1)
        for ot in range(NTI):
            ch = "dWe" if ot % 2 == 0 else "dWo"
            b.wait(t, ch, (dWe0 if ot % 2 == 0 else dWo0) + 32 * (ot // 2 + 1))
            if ot >= 2:
                b.wait(t, "vH", vH0 + ot - 1)
            pg, pu = PSg[ot % 2], PSu[ot % 2]
            for kk in range(NTD):
                t.matmul(pg[:], GW[:, (ot % 2) * D + kk * 128:(ot % 2) * D + kk * 128 + 128],
                         XQ[:, ts(kk, TH)], start=(kk == 0), stop=(kk == NTD - 1))
                last = t.matmul(pu[:], UW[:, (ot % 2) * D + kk * 128:(ot % 2) * D + kk * 128 + 128],
                                XQ[:, ts(kk, TH)], start=(kk == 0), stop=(kk == NTD - 1))
            b.inc("tM", last)

    @b.on("scalar")
    def _(s):
        for ot in range(NTI):
            b.wait(s, "vE", vE_gu0 + 2 * ot + 1)
            last = s.activation(SIL[:, ts(ot % 2, TH)], GE[:, ts(ot % 2, TH)], AF.Silu)
            b.inc("sA", last)

    @b.on("vector")
    def _(v):
        for ot in range(NTI):
            b.wait(v, "tM", tM0 + ot + 1)
            last = v.tensor_tensor(GE[:, ts(ot % 2, TH)], PSg[ot % 2][:], FBg[:, :TH], OP.mult)
            b.inc("vE", last)
            last = v.tensor_tensor(UE[:, ts(ot % 2, TH)], PSu[ot % 2][:], FBu[:, :TH], OP.mult)
            b.inc("vE", last)
            if ot >= 1:
                b.wait(v, "sA", sA_sil0 + ot)
                last = v.tensor_tensor(hb(ot - 1), SIL[:, ts((ot - 1) % 2, TH)],
                                       UE[:, ts((ot - 1) % 2, TH)], OP.mult)
                b.inc("vH", last)
        b.wait(v, "sA", sA_sil0 + NTI)
        last = v.tensor_tensor(hb(NTI - 1), SIL[:, ts((NTI - 1) % 2, TH)],
                               UE[:, ts((NTI - 1) % 2, TH)], OP.mult)
        b.inc("vH", last)

    vH_gu = b.mark("vH")
    tap("t_hb", hb(0), "vH", vH_gu)
    b.free(u + "SIL", u + "UE", u + "GE", u + "UW", u + "GW")

    @b.on("vector")
    def _(v):
        b.wait(v, "vH", vH_gu)
        last = radix4_rounds(v, hb, NTI, TT[:, :TH], MXA[:, :TH])
        b.inc("vF", last)

    vF2 = b.mark("vF")
    sE1 = b.mark("sE")
    tH1 = b.mark("tH")
    vM0_b = b.mark("vM")

    @b.on("tensor")
    def _(t):
        b.wait(t, "vF", vF2)
        for i in range(NTI):
            if i >= 2:
                b.wait(t, "sE", sE1 + i - 1)
            last = t.matmul(PSg[i % 2][:], k["h128f"][:], hb(i), start=True, stop=True)
            b.inc("tH", last)

    @b.on("scalar")
    def _(s):
        for i in range(NTI):
            b.wait(s, "tH", tH1 + i + 1)
            if i >= 2:
                b.wait(s, "vM", vM0_b + i - 1)
            s.copy(hb(i), PSg[i % 2][:])
            last = s.activation(ABT[:, ts(i % 2, TH)], PSg[i % 2][:], AF.Abs)
            b.inc("sE", last)

    sE_h2 = b.mark("sE")

    @b.on("vector")
    def _(v):
        v.memset(MXA[:, :TH], 0.0)
        for i in range(NTI):
            b.wait(v, "sE", sE1 + i + 1)
            last = v.tensor_tensor(MXA[:, :TH], MXA[:, :TH], ABT[:, ts(i % 2, TH)], OP.max)
            b.inc("vM", last)

    emit_tok_scales(b, k, MXA, PSg, TT, FW8192,
                    [(CB, "crow", None, None), (FBg, "wrow", 2, 0.5)])
    vE_cb2 = b.mark("vR")

    @b.on("vector")
    def _(v):
        b.wait(v, "vR", vE_cb2)
        last = None
        for i in range(NTI):
            v.tensor_tensor(TT[:, :TH], hb(i), CB[:, :TH], OP.mult)
            last = round_ops(v, TT[:, :TH], hq(i), RT, TH)
        b.inc("vQ", last)

    vQ2 = b.mark("vQ")
    tap("t_hq", hq(0), "vQ", vQ2)
    tap("t_hb2", hb(0), "sE", sE1 + NTI)
    tap("t_cb2", CB[:, :TH], "vR", b.mark("vR"))
    b.free(u + "HBb", u + "HBa")

    DW = b.sbuf(u + "DW", [128, 2 * 16 * 128], BF16)
    XO = b.sbuf(u + "XO", [128, 2 * TH], F32)
    NDW = NTI // 16
    dDe0 = b.mark("dDe")
    dDo0 = b.mark("dDo")
    dXe0 = b.mark("dXe")
    dXo0 = b.mark("dXo")
    dOe0 = b.mark("dOe")
    dOo0 = b.mark("dOo")
    tM1 = b.mark("tM")
    vE_d0 = b.mark("vE")

    @b.on("sync")
    def _(sy):
        b.wait(sy, "vQ", vQ2)
        nload = 0
        for dt in range(NTD):
            sl, lt = divmod(dt, 2)
            for rr in range(NDW):
                if nload >= 2:
                    b.wait(sy, "tM", tM1 + nload - 1)
                ch = "dDe" if nload % 2 == 0 else "dDo"
                dma = sy.dma_start(out=DW[:, ts(nload % 2, 16 * 128)],
                                   in_=tern_d[sl, lt].rearrange("p k o -> p (k o)")[:, ts(rr, 16 * 128)])
                b.inc(ch, dma, 16)
                nload += 1

    @b.on("gpsimd")
    def _(g):
        b.wait(g, "vQ", vQ2)
        for dt in range(NTD):
            lch = "dXe" if dt % 2 == 0 else "dXo"
            och = "dOe" if dt % 2 == 0 else "dOo"
            if dt >= 2:
                b.wait(g, och, (dOe0 if dt % 2 == 0 else dOo0) + 16 * (dt // 2))
            dma = g.dma_start(out=XO[:, ts(dt % 2, TH)], in_=x_in[ts(dt, 128), ts(half, TH)])
            b.inc(lch, dma, 16)
            b.wait(g, "vE", vE_d0 + dt + 1)
            dma = g.dma_start(out=x_out[ts(dt, 128), ts(half, TH)], in_=XO[:, ts(dt % 2, TH)])
            b.inc(och, dma, 16)

    @b.on("tensor")
    def _(t):
        b.wait(t, "vQ", vQ2)
        nload = 0
        for dt in range(NTD):
            if dt >= 2:
                b.wait(t, "vE", vE_d0 + (dt - 1))
            ps = PSu[dt % 2]
            for rr in range(NDW):
                ch = "dDe" if nload % 2 == 0 else "dDo"
                b.wait(t, ch, (dDe0 if nload % 2 == 0 else dDo0) + 16 * (nload // 2 + 1))
                for kk in range(16):
                    gk = rr * 16 + kk
                    last = t.matmul(ps[:], DW[:, (nload % 2) * 2048 + kk * 128:(nload % 2) * 2048 + kk * 128 + 128],
                                    hq(gk), start=(gk == 0), stop=(gk == NTI - 1))
                b.inc("tM", last)
                nload += 1

    @b.on("vector")
    def _(v):
        for dt in range(NTD):
            b.wait(v, "tM", tM1 + NDW * (dt + 1))
            lch = "dXe" if dt % 2 == 0 else "dXo"
            b.wait(v, lch, (dXe0 if dt % 2 == 0 else dXo0) + 16 * (dt // 2 + 1))
            v.tensor_tensor(TT[:, :TH], PSu[dt % 2][:], FBg[:, :TH], OP.mult)
            last = v.tensor_tensor(XO[:, ts(dt % 2, TH)], XO[:, ts(dt % 2, TH)], TT[:, :TH], OP.add)
            b.inc("vE", last)

    dOe_end = b.mark("dOe")
    dOo_end = b.mark("dOo")

    @b.on("gpsimd")
    def _(g):
        b.wait(g, "dOe", dOe_end)
        b.wait(g, "dOo", dOo_end)

    b.free(u + "XO", u + "DW", u + "XQ", u + "HQb", u + "HQa",
           u + "ROWS", u + "ABT", u + "TT", u + "FBu", u + "FBg", u + "CB", u + "MXA",
           u + "RTb", u + "RTa", u + "RTd", u + "RTk", u + "RTi",
           u + "PSu1", u + "PSu0", u + "PSg1", u + "PSg0",
           u + "PSb1", u + "PSb0", u + "PSssq", u + "PSsum")



def build_ffn_test():
    nc = bass.Bass("TRN2", target_bir_lowering=False, debug=False, detect_race_conditions=False)
    x_in = nc.dram_tensor("x_in", [D, TPC], F32, kind="ExternalInput").ap()
    tg = nc.dram_tensor("tg", [NCORE, 8, 128, NTD, 128], BF16, kind="ExternalInput").ap()
    tu = nc.dram_tensor("tu", [NCORE, 8, 128, NTD, 128], BF16, kind="ExternalInput").ap()
    td = nc.dram_tensor("td", [NCORE, 2, 128, NTI, 128], BF16, kind="ExternalInput").ap()
    wsc_in = nc.dram_tensor("wsc_in", [1, 16], F32, kind="ExternalInput").ap()
    x_out = nc.dram_tensor("x_out", [D, TPC], F32, kind="ExternalOutput").ap()
    tapnames = ["t_ln", "t_fw", "t_hev", "t_cb", "t_fbg", "t_hb", "t_hb2", "t_cb2", "t_mxa", "t_tt"]
    taps = {n: nc.dram_tensor(n, [128, TH], F32, kind="ExternalOutput").ap()[:, :]
            for n in tapnames}
    taps["t_xq"] = nc.dram_tensor("t_xq", [128, TH], BF16, kind="ExternalOutput").ap()[:, :]
    taps["t_hq"] = nc.dram_tensor("t_hq", [128, TH], BF16, kind="ExternalOutput").ap()[:, :]

    b = Build(nc)
    k = alloc_persist(b)
    emit_const_setup(b, k)

    @b.on("gpsimd")
    def _(g):
        dma = g.dma_start(out=k["wsc"][:], in_=wsc_in[:, :])
        b.inc("wsc", dma, 16)

    pstmp = b.psum("pstmp_wsc", [128, 8])
    emit_wsc_cols(b, k, pstmp)
    b.free("pstmp_wsc")
    wn = b.mark("wsccol")
    emit_ffn_call(b, k, x_in, x_out, tg, tu, td, 0, [("wsccol", wn)], taps=taps)
    b.realize()
    return nc



def emit_rope(b, k, pos_row, dstkey="rs", slen=S, tag=""):
    RSC = b.sbuf(tag + "rsc", [64, 512], F32)
    THR = b.sbuf(tag + "thr", [1, 128], F32)
    JR = b.sbuf(tag + "jr", [1, 128], I32)
    PR = b.sbuf(tag + "pr", [1, slen], F32)
    RK = {"i32": b.sbuf(tag + "rk_i", [128, 512], I32), "kf": b.sbuf(tag + "rk_k", [128, 512], F32),
          "d": b.sbuf(tag + "rk_d", [128, 512], F32), "a": b.sbuf(tag + "rk_a", [128, 512], F32),
          "bb": b.sbuf(tag + "rk_b", [128, 512], F32)}
    W1 = b.sbuf(tag + "rk_w", [128, 512], F32)
    PSR = b.psum(tag + "psr", [128, 512])
    TWO_PI = 2.0 * math.pi
    LN1E4 = math.log(10000.0)

    dRp0 = b.mark("dRp")

    @b.on("gpsimd")
    def _(g):
        dma = g.dma_start(out=PR[:], in_=pos_row[:, :])
        b.inc("dRp", dma, 16)
        last = g.iota(JR[:], [[1, 128]], channel_multiplier=0)
        b.inc("gI2", last)

    gI2 = b.mark("gI2")

    @b.on("vector")
    def _(v):
        b.wait(v, "gI2", gI2)
        last = v.tensor_copy(THR[:], JR[:])
        b.fence(v, last, "fv")
        last = v.tensor_copy(RSC[0:1, 128:256], THR[:])
        b.fence(v, last, "fv")
        for ii, thr_c in enumerate((32.0, 64.0, 96.0)):
            last = v.tensor_scalar(RSC[0:1, 0:128], RSC[0:1, 128:256], thr_c, -32.0,
                                   OP.is_ge, OP.mult)
            b.fence(v, last, "fv")
            last = v.tensor_tensor(THR[:], THR[:], RSC[0:1, 0:128], OP.add)
            if ii < 2:
                b.fence(v, last, "fv")
        b.inc("vRo", last)

    vRo1 = b.mark("vRo")

    @b.on("scalar")
    def _(s):
        b.wait(s, "vRo", vRo1)
        last = s.activation(THR[:], THR[:], AF.Exp, bias=0.0, scale=-LN1E4 / 64.0)
        b.inc("sRo", last)

    sRo1 = b.mark("sRo")

    vRo_base = b.mark("vRo")

    @b.on("tensor")
    def _(t):
        b.wait(t, "init", 1)
        b.wait(t, "sRo", sRo1)
        b.wait(t, "dRp", dRp0 + 16)
        for c in range(slen // 512):
            if c >= 1:
                b.wait(t, "vRo", vRo_base + c)
            last = t.matmul(PSR[:], THR[:], PR[0:1, ts(c, 512)], start=True, stop=True)
            b.inc("tRo", last)

    tRo_b = b.mark("tRo") - slen // 512

    @b.on("vector")
    def _(v):
        C1 = 12868.0 / 2048.0
        C2 = TWO_PI - C1
        for c in range(slen // 512):
            b.wait(v, "tRo", tRo_b + c + 1)
            last = v.tensor_scalar(W1[:], PSR[:], math.pi / 4.0, 1.0 / TWO_PI, OP.add, OP.mult)
            b.fence(v, last, "fv")
            last = v.tensor_copy(RK["i32"][:], W1[:])
            b.fence(v, last, "fv")
            last = v.tensor_copy(RK["kf"][:], RK["i32"][:])
            b.fence(v, last, "fv")
            last = v.tensor_scalar(RK["a"][:], RK["kf"][:], -C1, None, OP.mult)
            b.fence(v, last, "fv")
            last = v.tensor_tensor(W1[:], PSR[:], RK["a"][:], OP.add)
            b.fence(v, last, "fv")
            last = v.tensor_scalar(RK["bb"][:], RK["kf"][:], -C2, math.pi / 4.0,
                                   OP.mult, OP.add)
            b.fence(v, last, "fv")
            last = v.tensor_tensor(W1[:], W1[:], RK["bb"][:], OP.add)
            b.fence(v, last, "fv")
            last = v.tensor_scalar(RK["a"][:], W1[:], math.pi, -TWO_PI, OP.is_gt, OP.mult)
            b.fence(v, last, "fv")
            last = v.tensor_tensor(W1[:], W1[:], RK["a"][:], OP.add)
            b.fence(v, last, "fv")
            last = v.tensor_scalar(RK["a"][:], W1[:], -math.pi, TWO_PI, OP.is_lt, OP.mult)
            b.fence(v, last, "fv")
            last = v.tensor_tensor(k[dstkey][:, ts(c, 512)], W1[:], RK["a"][:], OP.add)
            b.inc("vRo", last)

    vRo2 = b.mark("vRo")

    @b.on("scalar")
    def _(s):
        for c in range(slen // 512):
            b.wait(s, "vRo", vRo_base + c + 1)
            last = s.activation(k[dstkey][:, ts(c, 512)], k[dstkey][:, ts(c, 512)], AF.Sin, bias=0.0, scale=1.0)
            b.inc("sRo", last)

    sRo2 = b.mark("sRo")

    @b.on("vector")
    def _(v):
        b.wait(v, "sRo", sRo2)
        last = v.tensor_scalar(k[dstkey][:], k[dstkey][:], math.sqrt(2.0), None, OP.mult)
        b.inc("rope", last)

    b.free(tag + "psr", tag + "rk_w", tag + "rk_b", tag + "rk_a", tag + "rk_d",
           tag + "rk_k", tag + "rk_i", tag + "pr", tag + "jr", tag + "thr", tag + "rsc")



def emit_weight_prep(b, k, io):
    WB = b.sbuf("wpWB", [128, 2 * 512], F32)
    AB = b.sbuf("wpAB", [128, 2 * 512], F32)
    TB = b.sbuf("wpTB", [128, 512], BF16)
    TB2 = b.sbuf("wpTB2", [128, 512], F32)
    TO = b.sbuf("wpTO", [128, 512], BF16)
    ARR = b.sbuf("wpAR", [1, 16], F32)
    PSW = b.psum("wpPSW", [1, 512])
    PSX = b.psum("wpPSX", [128, 512])

    slices = [(io["wg_rows"], IFF // 8, D, 0), (io["wu_rows"], IFF // 8, D, 1),
              (io["wd_rows"], D // 8, IFF, 2)]
    ropeN_wp = b.mark("rope")

    for wi, (src, R, Cc, widx) in enumerate(slices):
        ntile = (R // 128) * (Cc // 512)
        dch = "dWPe" if True else ""
        d0e = b.mark("dWPe")
        d0o = b.mark("dWPo")
        sA0 = b.mark("sA")
        vM0 = b.mark("vM")

        @b.on("gpsimd")
        def _(g, src=src, R=R, Cc=Cc, ntile=ntile, d0e=d0e, d0o=d0o, sA0=sA0):
            b.wait(g, "init", 1)
            b.wait(g, "rope", ropeN_wp)
            for i in range(ntile):
                ch = "dWPe" if i % 2 == 0 else "dWPo"
                b.wait(g, "sA", sA0 + (i - 1 if i >= 2 else 0))
                r, c = divmod(i, Cc // 512)
                dma = g.dma_start(out=WB[:, ts(i % 2, 512)],
                                  in_=src[ts(r, 128), ts(c, 512)])
                b.inc(ch, dma, 16)

        @b.on("scalar")
        def _(s, ntile=ntile, d0e=d0e, d0o=d0o, tS0=b.mark("tS")):
            b.wait(s, "tS", tS0)
            for i in range(ntile):
                ch = "dWPe" if i % 2 == 0 else "dWPo"
                b.wait(s, ch, (d0e if i % 2 == 0 else d0o) + 16 * (i // 2 + 1))
                last = s.activation(AB[:, ts(i % 2, 512)], WB[:, ts(i % 2, 512)], AF.Abs)
                b.inc("sA", last)

        @b.on("tensor")
        def _(t, ntile=ntile, sA0=sA0, vM0=vM0):
            b.wait(t, "init", 1)
            b.wait(t, "vM", vM0)
            for i in range(ntile):
                b.wait(t, "sA", sA0 + i + 1)
                last = t.matmul(PSW[:], k["ones_col_f"][:], AB[:, ts(i % 2, 512)],
                                start=(i == 0), stop=(i == ntile - 1))
            b.inc("tS", last)

        tSw = b.mark("tS")

        @b.on("vector")
        def _(v, wi=wi, tSw=tSw):
            if wi == 0:
                v.memset(ARR[:], 0.0)
            b.wait(v, "tS", tSw)
            last = v.reduce_sum(ARR[0:1, wi:wi + 1], PSW[:], axis=mybir.AxisListType.X)
            b.inc("vM", last)

    vM_ar = b.mark("vM")
    if PHASE_LIMIT == 10:
        return [("vM", vM_ar)]

    @b.on("gpsimd")
    def _(g):
        b.wait(g, "vM", vM_ar)
        dma = g.dma_start(out=io["ar_in"][:, :], in_=ARR[0:1, 0:8])
        b.inc("dAR", dma, 16)

    dAR1 = b.mark("dAR")

    @b.on("gpsimd")
    def _(g):
        b.wait(g, "dAR", dAR1)
        cc = g.collective_compute("AllReduce", OP.add,
                                  replica_groups=[list(range(NCORE))],
                                  ins=[io["ar_in"].opt()],
                                  outs=[io["ar_out"].opt()])
        b.inc("ccAR", cc)

    ccAR1 = b.mark("ccAR")

    @b.on("gpsimd")
    def _(g):
        b.wait(g, "ccAR", ccAR1)
        dma = g.dma_start(out=ARR[0:1, 8:16], in_=io["ar_out"][:, :])
        b.inc("dAR", dma, 16)

    dAR2 = b.mark("dAR")
    if PHASE_LIMIT == 11:
        return [("dAR", dAR2)]

    @b.on("vector")
    def _(v):
        b.wait(v, "dAR", dAR2)
        last = v.tensor_scalar(k["wsc"][0:1, 0:3], ARR[0:1, 8:11], 1.0 / (IFF * D), EPS,
                               OP.mult, OP.max)
        b.inc("wsc", last)

    wsc1 = b.mark("wsc")

    @b.on("tensor")
    def _(t):
        b.wait(t, "wsc", wsc1)
        last = t.matmul(PSX[:, 0:3], k["ones_row_f"][:], k["wsc"][0:1, 0:3],
                        start=True, stop=True)
        b.inc("tS", last)

    tS_wc = b.mark("tS")

    @b.on("vector")
    def _(v):
        b.wait(v, "tS", tS_wc)
        last = v.tensor_copy(k["wscol"][:, 0:3], PSX[:, 0:3])
        b.fence(v, last, "fv")
        last = v.tensor_scalar(k["wscol"][:, 3:6], k["wscol"][:, 0:3], 0.5, None, OP.mult)
        b.fence(v, last, "fv")
        last = v.tensor_scalar(k["wscol"][:, 6:9], k["wscol"][:, 3:6], -1.0, None, OP.mult)
        b.inc("wsccol", last)

    wcol = b.mark("wsccol")
    if PHASE_LIMIT == 12:
        return [("wsccol", wcol)]

    for wi, (src, R, Cc, widx) in enumerate(slices):
        dst = [io["ag_g_in"], io["ag_u_in"], io["ag_d_in"]][wi]
        ntA = R // 128
        ntB = Cc // 512
        d1e = b.mark("dWQe")
        d1o = b.mark("dWQo")
        vT0 = b.mark("vT")
        tX1 = b.mark("tX")
        sE1 = b.mark("sE")
        dTo0 = b.mark("dTo")

        @b.on("gpsimd")
        def _(g, src=src, ntA=ntA, ntB=ntB, vT0=vT0, sE1=sE1, dst=dst):
            def store(i):
                b.wait(g, "sE", sE1 + 4 * (i + 1))
                a, c = divmod(i, ntB)
                dma = g.dma_start(
                    out=dst[a, :, 4 * c:4 * c + 4, :],
                    in_=TO[:, :512].rearrange("p (cc o) -> p cc o", cc=4))
                b.inc("dTo", dma, 16)
            for i in range(ntA * ntB):
                ch = "dWQe" if i % 2 == 0 else "dWQo"
                b.wait(g, "vT", vT0 + (i - 1 if i >= 2 else 0))
                a, c = divmod(i, ntB)
                dma = g.dma_start(out=WB[:, ts(i % 2, 512)],
                                  in_=src[ts(a, 128), ts(c, 512)])
                b.inc(ch, dma, 16)
                if i >= 1:
                    store(i - 1)
            store(ntA * ntB - 1)

        @b.on("vector")
        def _(v, ntA=ntA, ntB=ntB, d1e=d1e, d1o=d1o, widx=widx, sE1=sE1, tX1=tX1):
            b.wait(v, "wsccol", wcol)
            b.wait(v, "tX", tX1)
            for i in range(ntA * ntB):
                ch = "dWQe" if i % 2 == 0 else "dWQo"
                b.wait(v, ch, (d1e if i % 2 == 0 else d1o) + 16 * (i // 2 + 1))
                if i >= 1:
                    b.wait(v, "sE", sE1 + 4 * i)
                v.tensor_scalar(TB2[:, :512], WB[:, ts(i % 2, 512)],
                                k["wscol"][:, 3 + widx:4 + widx], None, OP.is_gt)
                v.tensor_scalar(AB[:, ts(i % 2, 512)], WB[:, ts(i % 2, 512)],
                                k["wscol"][:, 6 + widx:7 + widx], None, OP.is_lt)
                last = v.tensor_tensor(TB[:, :512], TB2[:, :512], AB[:, ts(i % 2, 512)],
                                       OP.subtract)
                b.inc("vT", last)

        @b.on("tensor")
        def _(t, ntA=ntA, ntB=ntB, vT0=vT0, sE1=sE1):
            for i in range(ntA * ntB):
                b.wait(t, "vT", vT0 + i + 1)
                b.wait(t, "sE", sE1 + 4 * i)
                for c4 in range(4):
                    last = t.matmul(PSX[:, ts(c4, 128)], TB[:, ts(c4, 128)],
                                    k["id128b"][:], start=True, stop=True)
                b.inc("tX", last)

        @b.on("scalar")
        def _(s, ntA=ntA, ntB=ntB, tX1=tX1, dTo0=dTo0):
            for i in range(ntA * ntB):
                b.wait(s, "tX", tX1 + i + 1)
                b.wait(s, "dTo", dTo0 + 16 * i)
                for c4 in range(4):
                    last = s.copy(TO[:, ts(c4, 128)], PSX[:, ts(c4, 128)])
                b.inc("sE", last, 4)



    dTo_all = b.mark("dTo")
    if PHASE_LIMIT == 13:
        return [("dTo", dTo_all)]

    @b.on("gpsimd")
    def _(g):
        b.wait(g, "dTo", dTo_all)
        for ji, (key_in, key_out) in enumerate((("ag_g_in", "ag_g"), ("ag_u_in", "ag_u"),
                                                ("ag_d_in", "ag_d"))):
            cc = g.collective_compute("AllGather", OP.bypass,
                                      replica_groups=[list(range(NCORE))],
                                      ins=[io[key_in].opt()],
                                      outs=[io[key_out].opt()])
            b.inc("ternAG" if ji == 2 else "ccAG", cc)

    b.free("wpPSX", "wpPSW", "wpAR", "wpTO", "wpTB2", "wpTB", "wpAB", "wpWB")



def emit_proj(b, k, io, gates):
    x1 = io["x1"]
    NQ = TPC // TH
    LNW = b.sbuf("pjLNW", [128, NTD * TH], F32)
    SQ2 = b.sbuf("pjSQ", [128, 2 * TH], F32)
    WDQ = b.sbuf("pjWDQ", [128, 16 * 512], F32)
    WDKV = b.sbuf("pjWDKV", [128, 16 * 512], F32)
    OUTT = b.sbuf("pjOUT", [128, 16 * TH], F32)
    OHL = b.sbuf("pjOHL", [128, 2 * TH], BF16)
    ROWS2 = b.sbuf("pjROWS", [1, 8 * TH], F32)
    PS1 = b.psum("pjPS1", [1, TH])
    PS2a = b.psum("pjPS2a", [1, TH])
    PSB = [b.psum("pjPSB0", [128, TH]), b.psum("pjPSB1", [128, TH])]
    PSC = [b.psum("pjPSC0", [128, TH]), b.psum("pjPSC1", [128, TH])]

    def row2(i):
        return ROWS2[0:1, ts(i, TH)]

    @b.on("sync")
    def _(sy):
        for _c, _n in gates:
            b.wait(sy, _c, _n)
        dma = sy.dma_start(out=WDQ[:].rearrange("p (h c) -> p h c", h=16),
                           in_=io["Wdq"].rearrange("(h p) c -> p h c", p=128))
        b.inc("dPJ", dma, 16)
        dma = sy.dma_start(out=WDKV[:].rearrange("p (h c) -> p h c", h=16),
                           in_=io["Wdkv"].rearrange("(h p) c -> p h c", p=128))
        b.inc("dPJ", dma, 16)

    dPJw = b.mark("dPJ")

    for q in range(NQ):
        dXq0 = b.mark("dXq")
        sAq0 = b.mark("sA")
        tSq0 = b.mark("tS")
        dAGq0 = b.mark("dAGq")

        @b.on("gpsimd")
        def _(g, q=q, dAGq0=dAGq0):
            for _c, _n in gates:
                b.wait(g, _c, _n)
            if q > 0:
                b.wait(g, "dAGq", dAGq0)
            dma = g.dma_start(out=LNW[:].rearrange("p (i t) -> p i t", i=NTD),
                              in_=x1.rearrange("(i p) t -> p i t", p=128)[:, :, ts(q, TH)])
            b.inc("dXq", dma, 16)

        @b.on("scalar")
        def _(s, dXq0=dXq0, tSq0=tSq0):
            b.wait(s, "dXq", dXq0 + 16)
            for i in range(NTD):
                if i >= 2:
                    b.wait(s, "tS", tSq0 + i - 1)
                last = s.activation(SQ2[:, ts(i % 2, TH)], LNW[:, ts(i, TH)], AF.Square)
                b.inc("sA", last)

        @b.on("tensor")
        def _(t, sAq0=sAq0):
            b.wait(t, "init", 1)
            for i in range(NTD):
                b.wait(t, "sA", sAq0 + i + 1)
                t.matmul(PS1[:], k["ones_col_f"][:], LNW[:, ts(i, TH)],
                         start=(i == 0), stop=(i == NTD - 1))
                last = t.matmul(PS2a[:], k["ones_col_f"][:], SQ2[:, ts(i % 2, TH)],
                                start=(i == 0), stop=(i == NTD - 1))
                b.inc("tS", last)

        tS_st = b.mark("tS")

        @b.on("vector")
        def _(v, tS_st=tS_st):
            b.wait(v, "tS", tS_st)
            last = v.tensor_scalar(row2(0), PS1[:], 1.0 / D, None, OP.mult)
            b.fence(v, last, "fv")
            last = v.tensor_scalar(row2(1), PS2a[:], 1.0 / D, None, OP.mult)
            b.fence(v, last, "fv")
            last = v.tensor_tensor(row2(2), row2(0), row2(0), OP.mult)
            b.fence(v, last, "fv")
            last = v.tensor_tensor(row2(1), row2(1), row2(2), OP.subtract)
            b.fence(v, last, "fv")
            last = v.tensor_scalar(row2(1), row2(1), EPS, None, OP.add)
            b.fence(v, last, "fv")
            last = v.reciprocal(row2(2), row2(1))
            b.inc("vR", last)

        vR_r = b.mark("vR")

        @b.on("scalar")
        def _(s, vR_r=vR_r):
            b.wait(s, "vR", vR_r)
            last = s.activation(row2(3), row2(2), AF.Sqrt)
            b.inc("sR", last)

        sR_r = b.mark("sR")

        @b.on("tensor")
        def _(t, sR_r=sR_r):
            b.wait(t, "sR", sR_r)
            t.matmul(PSB[0][:], k["ones_row_f"][:], row2(0), start=True, stop=True)
            last = t.matmul(PSB[1][:], k["ones_row_f"][:], row2(3), start=True, stop=True)
            b.inc("tS", last)

        tS_bc = b.mark("tS")

        @b.on("vector")
        def _(v, tS_bc=tS_bc):
            b.wait(v, "tS", tS_bc)
            for i in range(NTD):
                v.tensor_tensor(LNW[:, ts(i, TH)], LNW[:, ts(i, TH)], PSB[0][:], OP.subtract)
                last = v.tensor_tensor(LNW[:, ts(i, TH)], LNW[:, ts(i, TH)], PSB[1][:], OP.mult)
            b.inc("vE", last)

        vE_ln2 = b.mark("vE")

        tM_p0 = b.mark("tM")

        @b.on("tensor")
        def _(t, vE_ln2=vE_ln2):
            b.wait(t, "vE", vE_ln2)
            b.wait(t, "dPJ", dPJw)
            for j in range(8):
                W = WDQ if j < 4 else WDKV
                cch = j % 4
                ps = PSC[j % 2]
                if j >= 2:
                    b.wait(t, "vE", vE_ln2 + j - 1)
                for hh in range(16):
                    last = t.matmul(ps[:], W[:, hh * 512 + cch * 128: hh * 512 + cch * 128 + 128],
                                    LNW[:, ts(hh, TH)], start=(hh == 0), stop=(hh == 15))
                b.inc("tM", last)

        @b.on("vector")
        def _(v, tM_p0=tM_p0):
            for j in range(8):
                b.wait(v, "tM", tM_p0 + j + 1)
                last = v.tensor_copy(OUTT[:, ts(j, TH)], PSC[j % 2][:])
                b.inc("vE", last)

        vE_craw = b.mark("vE")
        sA_r0 = b.mark("sA")
        tS_v0 = b.mark("tS")

        @b.on("scalar")
        def _(s, vE_craw=vE_craw, tS_v0=tS_v0):
            for j in range(8):
                b.wait(s, "vE", vE_craw - 8 + j + 1)
                if j >= 2:
                    b.wait(s, "tS", tS_v0 + j - 1)
                last = s.activation(SQ2[:, ts(j % 2, TH)], OUTT[:, ts(j, TH)], AF.Square)
                b.inc("sA", last)

        @b.on("tensor")
        def _(t, sA_r0=sA_r0):
            for j in range(8):
                b.wait(t, "sA", sA_r0 + j + 1)
                last = t.matmul(PS1[:] if j < 4 else PS2a[:], k["ones_col_f"][:],
                                SQ2[:, ts(j % 2, TH)], start=(j % 4 == 0), stop=(j % 4 == 3))
                b.inc("tS", last)

        tS_rms = b.mark("tS")

        @b.on("vector")
        def _(v, tS_rms=tS_rms):
            b.wait(v, "tS", tS_rms)
            last = v.tensor_scalar(row2(4), PS1[:], 1.0 / DKV, EPS, OP.mult, OP.add)
            b.fence(v, last, "fv")
            last = v.tensor_scalar(row2(5), PS2a[:], 1.0 / DKV, EPS, OP.mult, OP.add)
            b.fence(v, last, "fv")
            last = v.reciprocal(row2(4), row2(4))
            b.fence(v, last, "fv")
            last = v.reciprocal(row2(5), row2(5))
            b.inc("vR", last)

        vR_rms = b.mark("vR")

        @b.on("scalar")
        def _(s, vR_rms=vR_rms):
            b.wait(s, "vR", vR_rms)
            s.activation(row2(4), row2(4), AF.Sqrt)
            last = s.activation(row2(5), row2(5), AF.Sqrt)
            b.inc("sR", last)

        sR_rms = b.mark("sR")

        @b.on("tensor")
        def _(t, sR_rms=sR_rms):
            b.wait(t, "sR", sR_rms)
            t.matmul(PSB[0][:], k["ones_row_f"][:], row2(4), start=True, stop=True)
            last = t.matmul(PSB[1][:], k["ones_row_f"][:], row2(5), start=True, stop=True)
            b.inc("tS", last)

        tS_rbc = b.mark("tS")

        @b.on("vector")
        def _(v, tS_rbc=tS_rbc):
            b.wait(v, "tS", tS_rbc)
            for j in range(8):
                gcol = k["gq_col"] if j < 4 else k["gkv_col"]
                v.tensor_tensor(OUTT[:, ts(j, TH)], OUTT[:, ts(j, TH)],
                                PSB[0 if j < 4 else 1][:], OP.mult)
                last = v.tensor_scalar(OUTT[:, ts(j, TH)], OUTT[:, ts(j, TH)],
                                       gcol[:, (j % 4):(j % 4) + 1], None, OP.mult)
            b.inc("vE", last)

        vE_rms = b.mark("vE")

        vE_sp0 = b.mark("vE")
        dAG_s0 = b.mark("dAGq")
        vE_base_for_split = b.mark("vE")

        @b.on("vector")
        def _(v, vE_rms=vE_rms, dAG_s0=dAG_s0):
            b.wait(v, "vE", vE_rms)
            for j in range(24):
                if j >= 1:
                    b.wait(v, "dAGq", dAG_s0 + 32 * j)
                srcap = OUTT[:, ts(j, TH)] if j < 8 else LNW[:, ts(j - 8, TH)]
                last = v.tensor_copy(OHL[:, ts(0, TH)], srcap)
                b.fence(v, last, "fv")
                last = v.tensor_tensor(OHL[:, ts(1, TH)], srcap, OHL[:, ts(0, TH)], OP.subtract)
                b.inc("vE", last)

        @b.on("gpsimd")
        def _(g, q=q, vE_sp0=vE_sp0):
            for j in range(24):
                b.wait(g, "vE", vE_sp0 + j + 1)
                if j < 4:
                    ih, il = j, 4 + j
                elif j < 8:
                    ih, il = 8 + (j - 4), 12 + (j - 4)
                else:
                    ih, il = 16 + (j - 8), 32 + (j - 8)
                dma = g.dma_start(out=io["ag2_in"][ih, :, ts(q, TH)], in_=OHL[:, ts(0, TH)])
                b.inc("dAGq", dma, 16)
                dma = g.dma_start(out=io["ag2_in"][il, :, ts(q, TH)], in_=OHL[:, ts(1, TH)])
                b.inc("dAGq", dma, 16)

    dAG_all = b.mark("dAGq")

    @b.on("gpsimd")
    def _(g):
        b.wait(g, "dAGq", dAG_all)
        cc = g.collective_compute("AllGather", OP.bypass,
                                  replica_groups=[list(range(NCORE))],
                                  ins=[io["ag2_in"].opt()],
                                  outs=[io["ag2_out"].opt()])
        b.inc("projAG", cc)

    b.free("pjPSC1", "pjPSC0", "pjPSB1", "pjPSB0", "pjPS2a", "pjPS1",
           "pjROWS", "pjOHL", "pjOUT", "pjWDKV", "pjWDQ", "pjSQ", "pjLNW")



def emit_attn_masks(b, k, MK, pAG):
    QRB = b.sbuf("amQRB", [128, 512], F32)
    KCB = b.sbuf("amKCB", [128, 512], F32)
    QIR = b.sbuf("amQIR", [1, 512], F32)
    QII = b.sbuf("amQII", [1, 512], I32)
    KCF = b.sbuf("amKCF", [128, 1], F32)
    PSM = b.psum("amPSM", [128, 512])

    @b.on("gpsimd")
    def _(g):
        b.wait(g, "init", 1)
        b.wait(g, "projAG", pAG)
        last = g.iota(QII[:], [[1, 512]], channel_multiplier=0)
        b.inc("gI3", last)

    gI3 = b.mark("gI3")

    @b.on("vector")
    def _(v):
        b.wait(v, "gI3", gI3)
        last = v.tensor_copy(QIR[:], QII[:])
        b.fence(v, last, "fv")
        last = v.tensor_copy(KCF[:], k["i_ii"][:])
        b.inc("vA", last)

    vA1 = b.mark("vA")

    @b.on("tensor")
    def _(t):
        b.wait(t, "vA", vA1)
        last = t.matmul(PSM[:], k["ones_row_f"][:], QIR[:], start=True, stop=True)
        b.inc("tS", last)

    tSm = b.mark("tS")

    @b.on("vector")
    def _(v):
        b.wait(v, "tS", tSm)
        last = v.tensor_copy(QRB[:], PSM[:])
        b.fence(v, last, "fv")
        last = v.tensor_scalar(KCB[:], QRB[:], 0.0, KCF[:, 0:1], OP.mult, OP.add)
        b.fence(v, last, "fv")
        for r in range(4):
            last = v.tensor_scalar(MK[:, ts(r, 512)], KCB[:], 128.0 * r, None, OP.add)
            b.fence(v, last, "fv")
            last = v.tensor_tensor(MK[:, ts(r, 512)], QRB[:], MK[:, ts(r, 512)], OP.is_ge)
            if r < 3:
                b.fence(v, last, "fv")
        b.inc("amask", last)

    b.free("amPSM", "amKCF", "amQII", "amQIR", "amKCB", "amQRB")


def emit_attn(b, k, io):
    MK = b.sbuf("atMK", [128, 4 * 512], BF16)
    pAG = b.mark("projAG")
    emit_attn_masks(b, k, MK, pAG)
    amask_n = b.mark("amask")
    PW = 3 * 64 + 128

    WHL = b.sbuf("atWHL", [128, 2 * 4 * HPC * PW], BF16)
    WKRHL = b.sbuf("atWKRHL", [128, 2 * 16 * HPC * 64], BF16)
    NEGC = b.sbuf("atNEGC", [1, 8], F32)
    WQS = b.sbuf("atWQS", [128, HPC * 4 * 64], F32)
    WRS = b.sbuf("atWRS", [128, HPC * 4 * 64], F32)
    WKS = b.sbuf("atWKS", [128, HPC * 4 * 64], F32)
    WVS = b.sbuf("atWVS", [128, HPC * 4 * 128], F32)
    WKRS = b.sbuf("atWKRS", [128, 16 * HPC * 64], F32)

    @b.on("sync")
    def _(sy):
        b.wait(sy, "projAG", pAG)
        for nm, dst in (("Wuq", WQS), ("Wqr", WRS), ("Wuk", WKS), ("Wuv", WVS)):
            dma = sy.dma_start(out=dst[:].rearrange("p (c x) -> p c x", c=4),
                               in_=io[nm].rearrange("(c p) x -> p c x", p=128))
            b.inc("dAW", dma, 16)
        dma = sy.dma_start(out=WKRS[:].rearrange("p (h x) -> p h x", h=16),
                           in_=io["Wkr_h"].rearrange("(h p) x -> p h x", p=128))
        b.inc("dAW", dma, 16)

    dAW1 = b.mark("dAW")

    @b.on("vector")
    def _(v):
        b.wait(v, "dAW", dAW1)
        last = None
        for (wsrc, n) in ((WQS, HPC * 4 * 64), (WRS, HPC * 4 * 64), (WKS, HPC * 4 * 64),
                          (WVS, HPC * 4 * 128)):
            pass
        HALF = 4 * HPC * PW
        off = 0
        for wsrc, wid in ((WQS, HPC * 4 * 64), (WRS, HPC * 4 * 64), (WKS, HPC * 4 * 64),
                          (WVS, HPC * 4 * 128)):
            last = v.tensor_copy(WHL[:, off:off + wid], wsrc[:, :wid])
            b.fence(v, last, "fv")
            last = v.tensor_tensor(WHL[:, HALF + off: HALF + off + wid], wsrc[:, :wid],
                                   WHL[:, off:off + wid], OP.subtract)
            b.fence(v, last, "fv")
            off += wid
        KHALF = 16 * HPC * 64
        last = v.tensor_copy(WKRHL[:, :KHALF], WKRS[:, :KHALF])
        b.fence(v, last, "fv")
        last = v.tensor_tensor(WKRHL[:, KHALF:2 * KHALF], WKRS[:, :KHALF],
                               WKRHL[:, :KHALF], OP.subtract)
        b.inc("vA", last)

    vA_w = b.mark("vA")
    b.free("atWKRS", "atWVS", "atWKS", "atWRS", "atWQS")

    def wsl(hi, kind, cch, hh):
        HALF = 4 * HPC * PW
        base = (0 if hi else HALF)
        woff = {"q": 0, "r": HPC * 4 * 64, "k": 2 * HPC * 4 * 64, "v": 3 * HPC * 4 * 64}[kind]
        wid = 128 if kind == "v" else 64
        o = base + woff + (cch * HPC + hh) * wid
        return WHL[:, o:o + wid]

    def wkr(hi, hch, hh):
        KHALF = 16 * HPC * 64
        o = (0 if hi else KHALF) + (hch * HPC + hh) * 64
        return WKRHL[:, o:o + 64]

    emit_attn_kprep(b, k, io, wsl, wkr, NEGC, vA_w)
    for hh in range(HPC):
        gates = [("dKSe", b.mark("dKSe")), ("dKSo", b.mark("dKSo")),
                 ("dA2A", b.mark("dA2A")), ("vE", b.mark("vE"))]
        emit_attn_head(b, k, io, hh, MK, wsl, NEGC, amask_n, gates)

    dA2A_all = b.mark("dA2A")

    @b.on("gpsimd")
    def _(g):
        b.wait(g, "dA2A", dA2A_all)
        with allow_ici_all_to_all():
            cc = g.collective_compute("AllToAll", OP.bypass,
                                      replica_groups=[list(range(NCORE))],
                                      ins=[io["a2a_in"].opt()],
                                      outs=[io["a2a_out"].opt()])
        b.inc("attA2A", cc)

    b.free("atNEGC", "atWKRHL", "atWHL", "atMK")


def emit_attn_kprep(b, k, io, wsl, wkr, NEGC, vA_w):
    ag = io["ag2_out"]
    u = "kp"
    CCB = b.sbuf(u + "CCB", [128, 2 * 8 * 512], BF16)
    LNB = b.sbuf(u + "LNB", [128, 2 * 32 * 512], BF16)
    KT = b.sbuf(u + "KT", [128, 2 * 512], F32)
    KB = b.sbuf(u + "KB", [128, 4 * 512], BF16)
    KSQ = b.sbuf(u + "KSQ", [128, 512], F32)
    CMX = b.sbuf(u + "CMX", [1, 16], F32)
    PSr = b.psum(u + "PSr", [1, 512])
    PSr2 = b.psum(u + "PSr2", [1, 512])
    PSkr = b.psum(u + "PSkr", [128, 512])
    PSkb = b.psum(u + "PSkb", [128, 512])

    NKT = S // 512
    ropeN = b.mark("rope")
    dCK0 = {0: b.mark("dCKe"), 1: b.mark("dCKo")}
    tK0 = b.mark("tM")
    vK0 = b.mark("vE")
    sK0 = b.mark("sA")
    tR0 = b.mark("tS")
    vMX0 = b.mark("vM")
    dKS0 = {0: b.mark("dKSe"), 1: b.mark("dKSo")}

    @b.on("gpsimd")
    def _(g):
        b.wait(g, "vA", vA_w)

        def kstore(kt):
            ch = "dKSe" if kt % 2 == 0 else "dKSo"
            for hh in range(HPC):
                b.wait(g, "vE", vK0 + 2 * kt + hh + 1)
                dma = g.dma_start(out=io["kscr"][0, hh, :, ts(kt, 512)], in_=KB[:, ts(2 * hh, 512)])
                b.inc(ch, dma, 16)
                dma = g.dma_start(out=io["kscr"][1, hh, :, ts(kt, 512)], in_=KB[:, ts(2 * hh + 1, 512)])
                b.inc(ch, dma, 16)

        for kt in range(NKT):
            ch = "dCKe" if kt % 2 == 0 else "dCKo"
            if kt >= 2:
                b.wait(g, "tM", tK0 + (kt - 1))
            sl, tloc = kt // 2, (kt % 2) * 512
            base = (kt % 2) * 4096
            for cch in range(4):
                dma = g.dma_start(out=CCB[:, base + cch * 512: base + cch * 512 + 512],
                                  in_=ag[sl, 8 + cch, :, tloc:tloc + 512])
                b.inc(ch, dma, 16)
                dma = g.dma_start(out=CCB[:, base + 2048 + cch * 512: base + 2048 + cch * 512 + 512],
                                  in_=ag[sl, 12 + cch, :, tloc:tloc + 512])
                b.inc(ch, dma, 16)
            lbase = (kt % 2) * 16384
            for hch in range(16):
                dma = g.dma_start(out=LNB[:, lbase + hch * 512: lbase + hch * 512 + 512],
                                  in_=ag[sl, 16 + hch, :, tloc:tloc + 512])
                b.inc(ch, dma, 16)
                dma = g.dma_start(out=LNB[:, lbase + 8192 + hch * 512: lbase + 8192 + hch * 512 + 512],
                                  in_=ag[sl, 32 + hch, :, tloc:tloc + 512])
                b.inc(ch, dma, 16)
            if kt >= 1:
                kstore(kt - 1)
        kstore(NKT - 1)

    @b.on("tensor")
    def _(t):
        b.wait(t, "vA", vA_w)

        def psr(j):
            b.wait(t, "sA", sK0 + j + 1)
            if j >= 2:
                b.wait(t, "vM", vMX0 + j - 1)
            ps = PSr if j % 2 == 0 else PSr2
            last = t.matmul(ps[:], k["ones_col_f"][:], KSQ[:], start=True, stop=True)
            b.inc("tS", last)

        for kt in range(NKT):
            ch = "dCKe" if kt % 2 == 0 else "dCKo"
            b.wait(t, ch, dCK0[kt % 2] + 16 * 40 * (kt // 2 + 1))
            if kt >= 1:
                b.wait(t, "vE", vK0 + 2 * kt)
            base = (kt % 2) * 4096
            lbase = (kt % 2) * 16384
            for hh in range(HPC):
                ps = PSkr[64 * hh:64 * hh + 64, :]
                for hch in range(16):
                    lh = LNB[:, lbase + hch * 512: lbase + hch * 512 + 512]
                    ll = LNB[:, lbase + 8192 + hch * 512: lbase + 8192 + hch * 512 + 512]
                    wh, wl = wkr(True, hch, hh), wkr(False, hch, hh)
                    t.matmul(ps, wh, lh, start=(hch == 0), stop=False)
                    t.matmul(ps, wh, ll, start=False, stop=False)
                    t.matmul(ps, wl, lh, start=False, stop=(hch == 15))
            for hh in range(HPC):
                ps = PSkb[64 * (1 - hh):64 * (1 - hh) + 64, :]
                for cch in range(4):
                    ch_ = CCB[:, base + cch * 512: base + cch * 512 + 512]
                    cl_ = CCB[:, base + 2048 + cch * 512: base + 2048 + cch * 512 + 512]
                    wh, wl = wsl(True, "k", cch, hh), wsl(False, "k", cch, hh)
                    t.matmul(ps, wh, ch_, start=(cch == 0), stop=False)
                    t.matmul(ps, wh, cl_, start=False, stop=False)
                    last = t.matmul(ps, wl, ch_, start=False, stop=(cch == 3))
            b.inc("tM", last)
            if kt >= 1:
                psr(2 * (kt - 1))
                psr(2 * (kt - 1) + 1)
        psr(2 * (NKT - 1))
        psr(2 * (NKT - 1) + 1)

    @b.on("vector")
    def _(v):
        b.wait(v, "rope", ropeN)

        def rmax(j):
            hh = j % 2
            b.wait(v, "tS", tR0 + j + 1)
            if j >= 2:
                b.wait(v, "vM", vMX0 + j - 1)
            last = v.reduce_max(CMX[0:1, 8 + hh:9 + hh], (PSr if hh == 0 else PSr2)[:],
                                axis=mybir.AxisListType.X)
            b.fence(v, last, "fv")
            if j < 2:
                last = v.tensor_copy(CMX[0:1, hh:hh + 1], CMX[0:1, 8 + hh:9 + hh])
            else:
                last = v.tensor_tensor(CMX[0:1, hh:hh + 1], CMX[0:1, hh:hh + 1],
                                       CMX[0:1, 8 + hh:9 + hh], OP.max)
            b.inc("vM", last)

        for kt in range(NKT):
            b.wait(v, "tM", tK0 + kt + 1)
            if kt >= 1:
                b.wait(v, "sA", sK0 + 2 * kt)
                chp = "dKSe" if (kt - 1) % 2 == 0 else "dKSo"
                b.wait(v, chp, dKS0[(kt - 1) % 2] + 16 * 4 * ((kt - 1) // 2 + 1))
            for hh in range(HPC):
                v.tensor_tensor(KT[64 * hh:64 * hh + 64, ts(hh, 512)],
                                PSkr[64 * hh:64 * hh + 64, :],
                                k["rs"][64 * hh:64 * hh + 64, ts(kt, 512)], OP.mult)
                r0 = 64 * (1 - hh)
                v.tensor_copy(KT[r0:r0 + 64, ts(hh, 512)], PSkb[r0:r0 + 64, :])
                v.tensor_copy(KB[:, ts(2 * hh, 512)], KT[:, ts(hh, 512)])
                last = v.tensor_tensor(KB[:, ts(2 * hh + 1, 512)], KT[:, ts(hh, 512)],
                                       KB[:, ts(2 * hh, 512)], OP.subtract)
                b.inc("vE", last)
            if kt >= 1:
                rmax(2 * (kt - 1))
                rmax(2 * (kt - 1) + 1)
        rmax(2 * (NKT - 1))
        rmax(2 * (NKT - 1) + 1)

    @b.on("scalar")
    def _(s):
        for kt in range(NKT):
            for hh in range(HPC):
                j = 2 * kt + hh
                b.wait(s, "vE", vK0 + j + 1)
                if j >= 1:
                    b.wait(s, "tS", tR0 + j)
                last = s.activation(KSQ[:], KT[:, ts(hh, 512)], AF.Square)
                b.inc("sA", last)

    vM_end = b.mark("vM")

    @b.on("vector")
    def _(v):
        b.wait(v, "vM", vM_end)
        last = v.tensor_scalar(NEGC[0:1, 0:2], CMX[0:1, 0:2], -0.5, SOFT_C_SHIFT,
                               OP.mult, OP.add)
        b.inc("vCc", last)

    b.free("kpPSkb", "kpPSkr", "kpPSr2", "kpPSr", "kpCMX", "kpKSQ", "kpKB",
           "kpKT", "kpLNB", "kpCCB")


def emit_attn_head(b, k, io, hh, MK, wsl, NEGC, amask_n, gates):
    u = f"ah{hh}"
    ag = io["ag2_out"]
    NKT = S // 512
    QH = b.sbuf(u + "QH", [128, S], BF16)
    QL = b.sbuf(u + "QL", [128, S], BF16)
    KH = b.sbuf(u + "KH", [128, S], BF16)
    KL = b.sbuf(u + "KL", [128, S], BF16)
    VH = b.sbuf(u + "VH", [128, NTI * 128], BF16)
    VL = b.sbuf(u + "VL", [128, NTI * 128], BF16)
    NEGM = b.sbuf(u + "NM", [1, S], BF16)
    CCB2 = b.sbuf(u + "CCB2", [128, 8 * 512], BF16)
    QT = b.sbuf(u + "QT", [128, 512], F32)
    QS = b.sbuf(u + "QS", [128, 512], F32)
    PF = b.sbuf(u + "PF", [128, 2 * 512], F32)
    PHB = b.sbuf(u + "PHB", [128, 2 * 512], BF16)
    PLB = b.sbuf(u + "PLB", [128, 2 * 512], BF16)
    RLC = b.sbuf(u + "RLC", [128, 4], F32)
    RLR = b.sbuf(u + "RLR", [1, 512], F32)
    RLBS = b.sbuf(u + "RLBS", [128, 512], F32)
    OEV = b.sbuf(u + "OEV", [128, 2 * 512], F32)
    PSs = [b.psum(u + "PSs0", [128, 512]), b.psum(u + "PSs1", [128, 512])]
    PSavB = [b.psum(u + "PSav", [128, 512]), b.psum(u + "PSav1", [128, 512])]
    PSlB = [b.psum(u + "PSl", [128, 4]), b.psum(u + "PSl1", [128, 4])]
    PSr = b.psum(u + "PSr", [1, 512])
    PSx = b.psum(u + "PSx", [128, 512])

    dKH0 = b.mark("dKHe")

    @b.on("sync")
    def _(sy):
        b.wait(sy, "vCc", b.mark("vCc"))
        for _c, _n in gates:
            b.wait(sy, _c, _n)
        dma = sy.dma_start(out=KH[:], in_=io["kscr"][0, hh])
        b.inc("dKHe", dma, 16)
        dma = sy.dma_start(out=KL[:], in_=io["kscr"][1, hh])
        b.inc("dKHe", dma, 16)

    dKH1 = b.mark("dKHe")

    dCV0 = {0: b.mark("dCVe"), 1: b.mark("dCVo")}
    tV0 = b.mark("tM")
    vV0 = b.mark("vE")

    @b.on("gpsimd")
    def _(g):
        for _c, _n in gates:
            b.wait(g, _c, _n)
        for kt in range(NKT):
            ch = "dCVe" if kt % 2 == 0 else "dCVo"
            if kt >= 1:
                b.wait(g, "tM", tV0 + 4 * kt)
            sl, tloc = kt // 2, (kt % 2) * 512
            base = 0
            for cch in range(4):
                dma = g.dma_start(out=CCB2[:, base + cch * 512: base + cch * 512 + 512],
                                  in_=ag[sl, 8 + cch, :, tloc:tloc + 512])
                b.inc(ch, dma, 16)
                dma = g.dma_start(out=CCB2[:, base + 2048 + cch * 512: base + 2048 + cch * 512 + 512],
                                  in_=ag[sl, 12 + cch, :, tloc:tloc + 512])
                b.inc(ch, dma, 16)

    @b.on("tensor")
    def _(t):
        b.wait(t, "dKHe", dKH1)
        for kt in range(NKT):
            ch = "dCVe" if kt % 2 == 0 else "dCVo"
            b.wait(t, ch, dCV0[kt % 2] + 16 * 8 * (kt // 2 + 1))
            base = 0
            for k4 in range(4):
                j = 4 * kt + k4
                if j >= 2:
                    b.wait(t, "vE", vV0 + j - 1)
                ps = PSs[k4 % 2]
                for cch in range(4):
                    chs = CCB2[:, base + cch * 512 + k4 * 128: base + cch * 512 + k4 * 128 + 128]
                    cls = CCB2[:, base + 2048 + cch * 512 + k4 * 128: base + 2048 + cch * 512 + k4 * 128 + 128]
                    t.matmul(ps[:, :128], chs, wsl(True, "v", cch, hh), start=(cch == 0), stop=False)
                    t.matmul(ps[:, :128], cls, wsl(True, "v", cch, hh), start=False, stop=False)
                    last = t.matmul(ps[:, :128], chs, wsl(False, "v", cch, hh),
                                    start=False, stop=(cch == 3))
                b.inc("tM", last)

    @b.on("vector")
    def _(v):
        for kt in range(NKT):
            for k4 in range(4):
                j = 4 * kt + k4
                b.wait(v, "tM", tV0 + j + 1)
                v.tensor_copy(VH[:, ts(j, 128)], PSs[k4 % 2][:, :128])
                last = v.tensor_tensor(VL[:, ts(j, 128)], PSs[k4 % 2][:, :128],
                                       VH[:, ts(j, 128)], OP.subtract)
                b.inc("vE", last)

    vV_end = b.mark("vE")

    dCQ0 = {0: b.mark("dCQe"), 1: b.mark("dCQo")}
    tQ0 = b.mark("tS")
    vQp0 = b.mark("vR")
    sQ0 = b.mark("sA")

    @b.on("gpsimd")
    def _(g):
        for qt in range(NKT):
            ch = "dCQe" if qt % 2 == 0 else "dCQo"
            if qt >= 1:
                b.wait(g, "tS", tQ0 + (1 if qt == 1 else 2 * (qt - 1)))
            sl, tloc = qt // 2, (qt % 2) * 512
            base = 0
            for cch in range(4):
                dma = g.dma_start(out=CCB2[:, base + cch * 512: base + cch * 512 + 512],
                                  in_=ag[sl, 0 + cch, :, tloc:tloc + 512])
                b.inc(ch, dma, 16)
                dma = g.dma_start(out=CCB2[:, base + 2048 + cch * 512: base + 2048 + cch * 512 + 512],
                                  in_=ag[sl, 4 + cch, :, tloc:tloc + 512])
                b.inc(ch, dma, 16)

    @b.on("tensor")
    def _(t):
        b.wait(t, "vE", vV_end)

        def psrq(j):
            b.wait(t, "sA", sQ0 + j + 1)
            if j >= 1:
                b.wait(t, "vR", vQp0 + 3 * j + 2)
            last = t.matmul(PSr[:], k["ones_col_f"][:], QS[:, :], start=True, stop=True)
            b.inc("tS", last)

        for qt in range(NKT):
            ch = "dCQe" if qt % 2 == 0 else "dCQo"
            b.wait(t, ch, dCQ0[qt % 2] + 16 * 8 * (qt // 2 + 1))
            if qt >= 1:
                b.wait(t, "vR", vQp0 + (1 if qt == 1 else 3 * (qt - 1)))
            base = 0
            rbase = 64 * hh
            bbase = 64 * (1 - hh)
            for cch in range(4):
                chs = CCB2[:, base + cch * 512: base + cch * 512 + 512]
                cls = CCB2[:, base + 2048 + cch * 512: base + 2048 + cch * 512 + 512]
                t.matmul(PSx[bbase:bbase + 64, :], wsl(True, "q", cch, hh), chs, start=(cch == 0), stop=False)
                t.matmul(PSx[bbase:bbase + 64, :], wsl(True, "q", cch, hh), cls, start=False, stop=False)
                t.matmul(PSx[bbase:bbase + 64, :], wsl(False, "q", cch, hh), chs, start=False, stop=(cch == 3))
            for cch in range(4):
                chs = CCB2[:, base + cch * 512: base + cch * 512 + 512]
                cls = CCB2[:, base + 2048 + cch * 512: base + 2048 + cch * 512 + 512]
                t.matmul(PSx[rbase:rbase + 64, :], wsl(True, "r", cch, hh), chs, start=(cch == 0), stop=False)
                t.matmul(PSx[rbase:rbase + 64, :], wsl(True, "r", cch, hh), cls, start=False, stop=False)
                last = t.matmul(PSx[rbase:rbase + 64, :], wsl(False, "r", cch, hh), chs,
                                start=False, stop=(cch == 3))
            b.inc("tS", last)
            if qt >= 1:
                psrq(qt - 1)
        psrq(NKT - 1)

    @b.on("vector")
    def _(v):
        b.wait(v, "vCc", b.mark("vCc"))

        def negm_(j):
            b.wait(v, "tS", tQ0 + (2 * j + 3 if j < NKT - 1 else 2 * NKT))
            last = v.tensor_scalar(NEGM[0:1, ts(j, 512)], PSr[:], -0.5,
                                   NEGC[0:1, hh:hh + 1], OP.mult, OP.add)
            b.inc("vR", last)

        for qt in range(NKT):
            b.wait(v, "tS", tQ0 + (1 if qt == 0 else 2 * qt))
            if qt >= 1:
                b.wait(v, "sA", sQ0 + qt)
            rbase = 64 * hh
            bbase = 64 * (1 - hh)
            v.tensor_tensor(QT[rbase:rbase + 64, :], PSx[rbase:rbase + 64, :],
                            k["rs"][rbase:rbase + 64, ts(qt, 512)], OP.mult)
            last = v.tensor_copy(QT[bbase:bbase + 64, :], PSx[bbase:bbase + 64, :])
            b.inc("vR", last)
            last = v.tensor_scalar(QT[:, :], QT[:, :], RSQ, None, OP.mult)
            b.fence(v, last, "fv")
            v.tensor_copy(QH[:, ts(qt, 512)], QT[:, :])
            last = v.tensor_tensor(QL[:, ts(qt, 512)], QT[:, :], QH[:, ts(qt, 512)], OP.subtract)
            b.inc("vR", last)
            if qt >= 1:
                negm_(qt - 1)
        negm_(NKT - 1)

    @b.on("scalar")
    def _(s):
        for qt in range(NKT):
            b.wait(s, "vR", vQp0 + (2 if qt == 0 else 3 * qt + 1))
            if qt >= 1:
                b.wait(s, "tS", tQ0 + 2 * qt + 1)
            last = s.activation(QS[:, :], QT[:, :], AF.Square)
            b.inc("sA", last)

    vQp_end = b.mark("vR")

    tT0 = b.mark("tM")
    tA0 = b.mark("tA")
    sT0 = b.mark("sE")
    vT0 = b.mark("vH")
    tEp0 = b.mark("tS")
    vEp0 = b.mark("vE")
    dA0 = b.mark("dA2A")
    nj_cum = [0]
    for qt in range(NKT):
        nj_cum.append(nj_cum[-1] + 4 * (qt + 1))
    vQp_end_ = vQp_end
    ep_gate = []

    for qt in range(NKT):
        ep_gate.append(vEp0 + 4 * qt)

    @b.on("tensor")
    def _(t):
        b.wait(t, "vR", vQp_end_)
        b.wait(t, "amask", amask_n)

        def av(jp, qtp, fin):
            pslot = jp % 2
            kk = jp - nj_cum[qtp]
            PSav, PSl = PSavB[qtp % 2], PSlB[qtp % 2]
            b.wait(t, "vH", vT0 + jp + 1)
            t.matmul(PSav[:], VH[:, ts(kk, 128)], PHB[:, ts(pslot, 512)],
                     start=(jp == nj_cum[qtp]), stop=False)
            t.matmul(PSav[:], VH[:, ts(kk, 128)], PLB[:, ts(pslot, 512)], start=False, stop=False)
            t.matmul(PSav[:], VL[:, ts(kk, 128)], PHB[:, ts(pslot, 512)], start=False, stop=fin)
            for qc in range(4):
                t.matmul(PSl[:, qc:qc + 1], PHB[:, pslot * 512 + qc * 128: pslot * 512 + qc * 128 + 128],
                         k["ones_col_b"][:], start=(jp == nj_cum[qtp] and qc == 0), stop=False)
                last = t.matmul(PSl[:, qc:qc + 1], PLB[:, pslot * 512 + qc * 128: pslot * 512 + qc * 128 + 128],
                                k["ones_col_b"][:], start=False, stop=(fin and qc == 3))
            b.inc("tA", last)

        def epi_mm(qtp):
            b.wait(t, "vE", vEp0 + 4 * qtp + 1)
            for qc in range(4):
                last = t.matmul(PSx[0:1, ts(qc, 128)], RLC[:, qc:qc + 1], k["id128f"][:],
                                start=True, stop=True)
            b.inc("tS", last)
            b.wait(t, "vE", vEp0 + 4 * qtp + 2)
            last = t.matmul(PSx[:, :], k["ones_row_f"][:], RLR[0:1, :], start=True, stop=True)
            b.inc("tS", last)

        for qt in range(NKT):
            NJ = 4 * (qt + 1)
            if qt >= 2:
                b.wait(t, "vE", vEp0 + 4 * (qt - 1))
            for j in range(NJ):
                jg = nj_cum[qt] + j
                if jg >= 2:
                    b.wait(t, "sE", sT0 + jg - 1)
                ps = PSs[jg % 2]
                t.matmul(ps[:], KH[:, ts(j, 128)], QH[:, ts(qt, 512)], start=True, stop=False)
                t.matmul(ps[:], KH[:, ts(j, 128)], QL[:, ts(qt, 512)], start=False, stop=False)
                t.matmul(ps[:], KL[:, ts(j, 128)], QH[:, ts(qt, 512)], start=False, stop=False)
                last = t.matmul(ps[:], k["ones_row_b"][:], NEGM[0:1, ts(qt, 512)],
                                start=False, stop=True)
                b.inc("tM", last)
                if qt >= 1 and j == 1:
                    epi_mm(qt - 1)
                if j >= 1:
                    av(nj_cum[qt] + j - 1, qt, False)
            av(nj_cum[qt] + NJ - 1, qt, True)
        epi_mm(NKT - 1)

    @b.on("scalar")
    def _(s):
        for qt in range(NKT):
            for j in range(4 * (qt + 1)):
                jg = nj_cum[qt] + j
                b.wait(s, "tM", tT0 + jg + 1)
                if jg >= 2:
                    b.wait(s, "vH", vT0 + jg - 1)
                last = s.activation(PF[:, ts(jg % 2, 512)], PSs[jg % 2][:], AF.Exp)
                b.inc("sE", last)

    @b.on("vector")
    def _(v):
        for qt in range(NKT):
            for j in range(4 * (qt + 1)):
                jg = nj_cum[qt] + j
                b.wait(v, "sE", sT0 + jg + 1)
                if jg >= 2:
                    b.wait(v, "tA", tA0 + jg - 1)
                slot = jg % 2
                if j >= 4 * qt:
                    r = j - 4 * qt
                    v.tensor_tensor(PF[:, ts(slot, 512)], PF[:, ts(slot, 512)],
                                    MK[:, ts(r, 512)], OP.mult)
                v.tensor_copy(PHB[:, ts(slot, 512)], PF[:, ts(slot, 512)])
                last = v.tensor_tensor(PLB[:, ts(slot, 512)], PF[:, ts(slot, 512)],
                                       PHB[:, ts(slot, 512)], OP.subtract)
                b.inc("vH", last)
            b.wait(v, "tA", tA0 + nj_cum[qt + 1])
            last = v.reciprocal(RLC[:, 0:4], PSlB[qt % 2][:, 0:4])
            b.inc("vE", last)
            b.wait(v, "tS", tEp0 + 2 * qt + 1)
            last = v.tensor_copy(RLR[0:1, :], PSx[0:1, 0:512])
            b.inc("vE", last)
            b.wait(v, "tS", tEp0 + 2 * qt + 2)
            last = v.tensor_copy(RLBS[:, :], PSx[:, :512])
            b.inc("vE", last)
            if qt >= 2:
                b.wait(v, "dA2A", dA0 + 16 * (qt - 1))
            last = v.tensor_tensor(OEV[:, ts(qt % 2, 512)], PSavB[qt % 2][:],
                                   RLBS[:, :], OP.mult)
            b.inc("vE", last)

    @b.on("gpsimd")
    def _(g):
        for qt in range(NKT):
            b.wait(g, "vE", vEp0 + 4 * qt + 4)
            dest, tloc = qt // 2, (qt % 2) * 512
            dma = g.dma_start(out=io["a2a_in"][dest, hh, :, tloc:tloc + 512],
                              in_=OEV[:, ts(qt % 2, 512)])
            b.inc("dA2A", dma, 16)

    b.free(u + "PSx", u + "PSr", u + "PSl1", u + "PSl", u + "PSav1", u + "PSav",
           u + "PSs1", u + "PSs0",
           u + "OEV", u + "RLBS", u + "RLR", u + "RLC", u + "PLB", u + "PHB",
           u + "PF", u + "QS", u + "QT", u + "CCB2", u + "NM", u + "VL", u + "VH",
           u + "KL", u + "KH", u + "QL", u + "QH")



def emit_wout(b, k, io, gates):
    NQ = TPC // TH
    AOT = b.sbuf("woAOT", [128, 16 * TH], F32)
    WOB = b.sbuf("woWOB", [128, 2 * 16 * 128], F32)
    XOB = b.sbuf("woXOB", [128, 2 * TH], F32)
    OO = b.sbuf("woOO", [128, 2 * TH], F32)
    PSW = [b.psum("woPSW0", [128, TH]), b.psum("woPSW1", [128, TH])]

    for q in range(NQ):
        dAO0 = b.mark("dAO")
        dWOe0 = b.mark("dWOe")
        dWOo0 = b.mark("dWOo")
        dX10 = {0: b.mark("dX1e"), 1: b.mark("dX1o")}
        tW0 = b.mark("tM")
        vW0 = b.mark("vE")
        dX2o0 = {0: b.mark("dX2e"), 1: b.mark("dX2o")}

        @b.on("gpsimd")
        def _(g, q=q, vW0=vW0):
            for _c, _n in gates:
                b.wait(g, _c, _n)
            if q > 0:
                b.wait(g, "vE", vW0)
            for i in range(16):
                dma = g.dma_start(out=AOT[:, ts(i, TH)],
                                  in_=io["a2a_out"][i // 2, i % 2, :, ts(q, TH)])
                b.inc("dAO", dma, 16)

        @b.on("sync")
        def _(sy, q=q, tW0=tW0):
            for _c, _n in gates:
                b.wait(sy, _c, _n)
            for oc in range(16):
                ch = "dWOe" if oc % 2 == 0 else "dWOo"
                if oc >= 2 or q > 0:
                    b.wait(sy, "tM", tW0 + oc - 1)
                dma = sy.dma_start(
                    out=WOB[:, ts(oc % 2, 2048)].rearrange("p (h o) -> p h o", h=16),
                    in_=io["Wout"].rearrange("(h p) o -> p h o", p=128)[:, :, ts(oc, 128)])
                b.inc(ch, dma, 16)

        @b.on("gpsimd")
        def _(g, q=q, vW0=vW0, dX2o0=dX2o0):
            for dt in range(16):
                ch = "dX1e" if dt % 2 == 0 else "dX1o"
                och = "dX2e" if dt % 2 == 0 else "dX2o"
                if dt >= 2:
                    b.wait(g, och, dX2o0[dt % 2] + 16 * (dt // 2))
                dma = g.dma_start(out=XOB[:, ts(dt % 2, TH)],
                                  in_=io["x1"][ts(dt, 128), ts(q, TH)])
                b.inc(ch, dma, 16)
                b.wait(g, "vE", vW0 + dt + 1)
                dma = g.dma_start(out=io["x2"][ts(dt, 128), ts(q, TH)],
                                  in_=OO[:, ts(dt % 2, TH)])
                b.inc(och, dma, 16)

        @b.on("tensor")
        def _(t, dAO0=dAO0, dWOe0=dWOe0, dWOo0=dWOo0, vW0=vW0):
            b.wait(t, "dAO", dAO0 + 16 * 16)
            for oc in range(16):
                ch = "dWOe" if oc % 2 == 0 else "dWOo"
                b.wait(t, ch, (dWOe0 if oc % 2 == 0 else dWOo0) + 16 * (oc // 2 + 1))
                if oc >= 2:
                    b.wait(t, "vE", vW0 + oc - 1)
                ps = PSW[oc % 2]
                for dch in range(16):
                    last = t.matmul(ps[:], WOB[:, (oc % 2) * 2048 + dch * 128:(oc % 2) * 2048 + dch * 128 + 128],
                                    AOT[:, ts(dch, TH)], start=(dch == 0), stop=(dch == 15))
                b.inc("tM", last)

        @b.on("vector")
        def _(v, tW0=tW0, dX10=dX10):
            for oc in range(16):
                b.wait(v, "tM", tW0 + oc + 1)
                ch = "dX1e" if oc % 2 == 0 else "dX1o"
                b.wait(v, ch, dX10[oc % 2] + 16 * (oc // 2 + 1))
                v.tensor_scalar(OO[:, ts(oc % 2, TH)], PSW[oc % 2][:],
                                k["bout_col"][:, oc:oc + 1], None, OP.add)
                last = v.tensor_tensor(OO[:, ts(oc % 2, TH)], OO[:, ts(oc % 2, TH)],
                                       XOB[:, ts(oc % 2, TH)], OP.add)
                b.inc("vE", last)

    dx2e, dx2o = b.mark("dX2e"), b.mark("dX2o")
    b.free("woPSW1", "woPSW0", "woOO", "woXOB", "woWOB", "woAOT")
    return dx2e, dx2o



def build_full():
    nc = bass.Bass("TRN2", target_bir_lowering=False, debug=False,
                   detect_race_conditions=False)
    ein = lambda n, s, d=F32: nc.dram_tensor(n, s, d, kind="ExternalInput").ap()
    x_own = ein("x_own", [D, TPC])
    wg_rows = ein("wg_rows", [IFF // 8, D])
    wu_rows = ein("wu_rows", [IFF // 8, D])
    wd_rows = ein("wd_rows", [D // 8, IFF])
    Wdq = ein("Wdq", [D, DKV])
    Wdkv = ein("Wdkv", [D, DKV])
    Wuq = ein("Wuq", [DKV, HPC * 64])
    Wqr = ein("Wqr", [DKV, HPC * 64])
    Wuk = ein("Wuk", [DKV, HPC * 64])
    Wuv = ein("Wuv", [DKV, HPC * 128])
    Wkr_h = ein("Wkr_h", [D, HPC * 64])
    Wout = ein("Wout", [D, D])
    bout = ein("bout", [128, 16])
    gq = ein("gq", [128, 4])
    gkv = ein("gkv", [128, 4])
    posf = ein("posf", [1, S])
    y_own = nc.dram_tensor("y_own", [D, TPC], F32, kind="ExternalOutput").ap()

    dram = (lambda n, s, d=F32: nc.dram_tensor(n, s, d, kind="ExternalOutput").ap()) \
        if DBG_TAPS else (lambda n, s, d=F32: nc.dram_tensor(n, s, d).ap())
    io = {
        "wg_rows": wg_rows, "wu_rows": wu_rows, "wd_rows": wd_rows,
        "ar_in": dram("ar_in", [1, 8]), "ar_out": dram("ar_out", [1, 8]),
        "ag_g_in": dram("ag_g_in", [8, 128, NTD, 128], BF16),
        "ag_u_in": dram("ag_u_in", [8, 128, NTD, 128], BF16),
        "ag_d_in": dram("ag_d_in", [2, 128, NTI, 128], BF16),
        "ag_g": dram("ag_g", [NCORE, 8, 128, NTD, 128], BF16),
        "ag_u": dram("ag_u", [NCORE, 8, 128, NTD, 128], BF16),
        "ag_d": dram("ag_d", [NCORE, 2, 128, NTI, 128], BF16),
        "x1": dram("x1s", [D, TPC]), "x2": dram("x2s", [D, TPC]),
        "Wdq": Wdq, "Wdkv": Wdkv, "Wkr_h": Wkr_h,
        "ag2_in": dram("ag2_in", [48, 128, TPC], BF16),
        "ag2_out": dram("ag2_out", [NCORE, 48, 128, TPC], BF16),
        "Wuq": Wuq, "Wqr": Wqr, "Wuk": Wuk, "Wuv": Wuv,
        "kscr": dram("kscr", [2, HPC, 128, S], BF16),
        "a2a_in": dram("a2a_in", [NCORE, HPC, 128, TPC]),
        "a2a_out": dram("a2a_out", [NCORE, HPC, 128, TPC]),
        "Wout": Wout,
    }

    b = Build(nc)
    k = alloc_persist(b)
    emit_const_setup(b, k)

    @b.on("gpsimd")
    def _(g):
        b.wait(g, "init", 1)
        dma = g.dma_start(out=k["gq_col"][:], in_=gq[:, :])
        b.inc("dMisc", dma, 16)
        dma = g.dma_start(out=k["gkv_col"][:], in_=gkv[:, :])
        b.inc("dMisc", dma, 16)
        dma = g.dma_start(out=k["bout_col"][:], in_=bout[:, :])
        b.inc("dMisc", dma, 16)

    def fini(chs):
        @b.on("gpsimd")
        def _(g):
            for c, n in chs:
                b.wait(g, c, n)
        b.realize()
        return nc

    if PHASE_LIMIT == -1:
        return fini([("init", 1), ("dMisc", 48)])
    emit_rope(b, k, posf, "rs", S, "rf")
    if PHASE_LIMIT == 0:
        return fini([("rope", 1), ("dMisc", 48)])
    _wp = emit_weight_prep(b, k, io)
    if _wp is not None:
        return fini(_wp)
    terng = [("wsccol", b.mark("wsccol")), ("ternAG", b.mark("ternAG")),
             ("ccAG", b.mark("ccAG"))]
    if PHASE_LIMIT == 1:
        return fini(terng + [("rope", 1)])
    emit_ffn_call(b, k, x_own, io["x1"], io["ag_g"], io["ag_u"], io["ag_d"], 0, terng)
    ffn1_done = [("dOe", b.mark("dOe")), ("dOo", b.mark("dOo"))]
    if PHASE_LIMIT == 2:
        return fini(ffn1_done + [("rope", 1)])
    emit_proj(b, k, io, ffn1_done)
    if PHASE_LIMIT == 3:
        return fini([("projAG", b.mark("projAG")), ("rope", 1)])
    emit_attn(b, k, io)
    att_done = [("attA2A", b.mark("attA2A")), ("dMisc", 48)]
    if PHASE_LIMIT == 5:
        return fini(att_done)
    dx2e, dx2o = emit_wout(b, k, io, att_done)
    x2_done = terng + [("dX2e", dx2e), ("dX2o", dx2o)]
    if PHASE_LIMIT == 6:
        return fini(x2_done)
    emit_ffn_call(b, k, io["x2"], y_own, io["ag_g"], io["ag_u"], io["ag_d"], 1, x2_done)

    return fini([("dOe", b.mark("dOe")), ("dOo", b.mark("dOo"))])


_NC_CACHE = [None]


def kernel(**inputs):
    xin = np.ascontiguousarray(inputs["L0_attn_x"][0])
    wg = np.ascontiguousarray(inputs["L0_ffn_w_gate"])
    wu = np.ascontiguousarray(inputs["L0_ffn_w_up"])
    wd = np.ascontiguousarray(inputs["L0_ffn_w_down"])
    Wdq = np.ascontiguousarray(inputs["L0_attn_W_dq"])
    Wuq = np.ascontiguousarray(inputs["L0_attn_W_uq"])
    Wdkv = np.ascontiguousarray(inputs["L0_attn_W_dkv"])
    Wuk = np.ascontiguousarray(inputs["L0_attn_W_uk"])
    Wuv = np.ascontiguousarray(inputs["L0_attn_W_uv"])
    Wqr = np.ascontiguousarray(inputs["L0_attn_W_qr"])
    Wkr = np.ascontiguousarray(inputs["L0_attn_W_kr"])
    gq = np.ascontiguousarray(inputs["L0_attn_g_q"])
    gkv = np.ascontiguousarray(inputs["L0_attn_g_kv"])
    Wout = np.ascontiguousarray(inputs["L0_attn_W_out"])
    bout = np.ascontiguousarray(inputs["L0_attn_b_out"])

    posf = np.arange(S, dtype=np.float32).reshape(1, S)
    in_maps = []
    for c in range(NCORE):
        h0 = HPC * c
        hsl = slice(64 * h0, 64 * (h0 + HPC))
        dsl = slice(128 * h0, 128 * (h0 + HPC))
        in_maps.append({
            "x_own": np.ascontiguousarray(xin[c * TPC:(c + 1) * TPC].T),
            "wg_rows": np.ascontiguousarray(wg[c * (IFF // 8):(c + 1) * (IFF // 8)]),
            "wu_rows": np.ascontiguousarray(wu[c * (IFF // 8):(c + 1) * (IFF // 8)]),
            "wd_rows": np.ascontiguousarray(wd[c * (D // 8):(c + 1) * (D // 8)]),
            "Wdq": Wdq, "Wdkv": Wdkv,
            "Wuq": np.ascontiguousarray(Wuq[:, hsl]),
            "Wqr": np.ascontiguousarray(Wqr[:, hsl]),
            "Wuk": np.ascontiguousarray(Wuk[:, hsl]),
            "Wuv": np.ascontiguousarray(Wuv[:, dsl]),
            "Wkr_h": np.ascontiguousarray(Wkr[:, hsl]),
            "Wout": Wout,
            "bout": np.ascontiguousarray(bout.reshape(16, 128).T),
            "gq": np.ascontiguousarray(gq.reshape(4, 128).T),
            "gkv": np.ascontiguousarray(gkv.reshape(4, 128).T),
            "posf": posf,
        })
    if _NC_CACHE[0] is None:
        _NC_CACHE[0] = build_full()
    res = run_bass_kernel_spmd(_NC_CACHE[0], in_maps, core_ids=list(range(NCORE)))
    out = np.empty((1, S, D), dtype=np.float32)
    for c in range(NCORE):
        out[0, c * TPC:(c + 1) * TPC] = res.results[c]["y_own"].T
    return out
